# version 48
# baseline (speedup 1.0000x reference)
import numpy as np
from contextlib import ExitStack

import concourse.bass as bass
import concourse.mybir as mybir
from concourse.bass_utils import run_bass_kernel_spmd

F32 = mybir.dt.float32
BF16 = mybir.dt.bfloat16
I32 = mybir.dt.int32
U8 = mybir.dt.uint8
AF = mybir.ActivationFunctionType
ALU = mybir.AluOpType
AX = mybir.AxisListType

HD = 64
LN_EPS = 1e-5
FP6_MAX = 7.5
MX_EPS = 1e-30
DTB = {F32: 4, BF16: 2, I32: 4}
NONE_TOK = ("dve", 0)


class Cfg:
    def __init__(self, D, S_q, S_kv):
        self.D, self.S_q, self.S_kv = D, S_q, S_kv
        self.H = D // HD
        self.NP = self.H // 2
        self.FC = D // 128
        self.KC = S_kv // 128
        self.QT = S_q // 512
        self.NG = S_kv // 512
        self.NGQ = S_q // 512
        self.NTQ = S_q // 128
        self.JH = D // 512
        assert self.NP == self.FC


class Trace:
    def __init__(self):
        self.ops = []
        self.cnt = {}
        self.pending = {}

    def tok(self, sem):
        return (sem, self.cnt.get(sem, 0))

    def barrier(self):
        toks = [self.tok(s) for s in
                ("pe", "act", "dve", "pool",
                 "dxa", "dxb", "dxc", "dxd", "dwa", "dwb", "dwc", "dwd",
                 "dva", "dvb")]
        toks = [t for t in toks if t[1] > 0]
        for e in ("sync", "tensor", "scalar", "vector", "gpsimd"):
            self.pending.setdefault(e, []).extend(toks)

    def emit(self, eng, fn, waits=(), inc=None, inc_by=1):
        ws = list(self.pending.pop(eng, []))
        for t in waits:
            if t is None or t[0] is None or t[1] <= 0:
                continue
            s, v = t
            assert v <= self.cnt.get(s, 0), f"wait on future {s}={v} (now {self.cnt.get(s, 0)})"
            ws.append((s, v))
        if inc is not None:
            self.cnt[inc] = self.cnt.get(inc, 0) + inc_by
        self.ops.append((eng, fn, ws, inc, inc_by))
        return self.tok(inc) if inc else None

    def replay(self, engine_names, eng, sems):
        seen = {}
        for (e, fn, ws, inc, inc_by) in self.ops:
            if e not in engine_names:
                continue
            for (s, v) in ws:
                if seen.get(s, 0) >= v:
                    continue
                eng.wait_ge(sems[s], v)
                seen[s] = v
            inst = fn(eng)
            if inc is not None:
                assert inst is not None, f"op on {e} incrementing {inc} returned None"
                inst.then_inc(sems[inc], inc_by)


class DmaQ:

    def __init__(self, trace, engine, lane_sems):
        self.T = trace
        self.engine = engine
        self.lanes = lane_sems
        self.uses = [0] * len(lane_sems)
        self.i = 0

    def dma(self, fn, waits=()):
        ln = self.i % len(self.lanes)
        self.i += 1
        sem = self.lanes[ln]
        issue_waits = list(waits) + [(sem, 16 * self.uses[ln])]
        self.uses[ln] += 1
        self.T.emit(self.engine, fn, waits=issue_waits, inc=sem, inc_by=16)
        return (sem, 16 * self.uses[ln])

    def total_tokens(self):
        return [(s, 16 * u) for s, u in zip(self.lanes, self.uses) if u > 0]


BAND_MULT = float(np.exp(np.float64(0.1)))


def band_main_np():
    b = np.ones((128, 132), dtype=np.float32)
    for p in range(128):
        b[p, p:p + 5] = BAND_MULT
    return b


def corner_pred_np(on):
    c = np.ones((128, 2), dtype=np.float32)
    if on:
        c[126, 0] = BAND_MULT
        c[127, 0] = BAND_MULT
        c[127, 1] = BAND_MULT
    return c


def corner_succ_np(on):
    c = np.ones((128, 2), dtype=np.float32)
    if on:
        c[0, 0] = BAND_MULT
        c[0, 1] = BAND_MULT
        c[1, 1] = BAND_MULT
    return c


def build_nc(cfg: Cfg):
    D, FC, KC, QT, NP, JH = cfg.D, cfg.FC, cfg.KC, cfg.QT, cfg.NP, cfg.JH
    S_q, S_kv = cfg.S_q, cfg.S_kv

    nc = bass.Bass("TRN2", target_bir_lowering=False)
    ap = lambda t: t.ap()

    x_kv = ap(nc.dram_tensor("x_kv", [S_kv, D], F32, kind="ExternalInput"))
    wqT = ap(nc.dram_tensor("wqT", [D, D], F32, kind="ExternalInput"))
    wkT = ap(nc.dram_tensor("wkT", [D, D], F32, kind="ExternalInput"))
    wvT = ap(nc.dram_tensor("wvT", [D, D], F32, kind="ExternalInput"))
    woT = ap(nc.dram_tensor("woT", [D, D], F32, kind="ExternalInput"))
    wffn = ap(nc.dram_tensor("wffn", [D, D], F32, kind="ExternalInput"))
    bq2_d = ap(nc.dram_tensor("bq2", [128, FC], F32, kind="ExternalInput"))
    bk2_d = ap(nc.dram_tensor("bk2", [128, FC], F32, kind="ExternalInput"))
    ocr2_d = ap(nc.dram_tensor("ocr2", [128, FC], F32, kind="ExternalInput"))
    bv_d = ap(nc.dram_tensor("bv_row", [1, D], F32, kind="ExternalInput"))
    bo_d = ap(nc.dram_tensor("bo_row", [1, D], F32, kind="ExternalInput"))
    fb_d = ap(nc.dram_tensor("fb_row", [1, D], F32, kind="ExternalInput"))
    ident_d = ap(nc.dram_tensor("ident", [128, 128], BF16, kind="ExternalInput"))
    ones128_d = ap(nc.dram_tensor("ones128", [128, 1], BF16, kind="ExternalInput"))
    ones1x128_d = ap(nc.dram_tensor("ones1x128", [65, 128], BF16, kind="ExternalInput"))
    ones1x128f_d = ap(nc.dram_tensor("ones1x128f", [65, 128], F32, kind="ExternalInput"))
    band_d = ap(nc.dram_tensor("band_main", [128, 132], BF16, kind="ExternalInput"))
    corn_d = ap(nc.dram_tensor("corners", [128, 8], BF16, kind="ExternalInput"))
    out_d = ap(nc.dram_tensor("out", [S_q, D], F32, kind="ExternalOutput"))
    q_dram = ap(nc.dram_tensor("q_scratch", [FC, 128, S_q], BF16))
    wd_dram = ap(nc.dram_tensor("wd_scratch", [FC, 128, D], BF16))
    rl_dram = ap(nc.dram_tensor("rl_scratch", [1, 1024], F32))

    T = Trace()
    E, tok = T.emit, T.tok

    with ExitStack() as st:
        def layout(specs):
            out, o = {}, 0
            for name, dt, n in specs:
                o = (o + 63) // 64 * 64
                out[name] = (o, dt, n)
                o += n * DTB[dt]
            return out, o

        p3spec, p3_bytes = layout(
            [(nm, F32, D) for nm in
             ("x1_0", "x1_1", "xhat2_0", "xhat2_1",
              "wfstage_0", "wfstage_1",
              "qy_0", "qy_1", "qt_0", "qt_1", "qv_0", "qv_1",
              "qc_0", "qc_1", "qr_0", "qr_1", "qs_0", "qs_1",
              "qe_0", "qe_1",
              "x1_2", "x1_3")]
            + [(nm, F32, 2) for nm in
               ("l3sum", "l3var", "l3rstd", "l3nmr", "l3musq")]
            + [(nm, F32, D // 32) for nm in
               ("amax_0", "amax_1", "scale_0", "scale_1", "rscale_0", "rscale_1")]
            + [(nm, BF16, D) for nm in ("xd_0", "xd_1", "xdT_0", "xdT_1")]
            + [("woT", BF16, FC * D), ("wd", BF16, FC * D)])
        VW = (D // HD) * 65
        A_BYTES = max(FC * S_kv * 2 + KC * VW * 2, p3_bytes)

        b1spec, b1_bytes = layout(
            [("w1", BF16, FC * D), ("w2", BF16, FC * D), ("w3", BF16, FC * D),
             ("xTg", BF16, FC * 512),
             ("xstage_0", BF16, D),
             ("xhat_0", BF16, D), ("xhat_1", BF16, D),
             ("xhat_2", BF16, D), ("xhat_3", BF16, D),
             ("qst_0", BF16, 512), ("qst_1", BF16, 512),
             ("qqs_0", BF16, 512), ("qqs_1", BF16, 512),
             ("qqs_2", BF16, 512), ("lstats", F32, 32)])

        b2spec, b2_bytes = layout(
            [("attnout", BF16, NP * QT * 512), ("pt_0", BF16, 1024),
             ("pt_1", BF16, 1024), ("pt_2", BF16, 1024),
             ("lsb", F32, 1024), ("stageb", BF16, 512),
             ("q2_0", BF16, 512), ("q2_1", BF16, 512),
             ("negmh", F32, NP), ("nrm", F32, 8)])
        WH = 512
        wq_bytes = ((7 * WH * 4 + 63) // 64 * 64 + WH * 4 + 3 * 64 +
                    2 * WH * 4 + 2 * WH * 2 + 16 * 64)
        B_BYTES = max(b1_bytes, b2_bytes,
                      (b2_bytes + 63) // 64 * 64 + wq_bytes)

        regA = st.enter_context(nc.sbuf_tensor("regA", [128, A_BYTES], U8))
        regB = st.enter_context(nc.sbuf_tensor("regB", [128, B_BYTES], U8))

        def mkview(reg, off, dt, n):
            return reg.ap()[:, off:off + n * DTB[dt]].bitcast(dt)

        K_sb = mkview(regA, 0, BF16, FC * S_kv)
        V_sb = mkview(regA, FC * S_kv * 2, BF16, KC * VW)
        p3 = {nm: mkview(regA, o, dt, n) for nm, (o, dt, n) in p3spec.items()}
        p3["outst_0"] = p3["qe_0"]
        p3["outst_1"] = p3["qe_1"]
        for _par in range(2):
            p3[f"qvi_{_par}"] = mkview(regA, p3spec[f"qr_{_par}"][0], I32, D)
        p3m = {}
        for _nm in ("qy", "qt", "qv", "qc", "qr", "qs", "qe"):
            p3m[_nm] = mkview(regA, p3spec[f"{_nm}_0"][0], F32, 2 * D)
        p3m["qvi"] = mkview(regA, p3spec["qr_0"][0], I32, 2 * D)
        for _nm in ("amax", "scale", "rscale"):
            p3m[_nm] = mkview(regA, p3spec[f"{_nm}_0"][0], F32, 2 * (D // 32))
        p3m["src"] = mkview(regA, p3spec["xhat2_0"][0], F32, 2 * D)
        p3m["dst"] = mkview(regA, p3spec["xd_0"][0], BF16, 2 * D)

        b1 = {nm: mkview(regB, o, dt, n) for nm, (o, dt, n) in b1spec.items()}
        _ls = b1spec["lstats"][0]
        for _i, _nm in enumerate(("lsum", "lvar", "lrstd", "lnmr", "lmusq")):
            b1[_nm] = mkview(regB, _ls + 16 * _i, F32, 4)
        _f0 = (b2_bytes + 63) // 64 * 64
        _o2 = [_f0]
        def wqv(dt, n):
            o_al = (_o2[0] + 63) // 64 * 64
            _o2[0] = o_al + n * DTB[dt]
            return mkview(regB, o_al, dt, n)
        wq = {}
        wq_off = {}
        for k_nm in ("qy", "qt", "qv", "qc", "qr", "qs", "qe"):
            wq_off[k_nm] = (_o2[0] + 63) // 64 * 64
            wq[k_nm] = wqv(F32, WH)
        wq["qvi"] = wqv(I32, WH)
        wq["amax"] = wqv(F32, WH // 32)
        wq["scale"] = wqv(F32, WH // 32)
        wq["rscale"] = wqv(F32, WH // 32)
        wq["src_0"] = wqv(F32, WH)
        wq["src_1"] = wqv(F32, WH)
        wq["dst_0"] = wqv(BF16, WH)
        wq["dst_1"] = wqv(BF16, WH)
        assert _o2[0] <= B_BYTES, (_o2[0], B_BYTES)
        wq["wstg_0"] = mkview(regB, wq_off["qy"], BF16, D)
        wq["wstg_1"] = mkview(regB, wq_off["qv"], BF16, D)
        b2 = {nm: mkview(regB, o, dt, n) for nm, (o, dt, n) in b2spec.items()}
        b2["qq"] = b2["pt_0"]
        b2["rlb"] = b2["lsb"]

        cband = st.enter_context(nc.sbuf_tensor("cband", [128, 132], BF16))
        ccorn = st.enter_context(nc.sbuf_tensor("ccorn", [128, 8], BF16))
        cident = st.enter_context(nc.sbuf_tensor("cident", [128, 128], BF16))
        cones128 = st.enter_context(nc.sbuf_tensor("cones128", [128, 1], BF16))
        cones1x128 = st.enter_context(nc.sbuf_tensor("cones1x128", [65, 128], BF16))
        cones1x128f = st.enter_context(nc.sbuf_tensor("cones1x128f", [65, 128], F32))
        cbq2 = st.enter_context(nc.sbuf_tensor("cbq2", [128, FC], F32))
        cbk2 = st.enter_context(nc.sbuf_tensor("cbk2", [128, FC], F32))
        cocr2 = st.enter_context(nc.sbuf_tensor("cocr2", [128, FC], F32))
        cqbias = st.enter_context(nc.sbuf_tensor("cqbias", [128, FC], F32))
        crows = st.enter_context(nc.sbuf_tensor("crows", [65, D], BF16))
        cnrm = st.enter_context(nc.sbuf_tensor("cnrm", [128, 32], F32))

        psum = st.enter_context(nc.psum_tensor("psA", [128, 4096], F32))
        psum_ap = psum.ap()
        psum_b = psum_ap.bitcast(BF16)

        def bank(k, w=1):
            return psum_ap[:, 512 * k:512 * (k + w)]

        def bankb(k, w=1):
            return psum_b[:, 1024 * k:1024 * (k + w)]

        sems = {}
        for s in ("pe", "act", "dve", "pool",
                  "dxa", "dxb", "dxc", "dxd", "dwa", "dwb", "dwc", "dwd",
                  "dva", "dvb"):
            sems[s] = st.enter_context(nc.semaphore(s))
        QS = DmaQ(T, "sync", ["dxa", "dxb", "dxc", "dxd"])
        QG = DmaQ(T, "gpsimd", ["dwa", "dwb", "dwc", "dwd"])
        QV = DmaQ(T, "vector", ["dva", "dvb"])

        blk = st.enter_context(nc.Block())

        def ld(dst, src):
            return lambda e: e.dma_start(out=dst, in_=src)

        t_consts = []
        for dst, src in [
            (cband.ap()[:], band_d), (ccorn.ap()[:], corn_d),
            (cident.ap()[:], ident_d), (cones128.ap()[:], ones128_d),
            (cones1x128.ap()[:], ones1x128_d), (cones1x128f.ap()[:], ones1x128f_d),
            (cbq2.ap()[:], bq2_d), (cbk2.ap()[:], bk2_d), (cocr2.ap()[:], ocr2_d),
        ]:
            t_consts.append(QS.dma(ld(dst, src)))

        E("vector", lambda e: e.tensor_add(cqbias.ap()[:], cbq2.ap()[:], cocr2.ap()[:]),
          waits=t_consts, inc="dve")

        t_crows = [QG.dma(ld(crows.ap()[0:1, :], bv_d)),
                   QG.dma(ld(crows.ap()[32:33, :], bo_d)),
                   QG.dma(ld(crows.ap()[64:65, :], fb_d))]
        bv_row = crows.ap()[0:1, :]
        bo_row = crows.ap()[32:33, :]
        fb_row = crows.ap()[64:65, :]

        for c in range(FC):
            QG.dma(ld(b1["w1"][:, c * D:(c + 1) * D], wkT[128 * c:128 * (c + 1), :]))
        t_w1 = QG.total_tokens()
        for c in range(FC):
            QG.dma(ld(b1["w2"][:, c * D:(c + 1) * D], wvT[128 * c:128 * (c + 1), :]))
        t_w2 = QG.total_tokens()
        for c in range(FC):
            QG.dma(ld(b1["w3"][:, c * D:(c + 1) * D], wqT[128 * c:128 * (c + 1), :]))
        t_w3 = QG.total_tokens()

        vones = V_sb.rearrange("p (j h c) -> p j h c", j=KC, c=65)[:, :, :, 64:65]
        E("vector", lambda e: e.memset(vones, 1.0), inc="dve")
        t_vones = tok("dve")
        vxs = [V_sb[:, (KC - 4 + i) * VW:(KC - 4 + i) * VW + D] for i in range(4)]
        vones_tail = vones[:, KC - 4:KC, :, :]

        def ln_stats(xst, buf, col, pre=(), dummy=None):
            dmy = buf["dummy"][:] if dummy is None else dummy
            E("scalar", lambda e: e.activation(dmy, xst, AF.Copy,
                                               accum_out=buf["lsum"][:, col:col + 1]),
              waits=list(pre) + [tok("act")], inc="act")
            E("scalar", lambda e: e.activation(dmy, xst, AF.Square,
                                               accum_out=buf["lvar"][:, col:col + 1]),
              waits=[tok("act")], inc="act")
            return tok("act")

        def ln_chain(buf, c0, c1, pre=()):
            sl = slice(c0, c1)
            E("vector", lambda e: e.tensor_single_scalar(
                buf["lsum"][:, sl], buf["lsum"][:, sl], 1.0 / D, ALU.mult),
              waits=list(pre) + [tok("dve")], inc="dve")
            E("vector", lambda e: e.tensor_mul(
                buf["lmusq"][:, sl], buf["lsum"][:, sl], buf["lsum"][:, sl]),
              waits=[tok("dve")], inc="dve")
            E("vector", lambda e: e.tensor_scalar(
                buf["lvar"][:, sl], buf["lvar"][:, sl], 1.0 / D, LN_EPS,
                ALU.mult, ALU.add),
              waits=[tok("dve")], inc="dve")
            E("vector", lambda e: e.tensor_sub(
                buf["lvar"][:, sl], buf["lvar"][:, sl], buf["lmusq"][:, sl]),
              waits=[tok("dve")], inc="dve")
            t_var = tok("dve")
            E("scalar", lambda e: e.activation(buf["lrstd"][:, sl], buf["lvar"][:, sl],
                                               AF.Ln),
              waits=[t_var, tok("act")], inc="act")
            E("scalar", lambda e: e.activation(buf["lrstd"][:, sl], buf["lrstd"][:, sl],
                                               AF.Exp, scale=-0.5),
              waits=[tok("act")], inc="act")
            t_rstd = tok("act")
            E("vector", lambda e: e.scalar_tensor_tensor(
                buf["lnmr"][:, sl], buf["lsum"][:, sl], -1.0, buf["lrstd"][:, sl],
                ALU.mult, ALU.mult),
              waits=[t_rstd, tok("dve")], inc="dve")
            return tok("dve"), t_rstd

        def ln_apply(xst, xhat, buf, col, pre=()):
            E("scalar", lambda e: e.activation(
                xhat, xst, AF.Identity, bias=buf["lnmr"][:, col:col + 1],
                scale=buf["lrstd"][:, col:col + 1]),
              waits=list(pre) + [tok("act")], inc="act")
            return tok("act")

        xst_free = [t_vones] * 4
        xhat_free = [NONE_TOK, NONE_TOK, NONE_TOK, NONE_TOK]
        trbank_free = [NONE_TOK, NONE_TOK]
        kqbank_free = [NONE_TOK, NONE_TOK]
        vbank_free = [NONE_TOK, NONE_TOK]
        xTg_free = [NONE_TOK]
        ln_reader = [("act", 0), NONE_TOK]
        qst_free = [[], []]
        qqs_pe_free = [("pe", 0), ("pe", 0), ("pe", 0)]
        nrm_scr_free = NONE_TOK
        pending_norm = []
        qqs_flip = [0]
        bank_norm_free = [NONE_TOK, NONE_TOK]

        def emit_norm(src_sl, col, first, t_src, par):
            qf = qqs_flip[0]
            qqs_flip[0] = (qf + 1) % 3
            E("scalar", lambda e, src_sl=src_sl, qf=qf: e.activation(
                b1[f"qqs_{qf}"][:], src_sl, AF.Square),
              waits=[t_src, qqs_pe_free[qf], tok("act")], inc="act")
            t_sq = tok("act")
            pending_norm.append((col, first, t_sq, par, qf))

        def flush_norm(force=False):
            nonlocal nrm_scr_free
            while pending_norm and (force or len(pending_norm) >= 3):
                col, first, t_sq, par, qf = pending_norm.pop(0)
                nbank = bank(par)
                E("tensor", lambda e, nbank=nbank, qf=qf: e.matmul(
                    nbank[0:1, :], cones128.ap()[:], b1[f"qqs_{qf}"][:],
                    start=True, stop=True),
                  waits=[t_sq, bank_norm_free[par], trbank_free[par]], inc="pe")
                t_mm = tok("pe")
                qqs_pe_free[qf] = t_mm
                E("vector", lambda e, nbank=nbank: e.reduce_max(
                    cnrm.ap()[0:1, 30:31], nbank[0:1, :], axis=AX.X),
                  waits=[t_mm, nrm_scr_free], inc="dve")
                if first:
                    E("vector", lambda e, col=col: e.tensor_copy(
                        cnrm.ap()[0:1, col:col + 1], cnrm.ap()[0:1, 30:31]),
                      waits=[tok("dve")], inc="dve")
                else:
                    E("vector", lambda e, col=col: e.tensor_max(
                        cnrm.ap()[0:1, col:col + 1], cnrm.ap()[0:1, col:col + 1],
                        cnrm.ap()[0:1, 30:31]),
                      waits=[tok("dve")], inc="dve")
                nrm_scr_free = tok("dve")
                bank_norm_free[par] = tok("dve")

        def p1_front(g):
            nonlocal ln_reader
            applies = []
            t_stats = []
            for sub in range(4):
                i = 4 * g + sub
                xst = vxs[sub]
                t_x = QG.dma(ld(xst, x_kv[128 * i:128 * (i + 1), :]),
                             waits=[xst_free[sub]])
                t_stats.append(ln_stats(
                    xst, b1, sub, pre=[t_x, ln_reader[1], xhat_free[sub]],
                    dummy=b1[f"xhat_{sub}"][:]))
            t_ch_dve, t_ch_act = ln_chain(b1, 0, 4,
                                          pre=[t_stats[-1], ln_reader[0]])
            ln_reader = [tok("act"), t_ch_dve]
            for sub in range(4):
                xst = vxs[sub]
                xhat = b1[f"xhat_{sub}"][:]
                t_apply = ln_apply(xst, xhat, b1, sub,
                                   pre=[t_ch_dve, xhat_free[sub]])
                ln_reader[0] = t_apply
                xst_free[sub] = t_apply
                applies.append(t_apply)
            return applies

        def p1_tr(g, applies):
            tr_evac = []
            for sub in range(4):
                xhat = b1[f"xhat_{sub}"][:]
                pb = sub % 2
                tr_out = bankb(pb)
                for c in range(FC):
                    w = ([applies[sub], trbank_free[pb], bank_norm_free[pb]]
                         + t_consts) if c == 0 else []
                    E("tensor", lambda e, c=c, tr_out=tr_out, xhat=xhat: e.transpose(
                        tr_out[:, 128 * c:128 * (c + 1)],
                        xhat[:, 128 * c:128 * (c + 1)], cident.ap()[:]),
                      waits=w, inc="pe" if c == FC - 1 else None)
                t_tr = tok("pe")
                xhat_free[sub] = t_tr
                xTg_v = b1["xTg"].rearrange("p (c t) -> p c t", c=FC)[:, :, 128 * sub:128 * (sub + 1)]
                tr_v = tr_out[:, 0:FC * 128].rearrange("p (c t) -> p c t", c=FC)
                E("vector", lambda e, xTg_v=xTg_v, tr_v=tr_v: e.tensor_copy(xTg_v, tr_v),
                  waits=[t_tr] + (xTg_free if sub == 0 else []), inc="dve")
                tr_evac.append(tok("dve"))
                trbank_free[pb] = tok("dve")
            return tr_evac[-1]

        def p1_proj(g, t_xTg):
            nonlocal xTg_free
            for jp in range(FC):
                pb = jp % 2
                kps = bank(2 + pb)
                for c in range(FC):
                    w = ([t_xTg, kqbank_free[pb], bank_norm_free[pb]] + t_w1) if c == 0 else []
                    E("tensor", lambda e, c=c, jp=jp, kps=kps: e.matmul(
                        kps,
                        b1["w1"][:, c * D + 128 * jp: c * D + 128 * (jp + 1)],
                        b1["xTg"][:, 512 * c:512 * (c + 1)],
                        start=(c == 0), stop=(c == FC - 1)),
                      waits=w, inc="pe" if c == FC - 1 else None)
                t_kmm = tok("pe")
                dst = K_sb[:, jp * S_kv + 512 * g: jp * S_kv + 512 * (g + 1)]
                E("vector", lambda e, dst=dst, kps=kps, jp=jp: e.tensor_scalar(
                    dst, kps, cbk2.ap()[:, jp:jp + 1], None, ALU.add),
                  waits=[t_kmm] + t_consts, inc="dve")
                kqbank_free[pb] = tok("dve")
                emit_norm(dst, jp, g == 0, tok("dve"), pb)
                flush_norm()
            for sub in range(4):
                j = 4 * g + sub
                pb = sub % 2
                vps = bank(4 + 2 * pb, 2)[:, 0:D]
                for jh in range(JH):
                    w = []
                    if jh == 0:
                        w = [t_xTg, vbank_free[pb]]
                        if g == 0 and sub == 0:
                            w += t_crows + t_w2
                    E("tensor", lambda e, vps=vps, jh=jh: e.matmul(
                        vps[:, 512 * jh:512 * (jh + 1)],
                        cones1x128.ap()[0:1, :], bv_row[:, 512 * jh:512 * (jh + 1)],
                        start=True, stop=False),
                      waits=w, inc=None)
                    for c in range(FC):
                        E("tensor", lambda e, c=c, sub=sub, jh=jh, vps=vps: e.matmul(
                            vps[:, 512 * jh:512 * (jh + 1)],
                            b1["xTg"][:, 512 * c + 128 * sub: 512 * c + 128 * (sub + 1)],
                            b1["w2"][:, c * D + 512 * jh: c * D + 512 * (jh + 1)],
                            start=False, stop=(c == FC - 1)),
                          inc="pe" if (c == FC - 1 and jh == JH - 1) else None)
                t_vmm = tok("pe")
                dst = V_sb[:, j * VW:(j + 1) * VW].rearrange(
                    "p (h c) -> p h c", c=65)[:, :, 0:64]
                vsrc = vps.rearrange("p (h c) -> p h c", c=64)
                E("vector", lambda e, dst=dst, vsrc=vsrc: e.tensor_copy(dst, vsrc),
                  waits=[t_vmm], inc="dve")
                vbank_free[pb] = tok("dve")
            if g < cfg.NGQ:
                for jp in range(FC):
                    pb = jp % 2
                    qps = bank(2 + pb)
                    for c in range(FC):
                        w = ([kqbank_free[pb], bank_norm_free[pb], t_xTg] + t_w3) if c == 0 else []
                        E("tensor", lambda e, c=c, jp=jp, qps=qps: e.matmul(
                            qps,
                            b1["w3"][:, c * D + 128 * jp: c * D + 128 * (jp + 1)],
                            b1["xTg"][:, 512 * c:512 * (c + 1)],
                            start=(c == 0), stop=(c == FC - 1)),
                          waits=w, inc="pe" if c == FC - 1 else None)
                    t_qmm = tok("pe")
                    qb = jp % 2
                    qdst = b1[f"qst_{qb}"][:]
                    E("vector", lambda e, qdst=qdst, qps=qps, jp=jp: e.tensor_scalar(
                        qdst, qps, cqbias.ap()[:, jp:jp + 1], None, ALU.add),
                      waits=[t_qmm] + qst_free[qb], inc="dve")
                    kqbank_free[pb] = tok("dve")
                    t_qev = tok("dve")
                    t_qs = QS.dma(lambda e, qdst=qdst, jp=jp, g=g: e.dma_start(
                        out=q_dram[jp, :, 512 * g:512 * (g + 1)], in_=qdst),
                        waits=[t_qev])
                    emit_norm(qdst, 8 + jp, g == 0, t_qev, pb)
                    flush_norm()
                    qst_free[qb] = [t_qs, tok("act")]
            if g == cfg.NG - 1:
                flush_norm(force=True)
                flush_norm(force=True)
            xTg_free = [tok("pe")]

        ap_cur = p1_front(0)
        for g in range(cfg.NG):
            t_xTg = p1_tr(g, ap_cur)
            if g + 1 < cfg.NG:
                ap_cur = p1_front(g + 1)
            p1_proj(g, t_xTg)
        flush_norm()
        E("vector", lambda e: e.memset(vones_tail, 1.0),
          waits=list(ap_cur) + [tok("dve")], inc="dve")
        t_P1_pe = tok("pe")

        T.barrier()

        nrm = cnrm.ap()
        t_negmh = []
        for p in range(NP):
            E("vector", lambda e, p=p: e.tensor_mul(
                nrm[0:1, 16 + p:17 + p], nrm[0:1, p:p + 1], nrm[0:1, 8 + p:9 + p]),
              waits=[tok("dve")], inc="dve")
            E("scalar", lambda e, p=p: e.activation(
                nrm[0:1, 24:25], nrm[0:1, 16 + p:17 + p], AF.Ln),
              waits=[tok("dve"), tok("act")], inc="act")
            E("scalar", lambda e: e.activation(
                nrm[0:1, 24:25], nrm[0:1, 24:25], AF.Exp, scale=0.5),
              waits=[tok("act")], inc="act")
            E("vector", lambda e: e.tensor_scalar(
                nrm[0:1, 24:25], nrm[0:1, 24:25], -0.125, 7.9, ALU.mult, ALU.add),
              waits=[tok("act")], inc="dve")
            t_val = tok("dve")
            E("tensor", lambda e: e.matmul(
                bank(6)[:, 0:1], cones1x128f.ap()[0:1, :], nrm[0:1, 24:25],
                start=True, stop=True),
              waits=[t_val], inc="pe")
            E("vector", lambda e, p=p: e.tensor_copy(b2["negmh"][:, p:p + 1],
                                                     bank(6)[:, 0:1]),
              waits=[tok("pe")], inc="dve")
            t_negmh.append(tok("dve"))

        T.barrier()


        stbank_free = [NONE_TOK, NONE_TOK]
        pt_free = [NONE_TOK, NONE_TOK, NONE_TOK]
        av_free = NONE_TOK
        sums_free = NONE_TOK
        b6_free = tok("dve")


        def band_ops(qt, j):
            res = []
            e = j - 4 * qt
            if 0 <= e <= 3:
                lo = 128 * e - 2
                if e == 0:
                    res.append((0, 130, cband.ap()[:, 2:132], 0, 128))
                elif lo + 132 > 512:
                    res.append((lo, 512, cband.ap()[:, 0:512 - lo], 0, 128))
                else:
                    res.append((lo, lo + 132, cband.ap()[:, 0:132], 0, 128))
            if j == (4 * qt - 1) % KC:
                cap = ccorn.ap()[64:128, 4:6] if qt == 0 else ccorn.ap()[64:128, 0:2]
                res.append((0, 2, cap, 64, 128))
            if j == 4 * qt + 4:
                cap = ccorn.ap()[0:64, 6:8] if qt == QT - 1 else ccorn.ap()[0:64, 2:4]
                res.append((510, 512, cap, 0, 64))
            return res

        items = [(p, qt, j) for p in range(NP) for qt in range(QT) for j in range(KC)]
        av_free = [NONE_TOK, NONE_TOK, NONE_TOK, NONE_TOK]
        exp_toks = {}
        pending_tail = None
        NT2 = NP * QT
        q2_tok = [None] * NT2
        q2_qkdone = [("pe", 0), ("pe", 0)]

        def emit_qload(t2):
            p_, qt_ = t2 // QT, t2 % QT
            q2_tok[t2] = QS.dma(
                lambda e, p_=p_, qt_=qt_, t2=t2: e.dma_start(
                    out=b2[f"q2_{t2 % 2}"][:],
                    in_=q_dram[p_, :, 512 * qt_:512 * (qt_ + 1)]),
                waits=[q2_qkdone[t2 % 2]])

        emit_qload(0)
        emit_qload(1)

        def wq_steps(hidx):
            BLK = WH // 32
            B = wq
            par = hidx % 2
            src3 = wq[f"src_{par}"].rearrange("p (b k) -> p b k", k=32)
            dst = wq[f"dst_{par}"]
            dst3 = dst.rearrange("p (b k) -> p b k", k=32)
            rsb = B["rscale"].unsqueeze(2).to_broadcast([128, BLK, 32])
            ssb = B["scale"].unsqueeze(2).to_broadcast([128, BLK, 32])
            qy3 = B["qy"].rearrange("p (b k) -> p b k", k=32)
            qv3 = B["qv"].rearrange("p (b k) -> p b k", k=32)
            return [
                ("vector", lambda e: e.reduce_max(
                    B["amax"].unsqueeze(2), src3, axis=AX.X, apply_absolute_value=True)),
                ("vector", lambda e: e.tensor_scalar(
                    B["scale"][:], B["amax"][:], 1.0 / FP6_MAX, MX_EPS, ALU.mult, ALU.max)),
                ("vector", lambda e: e.reciprocal(B["rscale"][:], B["scale"][:])),
                ("pool", lambda e: e.tensor_tensor(qy3, src3, rsb, ALU.mult)),
                ("pool", lambda e: e.tensor_mul(B["qt"][:], B["qy"][:], B["qy"][:])),
                ("vector", lambda e: e.tensor_single_scalar(
                    B["qc"][:], B["qt"][:], 4.0, ALU.is_ge)),
                ("vector", lambda e: e.tensor_single_scalar(
                    B["qs"][:], B["qt"][:], 16.0, ALU.is_ge)),
                ("vector", lambda e: e.tensor_scalar(
                    B["qe"][:], B["qs"][:], -2.0, 8.0, ALU.mult, ALU.add)),
                ("vector", lambda e: e.scalar_tensor_tensor(
                    B["qr"][:], B["qc"][:], -4.0, B["qe"][:], ALU.mult, ALU.add)),
                ("vector", lambda e: e.tensor_scalar(
                    B["qe"][:], B["qs"][:], 0.25, 0.125, ALU.mult, ALU.add)),
                ("vector", lambda e: e.scalar_tensor_tensor(
                    B["qs"][:], B["qc"][:], 0.125, B["qe"][:], ALU.mult, ALU.add)),
                ("pool", lambda e: e.tensor_mul(B["qt"][:], B["qy"][:], B["qr"][:])),
                ("vector", lambda e: e.tensor_single_scalar(
                    B["qt"][:], B["qt"][:], -0.5, ALU.add)),
                ("vector", lambda e: e.tensor_copy(B["qvi"][:], B["qt"][:])),
                ("vector", lambda e: e.tensor_copy(B["qv"][:], B["qvi"][:])),
                ("vector", lambda e: e.tensor_tensor(B["qc"][:], B["qt"][:], B["qv"][:],
                                                     ALU.is_gt)),
                ("vector", lambda e: e.tensor_add(B["qv"][:], B["qv"][:], B["qc"][:])),
                ("pool", lambda e: e.tensor_mul(B["qv"][:], B["qv"][:], B["qs"][:])),
                ("vector", lambda e: e.tensor_scalar(
                    B["qv"][:], B["qv"][:], FP6_MAX, -FP6_MAX, ALU.min, ALU.max)),
                ("pool", lambda e: e.tensor_tensor(dst3, qv3, ssb, ALU.mult)),
            ]

        wq_queue = []
        NWH = FC * (D // WH)
        wq_state = {"tok": None, "h": 0, "dst_free": [None, None], "st": []}

        def wq_pump():
            if not wq_queue:
                if wq_state["h"] >= NWH:
                    return
                h = wq_state["h"]
                wq_state["h"] += 1
                jt, hh = h // (D // WH), h % (D // WH)
                par = h % 2
                w = [wq_state["tok"]] if wq_state["tok"] else []
                if wq_state["dst_free"][par] is not None:
                    w.append(wq_state["dst_free"][par])
                t_l = QS.dma(ld(wq[f"src_{par}"][:],
                                wffn[128 * jt:128 * (jt + 1), WH * hh:WH * (hh + 1)]),
                             waits=w)
                wq_state["tok"] = t_l
                wq_queue.extend(wq_steps(h))
                wq_queue.append(("dma", h))
                return
            item = wq_queue.pop(0)
            if item[0] == "dma":
                h = item[1]
                jt, hh = h // (D // WH), h % (D // WH)
                par = h % 2
                t_s = QS.dma(lambda e, jt=jt, hh=hh, par=par: e.dma_start(
                    out=wd_dram[jt, :, WH * hh:WH * (hh + 1)],
                    in_=wq[f"dst_{par}"][:]),
                    waits=[wq_state["tok"]])
                wq_state["dst_free"][par] = t_s
                wq_state["st"].append(t_s)
                return
            eng, fn = item
            sem = {"scalar": "act", "pool": "pool"}.get(eng, "dve")
            E(eng, fn, waits=[wq_state["tok"], tok(sem)], inc=sem)
            wq_state["tok"] = tok(sem)

        def emit_qk_band_exp(n, p, qt, j):
            u = n % 2
            stv = bank(2 * u, 2)
            t2 = p * QT + qt
            qsl = b2[f"q2_{t2 % 2}"]
            w0 = [stbank_free[u]]
            if j == 0:
                w0 = w0 + [q2_tok[t2]]
            E("tensor", lambda e, j=j, p=p, stv=stv, qsl=qsl: e.matmul(
                stv[:, 0:512],
                K_sb[0:64, p * S_kv + 128 * j: p * S_kv + 128 * (j + 1)],
                qsl[0:64, :],
                start=True, stop=True, tile_position=(0, 0)),
              waits=w0, inc=None)
            E("tensor", lambda e, j=j, p=p, stv=stv, qsl=qsl: e.matmul(
                stv[:, 512:1024],
                K_sb[64:128, p * S_kv + 128 * j: p * S_kv + 128 * (j + 1)],
                qsl[64:128, :],
                start=True, stop=True, tile_position=(64, 0)),
              inc="pe")
            t_qk = tok("pe")
            if j == KC - 1:
                q2_qkdone[t2 % 2] = t_qk
                if t2 + 2 < NT2:
                    emit_qload(t2 + 2)
            pt = b2[f"pt_{n % 3}"]
            w = [t_qk, pt_free[n % 3]] + ([t_negmh[p]] if n == 0 else [])
            E("scalar", lambda e, stv=stv, pt=pt, p=p: e.activation(
                pt[:], stv, AF.Exp, bias=b2["negmh"][:, p:p + 1], scale=0.125),
              waits=w, inc="act")
            t_exp = tok("act")
            stbank_free[u] = t_exp
            for (c0, c1, cap, r0, r1) in band_ops(qt, j):
                W = c1 - c0
                pt3 = pt.rearrange("p (h w) -> p h w", h=2)[r0:r1, :, c0:c1]
                cap3 = cap.unsqueeze(1).to_broadcast([r1 - r0, 2, W])
                E("pool", lambda e, pt3=pt3, cap3=cap3: e.tensor_tensor(
                    pt3, pt3, cap3, ALU.mult),
                  waits=[t_exp, tok("pool")], inc="pool")
                t_exp = tok("pool")
            exp_toks[n] = t_exp

        def emit_av(n, p, qt, j):
            pt = b2[f"pt_{n % 3}"]
            tp2 = (p * QT + qt) % 2
            ava, avb = bank(4 + 2 * tp2), bank(5 + 2 * tp2)
            w = [exp_toks.pop(n)]
            if j == 0:
                w += [av_free[2 * tp2], av_free[2 * tp2 + 1], t_vones]
            E("tensor", lambda e, p=p, j=j, pt=pt, ava=ava: e.matmul(
                ava[0:65, :],
                V_sb[:, j * VW + 65 * (2 * p): j * VW + 65 * (2 * p) + 65],
                pt[:, 0:512],
                start=(j == 0), stop=(j == KC - 1)),
              waits=w, inc=None)
            E("tensor", lambda e, p=p, j=j, pt=pt, avb=avb: e.matmul(
                avb[0:65, :],
                V_sb[:, j * VW + 65 * (2 * p + 1): j * VW + 65 * (2 * p + 2)],
                pt[:, 512:1024],
                start=(j == 0), stop=(j == KC - 1)),
              inc="pe")
            pt_free[n % 3] = tok("pe")

        lsb_free = NONE_TOK
        rlb_free = NONE_TOK
        stageb_free = NONE_TOK

        def emit_tail1(p, qt):
            nonlocal lsb_free
            t_avmm = tok("pe")
            tp2 = (p * QT + qt) % 2
            ava, avb = bank(4 + 2 * tp2), bank(5 + 2 * tp2)
            E("vector", lambda e, ava=ava: e.tensor_copy(
                b2["lsb"][64:65, 0:512], ava[64:65, :]),
              waits=[t_avmm, lsb_free, rlb_free], inc="dve")
            E("vector", lambda e, avb=avb: e.tensor_copy(
                b2["lsb"][64:65, 512:1024], avb[64:65, :]),
              waits=[tok("dve")], inc="dve")
            E("vector", lambda e: e.reciprocal(b2["lsb"][64:65, :],
                                               b2["lsb"][64:65, :]),
              waits=[tok("dve")], inc="dve")
            t_rl = tok("dve")
            lsb_free = t_rl
            return t_avmm, t_rl

        def emit_tail1b(t_rl):
            t_st = QS.dma(lambda e: e.dma_start(out=rl_dram, in_=b2["lsb"][64:65, :]),
                          waits=[t_rl])
            t_bc = QS.dma(lambda e: e.dma_start(
                out=b2["rlb"][:], in_=rl_dram.to_broadcast([128, 1024])),
                waits=[t_st])
            return t_bc

        def emit_tail2(p, qt, t_avmm_tile, t_bc):
            nonlocal rlb_free, stageb_free
            tp2 = (p * QT + qt) % 2
            ava, avb = bank(4 + 2 * tp2), bank(5 + 2 * tp2)
            ao_a = b2["attnout"][0:64, (p * QT + qt) * 512: (p * QT + qt + 1) * 512]
            ao_b = b2["attnout"][64:128, (p * QT + qt) * 512: (p * QT + qt + 1) * 512]
            E("vector", lambda e, ao_a=ao_a, ava=ava: e.tensor_tensor(
                ao_a, ava[0:64, :], b2["rlb"][0:64, 0:512], ALU.mult),
              waits=[t_bc, t_avmm_tile], inc="dve")
            av_free[2 * tp2] = tok("dve")
            E("vector", lambda e, avb=avb: e.tensor_tensor(
                b2["stageb"][0:64, :], avb[0:64, :], b2["rlb"][0:64, 512:1024],
                ALU.mult),
              waits=[stageb_free], inc="dve")
            t_avn_b = tok("dve")
            av_free[2 * tp2 + 1] = t_avn_b
            rlb_free = t_avn_b
            stageb_free = QS.dma(lambda e, ao_b=ao_b: e.dma_start(
                out=ao_b, in_=b2["stageb"][0:64, :]),
                waits=[t_avn_b])
            return stageb_free

        LAG = 3
        pending_1b = None

        def run_pending(n):
            nonlocal pending_1b, pending_tail
            if pending_1b is not None and n >= pending_1b[0]:
                _, tp, tqt, t_avm, t_rl = pending_1b
                t_bc = emit_tail1b(t_rl)
                pending_tail = (pending_1b[0] + 3, tp, tqt, t_avm, t_bc)
                pending_1b = None
            if pending_tail is not None and n >= pending_tail[0]:
                _, tp, tqt, t_avm, t_bc = pending_tail
                emit_tail2(tp, tqt, t_avm, t_bc)
                pending_tail = None

        for n, (p, qt, j) in enumerate(items):
            if n % 3 == 1:
                wq_pump()
            run_pending(n)
            if n >= LAG:
                pp, pq, pj = items[n - LAG]
                emit_av(n - LAG, pp, pq, pj)
                if pj == KC - 1:
                    t_avm, t_rl = emit_tail1(pp, pq)
                    pending_1b = (n + 3, pp, pq, t_avm, t_rl)
            emit_qk_band_exp(n, p, qt, j)
        for m in range(len(items) - LAG, len(items)):
            run_pending(m + len(items))
            pp, pq, pj = items[m]
            emit_av(m, pp, pq, pj)
            if pj == KC - 1:
                t_avm, t_rl = emit_tail1(pp, pq)
                pending_1b = (0, pp, pq, t_avm, t_rl)
        run_pending(10 ** 9)
        run_pending(10 ** 9)

        T.barrier()

        for c in range(FC):
            QG.dma(ld(p3["woT"][:, c * D:(c + 1) * D], woT[128 * c:128 * (c + 1), :]))
        t_wo = QG.total_tokens()

        def quant_steps(src_f32, dst_bf16, par, tab=None, blk=None):
            BLK = blk if blk is not None else D // 32
            B = tab if tab is not None else {
                k: p3[f"{k}_{par}"] for k in
                ("qy", "qt", "qv", "qc", "qr", "qs", "qe", "qvi",
                 "amax", "scale", "rscale")}
            src3 = src_f32.rearrange("p (b k) -> p b k", k=32)
            dst3 = dst_bf16.rearrange("p (b k) -> p b k", k=32)
            rsb = B["rscale"].unsqueeze(2).to_broadcast([128, BLK, 32])
            ssb = B["scale"].unsqueeze(2).to_broadcast([128, BLK, 32])
            qy3 = B["qy"].rearrange("p (b k) -> p b k", k=32)
            qv3 = B["qv"].rearrange("p (b k) -> p b k", k=32)
            tt = "vector"
            return [
                ("vector", lambda e: e.reduce_max(
                    B["amax"].unsqueeze(2), src3, axis=AX.X, apply_absolute_value=True)),
                ("vector", lambda e: e.tensor_scalar(
                    B["scale"][:], B["amax"][:], 1.0 / FP6_MAX, MX_EPS, ALU.mult, ALU.max)),
                ("vector", lambda e: e.reciprocal(B["rscale"][:], B["scale"][:])),
                (tt, lambda e: e.tensor_tensor(qy3, src3, rsb, ALU.mult)),
                ("scalar", lambda e: e.activation(B["qt"][:], B["qy"][:], AF.Square)),
                ("vector", lambda e: e.tensor_single_scalar(
                    B["qc"][:], B["qt"][:], 4.0, ALU.is_ge)),
                ("vector", lambda e: e.tensor_single_scalar(
                    B["qs"][:], B["qt"][:], 16.0, ALU.is_ge)),
                ("scalar", lambda e: e.activation(B["qe"][:], B["qs"][:], AF.Copy,
                                                  scale=-2.0, bias=8.0)),
                ("vector", lambda e: e.scalar_tensor_tensor(
                    B["qr"][:], B["qc"][:], -4.0, B["qe"][:], ALU.mult, ALU.add)),
                ("scalar", lambda e: e.activation(B["qe"][:], B["qs"][:], AF.Copy,
                                                  scale=0.25, bias=0.125)),
                ("vector", lambda e: e.scalar_tensor_tensor(
                    B["qs"][:], B["qc"][:], 0.125, B["qe"][:], ALU.mult, ALU.add)),
                ("vector", lambda e: e.tensor_mul(B["qt"][:], B["qy"][:], B["qr"][:])),
                ("scalar", lambda e: e.activation(B["qt"][:], B["qt"][:], AF.Copy,
                                                  scale=1.0, bias=-0.5)),
                ("vector", lambda e: e.tensor_copy(B["qvi"][:], B["qt"][:])),
                ("vector", lambda e: e.tensor_copy(B["qv"][:], B["qvi"][:])),
                ("vector", lambda e: e.tensor_tensor(B["qc"][:], B["qt"][:], B["qv"][:],
                                                     ALU.is_gt)),
                (tt, lambda e: e.tensor_add(B["qv"][:], B["qv"][:], B["qc"][:])),
                (tt, lambda e: e.tensor_mul(B["qv"][:], B["qv"][:], B["qs"][:])),
                ("vector", lambda e: e.tensor_scalar(
                    B["qv"][:], B["qv"][:], FP6_MAX, -FP6_MAX, ALU.min, ALU.max)),
                (tt, lambda e: e.tensor_tensor(dst3, qv3, ssb, ALU.mult)),
            ]

        qbuf_free = [NONE_TOK, NONE_TOK]

        def run_quant(tiles):
            chains = []
            for tt_ in tiles:
                src_, dst_, par, pre = tt_[:4]
                tab = tt_[4] if len(tt_) > 4 else None
                chains.append([list(pre) + ([qbuf_free[par]] if tab is None else []),
                               quant_steps(src_, dst_, par, tab)])
            nsteps = max(len(c[1]) for c in chains)
            finals = [None] * len(tiles)
            s3toks = [None] * len(tiles)
            for k in range(nsteps):
                for t, (wt, steps) in enumerate(chains):
                    if k >= len(steps):
                        continue
                    eng, fn = steps[k]
                    sem = {"scalar": "act", "pool": "pool"}.get(eng, "dve")
                    E(eng, fn, waits=wt + [tok(sem)], inc=sem)
                    chains[t][0] = [tok(sem)]
                    finals[t] = tok(sem)
                    if k == 3:
                        s3toks[t] = tok(sem)
            for (tt, tile) in enumerate(tiles):
                if len(tile) <= 4:
                    qbuf_free[tile[2]] = finals[tt]
            return finals, s3toks

        while wq_queue or wq_state["h"] < NWH:
            wq_pump()
        wf_free = [NONE_TOK, NONE_TOK]
        wd_trbank_free = [NONE_TOK, NONE_TOK]
        wdl = {}
        wdl_free = [NONE_TOK, NONE_TOK]
        for jt in range(FC):
            par = jt % 2
            wstg = wq[f"wstg_{par}"][:]
            t_ld = QS.dma(ld(wstg, wd_dram[jt]),
                          waits=wq_state["st"][-4:] + [wdl_free[par]])
            tr_out = bankb(4 + par)
            for c in range(FC):
                w = [t_ld, wd_trbank_free[par]] if c == 0 else []
                E("tensor", lambda e, c=c, tr_out=tr_out, wstg=wstg: e.transpose(
                    tr_out[:, 128 * c:128 * (c + 1)],
                    wstg[:, 128 * c:128 * (c + 1)],
                    cident.ap()[:]),
                  waits=w, inc="pe" if c == FC - 1 else None)
            t_tr = tok("pe")
            wdl_free[par] = t_tr
            wd_v = p3["wd"].rearrange("p (c j) -> p c j", c=FC)[:, :, 128 * jt:128 * (jt + 1)]
            tr_v = tr_out[:, 0:FC * 128].rearrange("p (c j) -> p c j", c=FC)
            E("vector", lambda e, wd_v=wd_v, tr_v=tr_v: e.tensor_copy(wd_v, tr_v),
              waits=[t_tr], inc="dve")
            wd_trbank_free[par] = tok("dve")
        t_wd = tok("dve")

        T.barrier()

        p3buf = {"dummy": b1["xstage_0"], "lsum": p3["l3sum"], "lvar": p3["l3var"],
                 "lrstd": p3["l3rstd"], "lnmr": p3["l3nmr"], "lmusq": p3["l3musq"]}
        x3_free = list(wf_free)
        ffnbank_free = NONE_TOK
        xdtr_free = [NONE_TOK, NONE_TOK]
        x1_free = [NONE_TOK] * 4
        xd_free = [NONE_TOK, NONE_TOK]
        xdT_free = [NONE_TOK, NONE_TOK]
        ln3_reader = [("act", 0), ("act", 0)]
        out_dma_par = [NONE_TOK, NONE_TOK]
        out_toks = []

        t_op_toks = {}
        t_x_toks = {}
        x1_toks = {}
        opbank_tok = {}
        tr_toks = {}
        xd_ready = {}
        apply_tok = {}

        def p3_stage1(i):
            ib = i % 2
            xst = p3[f"wfstage_{ib}"][:]
            t_x_toks[i] = QS.dma(ld(xst, x_kv[128 * i:128 * (i + 1), :]),
                                 waits=[x3_free[ib]])
            qt, s4 = i // 4, i % 4
            ops = bank(2 * ib, 2)[:, 0:D]
            for jh in range(JH):
                w = ([opbank_tok.pop(i - 2)] if (i - 2) in opbank_tok else []) \
                    if jh == 0 else []
                if i == 0 and jh == 0:
                    w += t_wo
                E("tensor", lambda e, ops=ops, jh=jh: e.matmul(
                    ops[:, 512 * jh:512 * (jh + 1)],
                    cones1x128.ap()[32:33, :], bo_row[:, 512 * jh:512 * (jh + 1)],
                    start=True, stop=False),
                  waits=w, inc=None)
                for c in range(FC):
                    ao = b2["attnout"][:, (c * QT + qt) * 512 + 128 * s4:
                                       (c * QT + qt) * 512 + 128 * (s4 + 1)]
                    E("tensor", lambda e, ao=ao, c=c, jh=jh, ops=ops: e.matmul(
                        ops[:, 512 * jh:512 * (jh + 1)],
                        ao, p3["woT"][:, c * D + 512 * jh: c * D + 512 * (jh + 1)],
                        start=False, stop=(c == FC - 1)),
                      inc="pe" if (c == FC - 1 and jh == JH - 1) else None)
            t_op_toks[i] = tok("pe")

        def p3_tr(i):
            ib = i % 2
            tr_out = bankb(4 + ib)
            t_xd = xd_ready.pop(i)
            for c in range(FC):
                w = [t_xd, xdtr_free[ib]] if c == 0 else []
                E("tensor", lambda e, c=c, tr_out=tr_out, ib=ib: e.transpose(
                    tr_out[:, 128 * c:128 * (c + 1)],
                    p3[f"xd_{ib}"][:, 128 * c:128 * (c + 1)], cident.ap()[:]),
                  waits=w, inc="pe" if c == FC - 1 else None)
            t_tr = tok("pe")
            xd_free[ib] = t_tr
            tr_toks[i] = t_tr

        def mk_quant_chain_m(i0):
            steps = []
            for k_, (eng, fn) in enumerate(quant_steps(
                    p3m["src"], p3m["dst"], 0, tab=p3m, blk=2 * (D // 32))):
                extras = []
                key = None
                if k_ == 3:
                    key = "qy"
                elif k_ == 7:
                    extras = [out_dma_par[0], out_dma_par[1]]
                elif k_ == 10:
                    key = "qelast"
                elif k_ == 19:
                    key = "final"
                steps.append((eng, fn, extras, key))
            return {"st8": [apply_tok.pop(i0, ("act", 0)),
                            apply_tok.pop(i0 + 1, ("act", 0)),
                            xd_free[0], xd_free[1], qbuf_free[0], qbuf_free[1]],
                    "steps": steps, "rec": {}}

        def mk_copy_chain(pair_lo):
            steps = []
            for par in range(2):
                i = pair_lo + par
                ib = i % 2
                tr_out = bankb(4 + ib)
                xdt = p3[f"xdT_{ib}"][:]
                steps.append(("scalar",
                              lambda e, xdt=xdt, tr_out=tr_out: e.activation(
                                  xdt, tr_out[:, 0:FC * 128], AF.Copy),
                              [tr_toks.pop(i), xdT_free[ib]], ("xdt", i)))
            return {"st8": [], "steps": steps, "rec": {}}

        def mk_prep_chain(i0p, chA, chB):
            steps = [None] * 19
            for par, pad in ((0, 0), (1, 1)):
                i = i0p + par
                ib, i4 = i % 2, i % 4
                xst = p3[f"wfstage_{ib}"][:]
                ops = bank(2 * ib, 2)[:, 0:D]
                x1 = p3[f"x1_{i4}"][:]
                steps += [None] * pad
                steps.append(("vector",
                              lambda e, x1=x1, ops=ops, xst=xst: e.tensor_tensor(
                                  x1, ops, xst, ALU.add),
                              [t_op_toks.pop(i), t_x_toks.pop(i),
                               (lambda i4=i4: x1_free[i4])], ("x1", par)))
                x1v = p3[f"x1_{i4}"][:]
                steps.append(("scalar",
                              lambda e, x1v=x1v, ib=ib: e.activation(
                                  p3buf["dummy"][:], x1v, AF.Copy,
                                  accum_out=p3buf["lsum"][:, ib:ib + 1]),
                              [ln3_reader[ib]], None))
                steps.append(("scalar",
                              lambda e, x1v=x1v, ib=ib: e.activation(
                                  p3buf["dummy"][:], x1v, AF.Square,
                                  accum_out=p3buf["lvar"][:, ib:ib + 1]),
                              [], None))
            buf = p3buf
            sl = slice(0, 2)
            steps += [
                ("vector", lambda e: e.tensor_single_scalar(
                    buf["lsum"][:, sl], buf["lsum"][:, sl], 1.0 / D, ALU.mult),
                 [], None),
                ("vector", lambda e: e.tensor_mul(
                    buf["lmusq"][:, sl], buf["lsum"][:, sl], buf["lsum"][:, sl]),
                 [], None),
                ("vector", lambda e: e.tensor_scalar(
                    buf["lvar"][:, sl], buf["lvar"][:, sl], 1.0 / D, LN_EPS,
                    ALU.mult, ALU.add), [], None),
                ("vector", lambda e: e.tensor_sub(
                    buf["lvar"][:, sl], buf["lvar"][:, sl], buf["lmusq"][:, sl]),
                 [], None),
                ("scalar", lambda e: e.activation(
                    buf["lrstd"][:, sl], buf["lvar"][:, sl], AF.Ln), [], None),
                ("scalar", lambda e: e.activation(
                    buf["lrstd"][:, sl], buf["lrstd"][:, sl], AF.Exp,
                    scale=-0.5), [], None),
                ("vector", lambda e: e.scalar_tensor_tensor(
                    buf["lnmr"][:, sl], buf["lsum"][:, sl], -1.0,
                    buf["lrstd"][:, sl], ALU.mult, ALU.mult), [], None),
            ]
            for par in range(2):
                i = i0p + par
                ib, i4 = i % 2, i % 4
                ch = chA if par == 0 else chB
                steps.append(("scalar",
                              lambda e, i4=i4, ib=ib: e.activation(
                                  p3[f"xhat2_{ib}"][:], p3[f"x1_{i4}"][:],
                                  AF.Identity,
                                  bias=p3buf["lnmr"][:, ib:ib + 1],
                                  scale=p3buf["lrstd"][:, ib:ib + 1]),
                              [(lambda ch=ch: ch["rec"]["qy"])], ("apply", par)))
            return {"st8": [], "steps": steps, "rec": {}}

        def run_zip(chains, mids=()):
            midmap = dict(mids)
            nsteps = max(len(c["steps"]) for c in chains)
            for r in range(nsteps):
                if r in midmap:
                    midmap.pop(r)()
                for c in chains:
                    if r >= len(c["steps"]) or c["steps"][r] is None:
                        continue
                    eng, fn, extras, key = c["steps"][r]
                    sem = {"scalar": "act", "pool": "pool"}.get(eng, "dve")
                    ext = [x() if callable(x) else x for x in extras]
                    E(eng, fn, waits=c["st8"] + ext + [tok(sem)], inc=sem)
                    c["st8"] = [tok(sem)]
                    if key is not None:
                        c["rec"][key] = tok(sem)
            for fn in midmap.values():
                fn()

        def p3_ffn(i, chC, qch):
            nonlocal ffnbank_free
            ib = i % 2
            i4 = i % 4
            x1 = p3[f"x1_{i4}"][:]
            xdt = p3[f"xdT_{ib}"][:]
            fps = bank(6, 2)[:, 0:D]
            for jh in range(JH):
                w = ([ffnbank_free, chC["rec"][("xdt", i)]] if jh == 0 else [])
                if i == 0 and jh == 0:
                    w += [t_wd]
                E("tensor", lambda e, fps=fps, jh=jh: e.matmul(
                    fps[:, 512 * jh:512 * (jh + 1)],
                    cones1x128.ap()[64:65, :], fb_row[:, 512 * jh:512 * (jh + 1)],
                    start=True, stop=False),
                  waits=w, inc=None)
                for c in range(FC):
                    E("tensor", lambda e, c=c, jh=jh, fps=fps, xdt=xdt: e.matmul(
                        fps[:, 512 * jh:512 * (jh + 1)],
                        xdt[:, 128 * c:128 * (c + 1)],
                        p3["wd"][:, c * D + 512 * jh: c * D + 512 * (jh + 1)],
                        start=False, stop=(c == FC - 1)),
                      inc="pe" if (c == FC - 1 and jh == JH - 1) else None)
            t_ffn = tok("pe")
            outst = p3[f"outst_{ib}"][:]
            w = [t_ffn, x1_toks.pop(i)]
            if qch is not None:
                w.append(qch["rec"]["qelast"])
            if len(out_toks) >= 2:
                w.append(out_toks[-2])
            E("vector", lambda e, outst=outst, fps=fps, x1=x1: e.tensor_tensor(
                outst, fps, x1, ALU.add),
              waits=w, inc="dve")
            t_out = tok("dve")
            x1_free[i4] = t_out
            ffnbank_free = t_out
            xdT_free[ib] = t_ffn
            out_toks.append(QS.dma(lambda e, outst=outst, i=i: e.dma_start(
                out=out_d[128 * i:128 * (i + 1), :], in_=outst),
                waits=[t_out]))
            out_dma_par[ib] = out_toks[-1]

        def resolve_recs(i0, chM, chC, chD):
            for par in range(2):
                qbuf_free[par] = chM["rec"]["final"]
                xd_ready[i0 + par] = chM["rec"]["final"]
            if chC is not None:
                for par in range(2):
                    xdtr_free[(i0 - 2 + par) % 2] = chC["rec"][("xdt", i0 - 2 + par)]
            if chD is not None:
                for par in range(2):
                    i = i0 + 2 + par
                    t_x1 = chD["rec"][("x1", par)]
                    x1_toks[i] = t_x1
                    x3_free[i % 2] = t_x1
                    opbank_tok[i] = t_x1
                    t_ap = chD["rec"][("apply", par)]
                    ln3_reader[i % 2] = t_ap
                    apply_tok[i] = t_ap

        p3_stage1(0)
        p3_stage1(1)
        t_x1_0 = []
        for par in range(2):
            ib = par
            xst = p3[f"wfstage_{ib}"][:]
            ops = bank(2 * ib, 2)[:, 0:D]
            x1 = p3[f"x1_{par}"][:]
            E("vector", lambda e, x1=x1, ops=ops, xst=xst: e.tensor_tensor(
                x1, ops, xst, ALU.add),
              waits=[t_op_toks.pop(par), t_x_toks.pop(par)], inc="dve")
            x1_toks[par] = tok("dve")
            x3_free[ib] = tok("dve")
            opbank_tok[par] = tok("dve")
            t_x1_0.append(tok("dve"))
        t_sts0 = [ln_stats(p3[f"x1_{par}"][:], p3buf, par, pre=[t_x1_0[par]])
                  for par in range(2)]
        t_ch_dve0, _ = ln_chain(p3buf, 0, 2, pre=[t_sts0[-1]])
        for par in range(2):
            apply_tok[par] = ln_apply(p3[f"x1_{par}"][:], p3[f"xhat2_{par}"][:],
                                      p3buf, par, pre=[t_ch_dve0])
            ln3_reader[par] = apply_tok[par]

        NPAIR = cfg.NTQ // 2
        for kw in range(NPAIR):
            i0 = 2 * kw
            if kw >= 1:
                p3_tr(i0 - 2)
                p3_tr(i0 - 1)
            if kw + 1 < NPAIR:
                p3_stage1(i0 + 2)
                p3_stage1(i0 + 3)
            chM = mk_quant_chain_m(i0)
            chC = mk_copy_chain(i0 - 2) if kw >= 1 else None
            chD = mk_prep_chain(i0 + 2, chM, chM) if kw + 1 < NPAIR else None
            chains = [c for c in (chM, chC, chD) if c is not None]
            mids = []
            if kw >= 1:
                mids = [(18, lambda i=i0 - 2, chC=chC, chM=chM: p3_ffn(i, chC, chM)),
                        (20, lambda i=i0 - 1, chC=chC, chM=chM: p3_ffn(i, chC, chM))]
            run_zip(chains, mids)
            resolve_recs(i0, chM, chC, chD)

        i0 = cfg.NTQ - 2
        p3_tr(i0)
        p3_tr(i0 + 1)
        chC = mk_copy_chain(i0)
        run_zip([chC])
        p3_ffn(i0, chC, None)
        p3_ffn(i0 + 1, chC, None)

        E("gpsimd", lambda e: e.engine_nop(),
          waits=QG.total_tokens() + QS.total_tokens(), inc=None)

        for name, filt in (("sync", ("sync",)), ("tensor", ("tensor",)),
                           ("scalar", ("scalar",)), ("vector", ("vector",)),
                           ("gpsimd", ("gpsimd", "pool"))):
            def body(eng, filt=filt):
                T.replay(filt, eng, sems)
            getattr(blk, name)(body)

    return nc



def _make_in_maps(cfg, x, Wq, Wk, Wv, Wo, bq, bk, bv, bo, ocr, Wf, fbias, n_cores):
    import ml_dtypes
    bft = ml_dtypes.bfloat16
    B, S, D = x.shape
    per_b = n_cores // B
    S_q = cfg.S_q
    FC = cfg.FC
    shared = {
        "wqT": np.ascontiguousarray(Wq.T), "wkT": np.ascontiguousarray(Wk.T),
        "wvT": np.ascontiguousarray(Wv.T), "woT": np.ascontiguousarray(Wo.T),
        "wffn": Wf,
        "bq2": np.ascontiguousarray(bq.reshape(FC, 128).T),
        "bk2": np.ascontiguousarray(bk.reshape(FC, 128).T),
        "ocr2": np.ascontiguousarray(ocr.reshape(FC, 128).T),
        "bv_row": bv.reshape(1, D).copy(), "bo_row": bo.reshape(1, D).copy(),
        "fb_row": fbias.reshape(1, D).copy(),
        "ident": np.eye(128, dtype=bft),
        "ones128": np.ones((128, 1), dtype=bft),
        "ones1x128": np.ones((65, 128), dtype=bft),
        "ones1x128f": np.ones((65, 128), dtype=np.float32),
        "band_main": band_main_np().astype(bft),
    }
    in_maps = []
    for c in range(n_cores):
        b, half = c // per_b, c % per_b
        q_off = half * S_q
        m = dict(shared)
        m["x_kv"] = np.ascontiguousarray(np.roll(x[b], -q_off, axis=0))
        corners = np.zeros((128, 8), dtype=np.float32)
        corners[:, 0:2] = corner_pred_np(True)
        corners[:, 2:4] = corner_succ_np(True)
        corners[:, 4:6] = corner_pred_np(q_off > 0)
        corners[:, 6:8] = corner_succ_np(q_off + S_q < S)
        m["corners"] = corners.astype(bft)
        in_maps.append(m)
    return in_maps


def kernel(L0_attn_x, L0_attn_ocr_context, L0_attn_Wq, L0_attn_bq, L0_attn_Wk,
           L0_attn_bk, L0_attn_Wv, L0_attn_bv, L0_attn_Wo, L0_attn_bo,
           L0_attn_ocr_bias, L0_ffn_weight, L0_ffn_bias):
    x = np.asarray(L0_attn_x, dtype=np.float32)
    B, S, D = x.shape
    n_cores = 8
    per_b = n_cores // B
    S_q = S // per_b
    cfg = Cfg(D, S_q, S)

    f32 = lambda a: np.ascontiguousarray(np.asarray(a, dtype=np.float32))
    in_maps = _make_in_maps(
        cfg, x, f32(L0_attn_Wq), f32(L0_attn_Wk), f32(L0_attn_Wv), f32(L0_attn_Wo),
        f32(L0_attn_bq), f32(L0_attn_bk), f32(L0_attn_bv), f32(L0_attn_bo),
        f32(L0_attn_ocr_bias), f32(L0_ffn_weight), f32(L0_ffn_bias), n_cores)

    nc = build_nc(cfg)
    res = run_bass_kernel_spmd(nc, in_maps, core_ids=list(range(n_cores)))
    out = np.empty((B, S, D), dtype=np.float32)
    for c in range(n_cores):
        b, half = c // per_b, c % per_b
        out[b, half * S_q:(half + 1) * S_q, :] = res.results[c]["out"]
    return out


# version 49
# speedup vs baseline: 1.0145x; 1.0145x over previous
import numpy as np
from contextlib import ExitStack

import concourse.bass as bass
import concourse.mybir as mybir
from concourse.bass_utils import run_bass_kernel_spmd

F32 = mybir.dt.float32
BF16 = mybir.dt.bfloat16
I32 = mybir.dt.int32
U8 = mybir.dt.uint8
AF = mybir.ActivationFunctionType
ALU = mybir.AluOpType
AX = mybir.AxisListType

HD = 64
LN_EPS = 1e-5
FP6_MAX = 7.5
MX_EPS = 1e-30
DTB = {F32: 4, BF16: 2, I32: 4}
NONE_TOK = ("dve", 0)


class Cfg:
    def __init__(self, D, S_q, S_kv):
        self.D, self.S_q, self.S_kv = D, S_q, S_kv
        self.H = D // HD
        self.NP = self.H // 2
        self.FC = D // 128
        self.KC = S_kv // 128
        self.QT = S_q // 512
        self.NG = S_kv // 512
        self.NGQ = S_q // 512
        self.NTQ = S_q // 128
        self.JH = D // 512
        assert self.NP == self.FC


class Trace:
    def __init__(self):
        self.ops = []
        self.cnt = {}
        self.pending = {}

    def tok(self, sem):
        return (sem, self.cnt.get(sem, 0))

    def barrier(self):
        toks = [self.tok(s) for s in
                ("pe", "act", "dve", "pool",
                 "dxa", "dxb", "dxc", "dxd", "dwa", "dwb", "dwc", "dwd",
                 "dva", "dvb")]
        toks = [t for t in toks if t[1] > 0]
        for e in ("sync", "tensor", "scalar", "vector", "gpsimd"):
            self.pending.setdefault(e, []).extend(toks)

    def emit(self, eng, fn, waits=(), inc=None, inc_by=1):
        ws = list(self.pending.pop(eng, []))
        for t in waits:
            if t is None or t[0] is None or t[1] <= 0:
                continue
            s, v = t
            assert v <= self.cnt.get(s, 0), f"wait on future {s}={v} (now {self.cnt.get(s, 0)})"
            ws.append((s, v))
        if inc is not None:
            self.cnt[inc] = self.cnt.get(inc, 0) + inc_by
        self.ops.append((eng, fn, ws, inc, inc_by))
        return self.tok(inc) if inc else None

    def replay(self, engine_names, eng, sems):
        seen = {}
        for (e, fn, ws, inc, inc_by) in self.ops:
            if e not in engine_names:
                continue
            for (s, v) in ws:
                if seen.get(s, 0) >= v:
                    continue
                eng.wait_ge(sems[s], v)
                seen[s] = v
            inst = fn(eng)
            if inc is not None:
                assert inst is not None, f"op on {e} incrementing {inc} returned None"
                inst.then_inc(sems[inc], inc_by)


class DmaQ:

    def __init__(self, trace, engine, lane_sems):
        self.T = trace
        self.engine = engine
        self.lanes = lane_sems
        self.uses = [0] * len(lane_sems)
        self.i = 0

    def dma(self, fn, waits=()):
        ln = self.i % len(self.lanes)
        self.i += 1
        sem = self.lanes[ln]
        issue_waits = list(waits) + [(sem, 16 * self.uses[ln])]
        self.uses[ln] += 1
        self.T.emit(self.engine, fn, waits=issue_waits, inc=sem, inc_by=16)
        return (sem, 16 * self.uses[ln])

    def total_tokens(self):
        return [(s, 16 * u) for s, u in zip(self.lanes, self.uses) if u > 0]


BAND_MULT = float(np.exp(np.float64(0.1)))


def band_main_np():
    b = np.ones((128, 132), dtype=np.float32)
    for p in range(128):
        b[p, p:p + 5] = BAND_MULT
    return b


def corner_pred_np(on):
    c = np.ones((128, 2), dtype=np.float32)
    if on:
        c[126, 0] = BAND_MULT
        c[127, 0] = BAND_MULT
        c[127, 1] = BAND_MULT
    return c


def corner_succ_np(on):
    c = np.ones((128, 2), dtype=np.float32)
    if on:
        c[0, 0] = BAND_MULT
        c[0, 1] = BAND_MULT
        c[1, 1] = BAND_MULT
    return c


def build_nc(cfg: Cfg):
    D, FC, KC, QT, NP, JH = cfg.D, cfg.FC, cfg.KC, cfg.QT, cfg.NP, cfg.JH
    S_q, S_kv = cfg.S_q, cfg.S_kv

    nc = bass.Bass("TRN2", target_bir_lowering=False)
    ap = lambda t: t.ap()

    x_kv = ap(nc.dram_tensor("x_kv", [S_kv, D], F32, kind="ExternalInput"))
    wqT = ap(nc.dram_tensor("wqT", [D, D], F32, kind="ExternalInput"))
    wkT = ap(nc.dram_tensor("wkT", [D, D], F32, kind="ExternalInput"))
    wvT = ap(nc.dram_tensor("wvT", [D, D], F32, kind="ExternalInput"))
    woT = ap(nc.dram_tensor("woT", [D, D], F32, kind="ExternalInput"))
    wffn = ap(nc.dram_tensor("wffn", [D, D], F32, kind="ExternalInput"))
    bq2_d = ap(nc.dram_tensor("bq2", [128, FC], F32, kind="ExternalInput"))
    bk2_d = ap(nc.dram_tensor("bk2", [128, FC], F32, kind="ExternalInput"))
    ocr2_d = ap(nc.dram_tensor("ocr2", [128, FC], F32, kind="ExternalInput"))
    bv_d = ap(nc.dram_tensor("bv_row", [1, D], F32, kind="ExternalInput"))
    bo_d = ap(nc.dram_tensor("bo_row", [1, D], F32, kind="ExternalInput"))
    fb_d = ap(nc.dram_tensor("fb_row", [1, D], F32, kind="ExternalInput"))
    ident_d = ap(nc.dram_tensor("ident", [128, 128], BF16, kind="ExternalInput"))
    ones128_d = ap(nc.dram_tensor("ones128", [128, 1], BF16, kind="ExternalInput"))
    ones1x128_d = ap(nc.dram_tensor("ones1x128", [65, 128], BF16, kind="ExternalInput"))
    ones1x128f_d = ap(nc.dram_tensor("ones1x128f", [65, 128], F32, kind="ExternalInput"))
    band_d = ap(nc.dram_tensor("band_main", [128, 132], BF16, kind="ExternalInput"))
    corn_d = ap(nc.dram_tensor("corners", [128, 8], BF16, kind="ExternalInput"))
    out_d = ap(nc.dram_tensor("out", [S_q, D], F32, kind="ExternalOutput"))
    q_dram = ap(nc.dram_tensor("q_scratch", [FC, 128, S_q], BF16))
    wd_dram = ap(nc.dram_tensor("wd_scratch", [FC, 128, D], BF16))
    rl_dram = ap(nc.dram_tensor("rl_scratch", [1, 1024], F32))

    T = Trace()
    E, tok = T.emit, T.tok

    with ExitStack() as st:
        def layout(specs):
            out, o = {}, 0
            for name, dt, n in specs:
                o = (o + 63) // 64 * 64
                out[name] = (o, dt, n)
                o += n * DTB[dt]
            return out, o

        p3spec, p3_bytes = layout(
            [(nm, F32, D) for nm in
             ("x1_0", "x1_1", "xhat2_0", "xhat2_1",
              "wfstage_0", "wfstage_1",
              "qy_0", "qy_1", "qt_0", "qt_1", "qv_0", "qv_1",
              "qc_0", "qc_1", "qr_0", "qr_1", "qs_0", "qs_1",
              "qe_0", "qe_1",
              "x1_2", "x1_3")]
            + [(nm, F32, 2) for nm in
               ("l3sum", "l3var", "l3rstd", "l3nmr", "l3musq")]
            + [(nm, F32, D // 32) for nm in
               ("amax_0", "amax_1", "scale_0", "scale_1", "rscale_0", "rscale_1")]
            + [(nm, BF16, D) for nm in ("xd_0", "xd_1", "xdT_0", "xdT_1")]
            + [("woT", BF16, FC * D), ("wd", BF16, FC * D)])
        VW = (D // HD) * 65
        A_BYTES = max(FC * S_kv * 2 + KC * VW * 2, p3_bytes)

        b1spec, b1_bytes = layout(
            [("w1", BF16, FC * D), ("w2", BF16, FC * D), ("w3", BF16, FC * D),
             ("xTg", BF16, FC * 512),
             ("xstage_0", BF16, D),
             ("xhat_0", BF16, D), ("xhat_1", BF16, D),
             ("xhat_2", BF16, D), ("xhat_3", BF16, D),
             ("qst_0", BF16, 512), ("qst_1", BF16, 512),
             ("qqs_0", BF16, 512), ("qqs_1", BF16, 512),
             ("qqs_2", BF16, 512), ("lstats", F32, 32)])

        b2spec, b2_bytes = layout(
            [("attnout", BF16, NP * QT * 512), ("pt_0", BF16, 1024),
             ("pt_1", BF16, 1024), ("pt_2", BF16, 1024),
             ("lsb", F32, 1024), ("stageb", BF16, 512),
             ("q2_0", BF16, 512), ("q2_1", BF16, 512),
             ("negmh", F32, NP), ("nrm", F32, 8)])
        WH = 512
        wq_bytes = ((7 * WH * 4 + 63) // 64 * 64 + WH * 4 + 3 * 64 +
                    2 * WH * 4 + 2 * WH * 2 + 16 * 64)
        B_BYTES = max(b1_bytes, b2_bytes,
                      (b2_bytes + 63) // 64 * 64 + wq_bytes)

        regA = st.enter_context(nc.sbuf_tensor("regA", [128, A_BYTES], U8))
        regB = st.enter_context(nc.sbuf_tensor("regB", [128, B_BYTES], U8))

        def mkview(reg, off, dt, n):
            return reg.ap()[:, off:off + n * DTB[dt]].bitcast(dt)

        K_sb = mkview(regA, 0, BF16, FC * S_kv)
        V_sb = mkview(regA, FC * S_kv * 2, BF16, KC * VW)
        p3 = {nm: mkview(regA, o, dt, n) for nm, (o, dt, n) in p3spec.items()}
        p3["outst_0"] = p3["qe_0"]
        p3["outst_1"] = p3["qe_1"]
        for _par in range(2):
            p3[f"qvi_{_par}"] = mkview(regA, p3spec[f"qr_{_par}"][0], I32, D)
        p3m = {}
        for _nm in ("qy", "qt", "qv", "qc", "qr", "qs", "qe"):
            p3m[_nm] = mkview(regA, p3spec[f"{_nm}_0"][0], F32, 2 * D)
        p3m["qvi"] = mkview(regA, p3spec["qr_0"][0], I32, 2 * D)
        for _nm in ("amax", "scale", "rscale"):
            p3m[_nm] = mkview(regA, p3spec[f"{_nm}_0"][0], F32, 2 * (D // 32))
        p3m["src"] = mkview(regA, p3spec["xhat2_0"][0], F32, 2 * D)
        p3m["dst"] = mkview(regA, p3spec["xd_0"][0], BF16, 2 * D)

        b1 = {nm: mkview(regB, o, dt, n) for nm, (o, dt, n) in b1spec.items()}
        _ls = b1spec["lstats"][0]
        for _i, _nm in enumerate(("lsum", "lvar", "lrstd", "lnmr", "lmusq")):
            b1[_nm] = mkview(regB, _ls + 16 * _i, F32, 4)
        _f0 = (b2_bytes + 63) // 64 * 64
        _o2 = [_f0]
        def wqv(dt, n):
            o_al = (_o2[0] + 63) // 64 * 64
            _o2[0] = o_al + n * DTB[dt]
            return mkview(regB, o_al, dt, n)
        wq = {}
        wq_off = {}
        for k_nm in ("qy", "qt", "qv", "qc", "qr", "qs", "qe"):
            wq_off[k_nm] = (_o2[0] + 63) // 64 * 64
            wq[k_nm] = wqv(F32, WH)
        wq["qvi"] = wqv(I32, WH)
        wq["amax"] = wqv(F32, WH // 32)
        wq["scale"] = wqv(F32, WH // 32)
        wq["rscale"] = wqv(F32, WH // 32)
        wq["src_0"] = wqv(F32, WH)
        wq["src_1"] = wqv(F32, WH)
        wq["dst_0"] = wqv(BF16, WH)
        wq["dst_1"] = wqv(BF16, WH)
        assert _o2[0] <= B_BYTES, (_o2[0], B_BYTES)
        wq["wstg_0"] = mkview(regB, wq_off["qy"], BF16, D)
        wq["wstg_1"] = mkview(regB, wq_off["qv"], BF16, D)
        b2 = {nm: mkview(regB, o, dt, n) for nm, (o, dt, n) in b2spec.items()}
        b2["qq"] = b2["pt_0"]
        b2["rlb"] = b2["lsb"]

        cband = st.enter_context(nc.sbuf_tensor("cband", [128, 132], BF16))
        ccorn = st.enter_context(nc.sbuf_tensor("ccorn", [128, 8], BF16))
        cident = st.enter_context(nc.sbuf_tensor("cident", [128, 128], BF16))
        cones128 = st.enter_context(nc.sbuf_tensor("cones128", [128, 1], BF16))
        cones1x128 = st.enter_context(nc.sbuf_tensor("cones1x128", [65, 128], BF16))
        cones1x128f = st.enter_context(nc.sbuf_tensor("cones1x128f", [65, 128], F32))
        cbq2 = st.enter_context(nc.sbuf_tensor("cbq2", [128, FC], F32))
        cbk2 = st.enter_context(nc.sbuf_tensor("cbk2", [128, FC], F32))
        cocr2 = st.enter_context(nc.sbuf_tensor("cocr2", [128, FC], F32))
        cqbias = st.enter_context(nc.sbuf_tensor("cqbias", [128, FC], F32))
        crows = st.enter_context(nc.sbuf_tensor("crows", [65, D], BF16))
        cnrm = st.enter_context(nc.sbuf_tensor("cnrm", [128, 32], F32))

        psum = st.enter_context(nc.psum_tensor("psA", [128, 4096], F32))
        psum_ap = psum.ap()
        psum_b = psum_ap.bitcast(BF16)

        def bank(k, w=1):
            return psum_ap[:, 512 * k:512 * (k + w)]

        def bankb(k, w=1):
            return psum_b[:, 1024 * k:1024 * (k + w)]

        sems = {}
        for s in ("pe", "act", "dve", "pool",
                  "dxa", "dxb", "dxc", "dxd", "dwa", "dwb", "dwc", "dwd",
                  "dva", "dvb"):
            sems[s] = st.enter_context(nc.semaphore(s))
        QS = DmaQ(T, "sync", ["dxa", "dxb", "dxc", "dxd"])
        QG = DmaQ(T, "gpsimd", ["dwa", "dwb", "dwc", "dwd"])
        QV = DmaQ(T, "vector", ["dva", "dvb"])

        blk = st.enter_context(nc.Block())

        def ld(dst, src):
            return lambda e: e.dma_start(out=dst, in_=src)

        t_consts = []
        for dst, src in [
            (cband.ap()[:], band_d), (ccorn.ap()[:], corn_d),
            (cident.ap()[:], ident_d), (cones128.ap()[:], ones128_d),
            (cones1x128.ap()[:], ones1x128_d), (cones1x128f.ap()[:], ones1x128f_d),
            (cbq2.ap()[:], bq2_d), (cbk2.ap()[:], bk2_d), (cocr2.ap()[:], ocr2_d),
        ]:
            t_consts.append(QS.dma(ld(dst, src)))

        E("vector", lambda e: e.tensor_add(cqbias.ap()[:], cbq2.ap()[:], cocr2.ap()[:]),
          waits=t_consts, inc="dve")

        t_crows = [QG.dma(ld(crows.ap()[0:1, :], bv_d)),
                   QG.dma(ld(crows.ap()[32:33, :], bo_d)),
                   QG.dma(ld(crows.ap()[64:65, :], fb_d))]
        bv_row = crows.ap()[0:1, :]
        bo_row = crows.ap()[32:33, :]
        fb_row = crows.ap()[64:65, :]

        for c in range(FC):
            QG.dma(ld(b1["w1"][:, c * D:(c + 1) * D], wkT[128 * c:128 * (c + 1), :]))
        t_w1 = QG.total_tokens()
        for c in range(FC):
            QG.dma(ld(b1["w2"][:, c * D:(c + 1) * D], wvT[128 * c:128 * (c + 1), :]))
        t_w2 = QG.total_tokens()
        for c in range(FC):
            QG.dma(ld(b1["w3"][:, c * D:(c + 1) * D], wqT[128 * c:128 * (c + 1), :]))
        t_w3 = QG.total_tokens()

        vones = V_sb.rearrange("p (j h c) -> p j h c", j=KC, c=65)[:, :, :, 64:65]
        E("vector", lambda e: e.memset(vones, 1.0), inc="dve")
        t_vones = tok("dve")
        vxs = [V_sb[:, (KC - 4 + i) * VW:(KC - 4 + i) * VW + D] for i in range(4)]
        vones_tail = vones[:, KC - 4:KC, :, :]

        def ln_stats(xst, buf, col, pre=(), dummy=None):
            dmy = buf["dummy"][:] if dummy is None else dummy
            E("scalar", lambda e: e.activation(dmy, xst, AF.Copy,
                                               accum_out=buf["lsum"][:, col:col + 1]),
              waits=list(pre) + [tok("act")], inc="act")
            E("scalar", lambda e: e.activation(dmy, xst, AF.Square,
                                               accum_out=buf["lvar"][:, col:col + 1]),
              waits=[tok("act")], inc="act")
            return tok("act")

        def ln_chain(buf, c0, c1, pre=()):
            sl = slice(c0, c1)
            E("vector", lambda e: e.tensor_single_scalar(
                buf["lsum"][:, sl], buf["lsum"][:, sl], 1.0 / D, ALU.mult),
              waits=list(pre) + [tok("dve")], inc="dve")
            E("vector", lambda e: e.tensor_mul(
                buf["lmusq"][:, sl], buf["lsum"][:, sl], buf["lsum"][:, sl]),
              waits=[tok("dve")], inc="dve")
            E("vector", lambda e: e.tensor_scalar(
                buf["lvar"][:, sl], buf["lvar"][:, sl], 1.0 / D, LN_EPS,
                ALU.mult, ALU.add),
              waits=[tok("dve")], inc="dve")
            E("vector", lambda e: e.tensor_sub(
                buf["lvar"][:, sl], buf["lvar"][:, sl], buf["lmusq"][:, sl]),
              waits=[tok("dve")], inc="dve")
            t_var = tok("dve")
            E("scalar", lambda e: e.activation(buf["lrstd"][:, sl], buf["lvar"][:, sl],
                                               AF.Ln),
              waits=[t_var, tok("act")], inc="act")
            E("scalar", lambda e: e.activation(buf["lrstd"][:, sl], buf["lrstd"][:, sl],
                                               AF.Exp, scale=-0.5),
              waits=[tok("act")], inc="act")
            t_rstd = tok("act")
            E("vector", lambda e: e.scalar_tensor_tensor(
                buf["lnmr"][:, sl], buf["lsum"][:, sl], -1.0, buf["lrstd"][:, sl],
                ALU.mult, ALU.mult),
              waits=[t_rstd, tok("dve")], inc="dve")
            return tok("dve"), t_rstd

        def ln_apply(xst, xhat, buf, col, pre=()):
            E("scalar", lambda e: e.activation(
                xhat, xst, AF.Identity, bias=buf["lnmr"][:, col:col + 1],
                scale=buf["lrstd"][:, col:col + 1]),
              waits=list(pre) + [tok("act")], inc="act")
            return tok("act")

        xst_free = [t_vones] * 4
        xhat_free = [NONE_TOK, NONE_TOK, NONE_TOK, NONE_TOK]
        trbank_free = [NONE_TOK, NONE_TOK]
        kqbank_free = [NONE_TOK, NONE_TOK]
        vbank_free = [NONE_TOK, NONE_TOK]
        xTg_free = [NONE_TOK]
        ln_reader = [("act", 0), NONE_TOK]
        qst_free = [[], []]
        qqs_pe_free = [("pe", 0), ("pe", 0), ("pe", 0)]
        nrm_scr_free = NONE_TOK
        pending_norm = []
        qqs_flip = [0]
        bank_norm_free = [NONE_TOK, NONE_TOK]

        def emit_norm(src_sl, col, first, t_src, par):
            qf = qqs_flip[0]
            qqs_flip[0] = (qf + 1) % 3
            E("scalar", lambda e, src_sl=src_sl, qf=qf: e.activation(
                b1[f"qqs_{qf}"][:], src_sl, AF.Square),
              waits=[t_src, qqs_pe_free[qf], tok("act")], inc="act")
            t_sq = tok("act")
            pending_norm.append((col, first, t_sq, par, qf))

        def flush_norm(force=False):
            nonlocal nrm_scr_free
            while pending_norm and (force or len(pending_norm) >= 3):
                col, first, t_sq, par, qf = pending_norm.pop(0)
                nbank = bank(par)
                E("tensor", lambda e, nbank=nbank, qf=qf: e.matmul(
                    nbank[0:1, :], cones128.ap()[:], b1[f"qqs_{qf}"][:],
                    start=True, stop=True),
                  waits=[t_sq, bank_norm_free[par], trbank_free[par]], inc="pe")
                t_mm = tok("pe")
                qqs_pe_free[qf] = t_mm
                E("vector", lambda e, nbank=nbank: e.reduce_max(
                    cnrm.ap()[0:1, 30:31], nbank[0:1, :], axis=AX.X),
                  waits=[t_mm, nrm_scr_free], inc="dve")
                if first:
                    E("vector", lambda e, col=col: e.tensor_copy(
                        cnrm.ap()[0:1, col:col + 1], cnrm.ap()[0:1, 30:31]),
                      waits=[tok("dve")], inc="dve")
                else:
                    E("vector", lambda e, col=col: e.tensor_max(
                        cnrm.ap()[0:1, col:col + 1], cnrm.ap()[0:1, col:col + 1],
                        cnrm.ap()[0:1, 30:31]),
                      waits=[tok("dve")], inc="dve")
                nrm_scr_free = tok("dve")
                bank_norm_free[par] = tok("dve")

        def p1_front(g):
            nonlocal ln_reader
            applies = []
            t_stats = []
            for sub in range(4):
                i = 4 * g + sub
                xst = vxs[sub]
                t_x = QG.dma(ld(xst, x_kv[128 * i:128 * (i + 1), :]),
                             waits=[xst_free[sub]])
                t_stats.append(ln_stats(
                    xst, b1, sub, pre=[t_x, ln_reader[1], xhat_free[sub]],
                    dummy=b1[f"xhat_{sub}"][:]))
            t_ch_dve, t_ch_act = ln_chain(b1, 0, 4,
                                          pre=[t_stats[-1], ln_reader[0]])
            ln_reader = [tok("act"), t_ch_dve]
            for sub in range(4):
                xst = vxs[sub]
                xhat = b1[f"xhat_{sub}"][:]
                t_apply = ln_apply(xst, xhat, b1, sub,
                                   pre=[t_ch_dve, xhat_free[sub]])
                ln_reader[0] = t_apply
                xst_free[sub] = t_apply
                applies.append(t_apply)
            return applies

        def p1_tr(g, applies):
            tr_evac = []
            for sub in range(4):
                xhat = b1[f"xhat_{sub}"][:]
                pb = sub % 2
                tr_out = bankb(pb)
                for c in range(FC):
                    w = ([applies[sub], trbank_free[pb], bank_norm_free[pb]]
                         + t_consts) if c == 0 else []
                    E("tensor", lambda e, c=c, tr_out=tr_out, xhat=xhat: e.transpose(
                        tr_out[:, 128 * c:128 * (c + 1)],
                        xhat[:, 128 * c:128 * (c + 1)], cident.ap()[:]),
                      waits=w, inc="pe" if c == FC - 1 else None)
                t_tr = tok("pe")
                xhat_free[sub] = t_tr
                xTg_v = b1["xTg"].rearrange("p (c t) -> p c t", c=FC)[:, :, 128 * sub:128 * (sub + 1)]
                tr_v = tr_out[:, 0:FC * 128].rearrange("p (c t) -> p c t", c=FC)
                E("vector", lambda e, xTg_v=xTg_v, tr_v=tr_v: e.tensor_copy(xTg_v, tr_v),
                  waits=[t_tr] + (xTg_free if sub == 0 else []), inc="dve")
                tr_evac.append(tok("dve"))
                trbank_free[pb] = tok("dve")
            return tr_evac[-1]

        def p1_proj(g, t_xTg):
            nonlocal xTg_free
            for jp in range(FC):
                pb = jp % 2
                kps = bank(2 + pb)
                for c in range(FC):
                    w = ([t_xTg, kqbank_free[pb], bank_norm_free[pb]] + t_w1) if c == 0 else []
                    E("tensor", lambda e, c=c, jp=jp, kps=kps: e.matmul(
                        kps,
                        b1["w1"][:, c * D + 128 * jp: c * D + 128 * (jp + 1)],
                        b1["xTg"][:, 512 * c:512 * (c + 1)],
                        start=(c == 0), stop=(c == FC - 1)),
                      waits=w, inc="pe" if c == FC - 1 else None)
                t_kmm = tok("pe")
                dst = K_sb[:, jp * S_kv + 512 * g: jp * S_kv + 512 * (g + 1)]
                E("vector", lambda e, dst=dst, kps=kps, jp=jp: e.tensor_scalar(
                    dst, kps, cbk2.ap()[:, jp:jp + 1], None, ALU.add),
                  waits=[t_kmm] + t_consts, inc="dve")
                kqbank_free[pb] = tok("dve")
                emit_norm(dst, jp, g == 0, tok("dve"), pb)
                flush_norm()
            for sub in range(4):
                j = 4 * g + sub
                pb = sub % 2
                vps = bank(4 + 2 * pb, 2)[:, 0:D]
                for jh in range(JH):
                    w = []
                    if jh == 0:
                        w = [t_xTg, vbank_free[pb]]
                        if g == 0 and sub == 0:
                            w += t_crows + t_w2
                    E("tensor", lambda e, vps=vps, jh=jh: e.matmul(
                        vps[:, 512 * jh:512 * (jh + 1)],
                        cones1x128.ap()[0:1, :], bv_row[:, 512 * jh:512 * (jh + 1)],
                        start=True, stop=False),
                      waits=w, inc=None)
                    for c in range(FC):
                        E("tensor", lambda e, c=c, sub=sub, jh=jh, vps=vps: e.matmul(
                            vps[:, 512 * jh:512 * (jh + 1)],
                            b1["xTg"][:, 512 * c + 128 * sub: 512 * c + 128 * (sub + 1)],
                            b1["w2"][:, c * D + 512 * jh: c * D + 512 * (jh + 1)],
                            start=False, stop=(c == FC - 1)),
                          inc="pe" if (c == FC - 1 and jh == JH - 1) else None)
                t_vmm = tok("pe")
                dst = V_sb[:, j * VW:(j + 1) * VW].rearrange(
                    "p (h c) -> p h c", c=65)[:, :, 0:64]
                vsrc = vps.rearrange("p (h c) -> p h c", c=64)
                E("vector", lambda e, dst=dst, vsrc=vsrc: e.tensor_copy(dst, vsrc),
                  waits=[t_vmm], inc="dve")
                vbank_free[pb] = tok("dve")
            if g < cfg.NGQ:
                for jp in range(FC):
                    pb = jp % 2
                    qps = bank(2 + pb)
                    for c in range(FC):
                        w = ([kqbank_free[pb], bank_norm_free[pb], t_xTg] + t_w3) if c == 0 else []
                        E("tensor", lambda e, c=c, jp=jp, qps=qps: e.matmul(
                            qps,
                            b1["w3"][:, c * D + 128 * jp: c * D + 128 * (jp + 1)],
                            b1["xTg"][:, 512 * c:512 * (c + 1)],
                            start=(c == 0), stop=(c == FC - 1)),
                          waits=w, inc="pe" if c == FC - 1 else None)
                    t_qmm = tok("pe")
                    qb = jp % 2
                    qdst = b1[f"qst_{qb}"][:]
                    E("vector", lambda e, qdst=qdst, qps=qps, jp=jp: e.tensor_scalar(
                        qdst, qps, cqbias.ap()[:, jp:jp + 1], None, ALU.add),
                      waits=[t_qmm] + qst_free[qb], inc="dve")
                    kqbank_free[pb] = tok("dve")
                    t_qev = tok("dve")
                    t_qs = QS.dma(lambda e, qdst=qdst, jp=jp, g=g: e.dma_start(
                        out=q_dram[jp, :, 512 * g:512 * (g + 1)], in_=qdst),
                        waits=[t_qev])
                    emit_norm(qdst, 8 + jp, g == 0, t_qev, pb)
                    flush_norm()
                    qst_free[qb] = [t_qs, tok("act")]
            if g == cfg.NG - 1:
                flush_norm(force=True)
                flush_norm(force=True)
            xTg_free = [tok("pe")]

        ap_cur = p1_front(0)
        for g in range(cfg.NG):
            t_xTg = p1_tr(g, ap_cur)
            if g + 1 < cfg.NG:
                ap_cur = p1_front(g + 1)
            p1_proj(g, t_xTg)
        flush_norm()
        E("vector", lambda e: e.memset(vones_tail, 1.0),
          waits=list(ap_cur) + [tok("dve")], inc="dve")
        t_P1_pe = tok("pe")

        T.barrier()

        nrm = cnrm.ap()
        t_negmh = []
        for p in range(NP):
            E("vector", lambda e, p=p: e.tensor_mul(
                nrm[0:1, 16 + p:17 + p], nrm[0:1, p:p + 1], nrm[0:1, 8 + p:9 + p]),
              waits=[tok("dve")], inc="dve")
            E("scalar", lambda e, p=p: e.activation(
                nrm[0:1, 24:25], nrm[0:1, 16 + p:17 + p], AF.Ln),
              waits=[tok("dve"), tok("act")], inc="act")
            E("scalar", lambda e: e.activation(
                nrm[0:1, 24:25], nrm[0:1, 24:25], AF.Exp, scale=0.5),
              waits=[tok("act")], inc="act")
            E("vector", lambda e: e.tensor_scalar(
                nrm[0:1, 24:25], nrm[0:1, 24:25], -0.125, 7.9, ALU.mult, ALU.add),
              waits=[tok("act")], inc="dve")
            t_val = tok("dve")
            E("tensor", lambda e: e.matmul(
                bank(6)[:, 0:1], cones1x128f.ap()[0:1, :], nrm[0:1, 24:25],
                start=True, stop=True),
              waits=[t_val], inc="pe")
            E("vector", lambda e, p=p: e.tensor_copy(b2["negmh"][:, p:p + 1],
                                                     bank(6)[:, 0:1]),
              waits=[tok("pe")], inc="dve")
            t_negmh.append(tok("dve"))

        T.barrier()


        stbank_free = [NONE_TOK, NONE_TOK]
        pt_free = [NONE_TOK, NONE_TOK, NONE_TOK]
        av_free = NONE_TOK
        sums_free = NONE_TOK
        b6_free = tok("dve")


        def band_ops(qt, j):
            res = []
            e = j - 4 * qt
            if 0 <= e <= 3:
                lo = 128 * e - 2
                if e == 0:
                    res.append((0, 130, cband.ap()[:, 2:132], 0, 128))
                elif lo + 132 > 512:
                    res.append((lo, 512, cband.ap()[:, 0:512 - lo], 0, 128))
                else:
                    res.append((lo, lo + 132, cband.ap()[:, 0:132], 0, 128))
            if j == (4 * qt - 1) % KC:
                cap = ccorn.ap()[64:128, 4:6] if qt == 0 else ccorn.ap()[64:128, 0:2]
                res.append((0, 2, cap, 64, 128))
            if j == 4 * qt + 4:
                cap = ccorn.ap()[0:64, 6:8] if qt == QT - 1 else ccorn.ap()[0:64, 2:4]
                res.append((510, 512, cap, 0, 64))
            return res

        items = [(p, qt, j) for p in range(NP) for qt in range(QT) for j in range(KC)]
        av_free = [NONE_TOK, NONE_TOK, NONE_TOK, NONE_TOK]
        exp_toks = {}
        pending_tail = None
        NT2 = NP * QT
        q2_tok = [None] * NT2
        q2_qkdone = [("pe", 0), ("pe", 0)]

        def emit_qload(t2):
            p_, qt_ = t2 // QT, t2 % QT
            q2_tok[t2] = QS.dma(
                lambda e, p_=p_, qt_=qt_, t2=t2: e.dma_start(
                    out=b2[f"q2_{t2 % 2}"][:],
                    in_=q_dram[p_, :, 512 * qt_:512 * (qt_ + 1)]),
                waits=[q2_qkdone[t2 % 2]])

        emit_qload(0)
        emit_qload(1)

        def wq_steps(hidx):
            BLK = WH // 32
            B = wq
            par = hidx % 2
            src3 = wq[f"src_{par}"].rearrange("p (b k) -> p b k", k=32)
            dst = wq[f"dst_{par}"]
            dst3 = dst.rearrange("p (b k) -> p b k", k=32)
            rsb = B["rscale"].unsqueeze(2).to_broadcast([128, BLK, 32])
            ssb = B["scale"].unsqueeze(2).to_broadcast([128, BLK, 32])
            qy3 = B["qy"].rearrange("p (b k) -> p b k", k=32)
            qv3 = B["qv"].rearrange("p (b k) -> p b k", k=32)
            return [
                ("vector", lambda e: e.reduce_max(
                    B["amax"].unsqueeze(2), src3, axis=AX.X, apply_absolute_value=True)),
                ("vector", lambda e: e.tensor_scalar(
                    B["scale"][:], B["amax"][:], 1.0 / FP6_MAX, MX_EPS, ALU.mult, ALU.max)),
                ("vector", lambda e: e.reciprocal(B["rscale"][:], B["scale"][:])),
                ("pool", lambda e: e.tensor_tensor(qy3, src3, rsb, ALU.mult)),
                ("pool", lambda e: e.tensor_mul(B["qt"][:], B["qy"][:], B["qy"][:])),
                ("vector", lambda e: e.tensor_single_scalar(
                    B["qc"][:], B["qt"][:], 4.0, ALU.is_ge)),
                ("vector", lambda e: e.tensor_single_scalar(
                    B["qs"][:], B["qt"][:], 16.0, ALU.is_ge)),
                ("vector", lambda e: e.tensor_scalar(
                    B["qe"][:], B["qs"][:], -2.0, 8.0, ALU.mult, ALU.add)),
                ("vector", lambda e: e.scalar_tensor_tensor(
                    B["qr"][:], B["qc"][:], -4.0, B["qe"][:], ALU.mult, ALU.add)),
                ("vector", lambda e: e.tensor_scalar(
                    B["qe"][:], B["qs"][:], 0.25, 0.125, ALU.mult, ALU.add)),
                ("vector", lambda e: e.scalar_tensor_tensor(
                    B["qs"][:], B["qc"][:], 0.125, B["qe"][:], ALU.mult, ALU.add)),
                ("pool", lambda e: e.tensor_mul(B["qt"][:], B["qy"][:], B["qr"][:])),
                ("vector", lambda e: e.tensor_single_scalar(
                    B["qt"][:], B["qt"][:], -0.5, ALU.add)),
                ("vector", lambda e: e.tensor_copy(B["qvi"][:], B["qt"][:])),
                ("vector", lambda e: e.tensor_copy(B["qv"][:], B["qvi"][:])),
                ("vector", lambda e: e.tensor_tensor(B["qc"][:], B["qt"][:], B["qv"][:],
                                                     ALU.is_gt)),
                ("vector", lambda e: e.tensor_add(B["qv"][:], B["qv"][:], B["qc"][:])),
                ("pool", lambda e: e.tensor_mul(B["qv"][:], B["qv"][:], B["qs"][:])),
                ("vector", lambda e: e.tensor_scalar(
                    B["qv"][:], B["qv"][:], FP6_MAX, -FP6_MAX, ALU.min, ALU.max)),
                ("pool", lambda e: e.tensor_tensor(dst3, qv3, ssb, ALU.mult)),
            ]

        wq_queue = []
        NWH = FC * (D // WH)
        wq_state = {"tok": None, "h": 0, "dst_free": [None, None], "st": []}

        def wq_pump():
            if not wq_queue:
                if wq_state["h"] >= NWH:
                    return
                h = wq_state["h"]
                wq_state["h"] += 1
                jt, hh = h // (D // WH), h % (D // WH)
                par = h % 2
                w = [wq_state["tok"]] if wq_state["tok"] else []
                if wq_state["dst_free"][par] is not None:
                    w.append(wq_state["dst_free"][par])
                t_l = QS.dma(ld(wq[f"src_{par}"][:],
                                wffn[128 * jt:128 * (jt + 1), WH * hh:WH * (hh + 1)]),
                             waits=w)
                wq_state["tok"] = t_l
                wq_queue.extend(wq_steps(h))
                wq_queue.append(("dma", h))
                return
            item = wq_queue.pop(0)
            if item[0] == "dma":
                h = item[1]
                jt, hh = h // (D // WH), h % (D // WH)
                par = h % 2
                t_s = QS.dma(lambda e, jt=jt, hh=hh, par=par: e.dma_start(
                    out=wd_dram[jt, :, WH * hh:WH * (hh + 1)],
                    in_=wq[f"dst_{par}"][:]),
                    waits=[wq_state["tok"]])
                wq_state["dst_free"][par] = t_s
                wq_state["st"].append(t_s)
                return
            eng, fn = item
            sem = {"scalar": "act", "pool": "pool"}.get(eng, "dve")
            E(eng, fn, waits=[wq_state["tok"], tok(sem)], inc=sem)
            wq_state["tok"] = tok(sem)

        def emit_qk_band_exp(n, p, qt, j):
            u = n % 2
            stv = bank(2 * u, 2)
            t2 = p * QT + qt
            qsl = b2[f"q2_{t2 % 2}"]
            w0 = [stbank_free[u]]
            if j == 0:
                w0 = w0 + [q2_tok[t2]]
            E("tensor", lambda e, j=j, p=p, stv=stv, qsl=qsl: e.matmul(
                stv[:, 0:512],
                K_sb[0:64, p * S_kv + 128 * j: p * S_kv + 128 * (j + 1)],
                qsl[0:64, :],
                start=True, stop=True, tile_position=(0, 0)),
              waits=w0, inc=None)
            E("tensor", lambda e, j=j, p=p, stv=stv, qsl=qsl: e.matmul(
                stv[:, 512:1024],
                K_sb[64:128, p * S_kv + 128 * j: p * S_kv + 128 * (j + 1)],
                qsl[64:128, :],
                start=True, stop=True, tile_position=(64, 0)),
              inc="pe")
            t_qk = tok("pe")
            if j == KC - 1:
                q2_qkdone[t2 % 2] = t_qk
                if t2 + 2 < NT2:
                    emit_qload(t2 + 2)
            pt = b2[f"pt_{n % 3}"]
            w = [t_qk, pt_free[n % 3]] + ([t_negmh[p]] if n == 0 else [])
            E("scalar", lambda e, stv=stv, pt=pt, p=p: e.activation(
                pt[:], stv, AF.Exp, bias=b2["negmh"][:, p:p + 1], scale=0.125),
              waits=w, inc="act")
            t_exp = tok("act")
            stbank_free[u] = t_exp
            for (c0, c1, cap, r0, r1) in band_ops(qt, j):
                W = c1 - c0
                pt3 = pt.rearrange("p (h w) -> p h w", h=2)[r0:r1, :, c0:c1]
                cap3 = cap.unsqueeze(1).to_broadcast([r1 - r0, 2, W])
                E("pool", lambda e, pt3=pt3, cap3=cap3: e.tensor_tensor(
                    pt3, pt3, cap3, ALU.mult),
                  waits=[t_exp, tok("pool")], inc="pool")
                t_exp = tok("pool")
            exp_toks[n] = t_exp

        def emit_av(n, p, qt, j):
            pt = b2[f"pt_{n % 3}"]
            tp2 = (p * QT + qt) % 2
            ava, avb = bank(4 + 2 * tp2), bank(5 + 2 * tp2)
            w = [exp_toks.pop(n)]
            if j == 0:
                w += [av_free[2 * tp2], av_free[2 * tp2 + 1], t_vones]
            E("tensor", lambda e, p=p, j=j, pt=pt, ava=ava: e.matmul(
                ava[0:65, :],
                V_sb[:, j * VW + 65 * (2 * p): j * VW + 65 * (2 * p) + 65],
                pt[:, 0:512],
                start=(j == 0), stop=(j == KC - 1)),
              waits=w, inc=None)
            E("tensor", lambda e, p=p, j=j, pt=pt, avb=avb: e.matmul(
                avb[0:65, :],
                V_sb[:, j * VW + 65 * (2 * p + 1): j * VW + 65 * (2 * p + 2)],
                pt[:, 512:1024],
                start=(j == 0), stop=(j == KC - 1)),
              inc="pe")
            pt_free[n % 3] = tok("pe")

        lsb_free = NONE_TOK
        rlb_free = NONE_TOK
        stageb_free = NONE_TOK

        def emit_tail1(p, qt):
            nonlocal lsb_free
            t_avmm = tok("pe")
            tp2 = (p * QT + qt) % 2
            ava, avb = bank(4 + 2 * tp2), bank(5 + 2 * tp2)
            E("vector", lambda e, ava=ava: e.tensor_copy(
                b2["lsb"][64:65, 0:512], ava[64:65, :]),
              waits=[t_avmm, lsb_free, rlb_free], inc="dve")
            E("vector", lambda e, avb=avb: e.tensor_copy(
                b2["lsb"][64:65, 512:1024], avb[64:65, :]),
              waits=[tok("dve")], inc="dve")
            E("vector", lambda e: e.reciprocal(b2["lsb"][64:65, :],
                                               b2["lsb"][64:65, :]),
              waits=[tok("dve")], inc="dve")
            t_rl = tok("dve")
            lsb_free = t_rl
            return t_avmm, t_rl

        def emit_tail1b(t_rl):
            t_st = QS.dma(lambda e: e.dma_start(out=rl_dram, in_=b2["lsb"][64:65, :]),
                          waits=[t_rl])
            t_bc = QS.dma(lambda e: e.dma_start(
                out=b2["rlb"][:], in_=rl_dram.to_broadcast([128, 1024])),
                waits=[t_st])
            return t_bc

        def emit_tail2(p, qt, t_avmm_tile, t_bc):
            nonlocal rlb_free, stageb_free
            tp2 = (p * QT + qt) % 2
            ava, avb = bank(4 + 2 * tp2), bank(5 + 2 * tp2)
            ao_a = b2["attnout"][0:64, (p * QT + qt) * 512: (p * QT + qt + 1) * 512]
            ao_b = b2["attnout"][64:128, (p * QT + qt) * 512: (p * QT + qt + 1) * 512]
            E("vector", lambda e, ao_a=ao_a, ava=ava: e.tensor_tensor(
                ao_a, ava[0:64, :], b2["rlb"][0:64, 0:512], ALU.mult),
              waits=[t_bc, t_avmm_tile], inc="dve")
            av_free[2 * tp2] = tok("dve")
            E("vector", lambda e, avb=avb: e.tensor_tensor(
                b2["stageb"][0:64, :], avb[0:64, :], b2["rlb"][0:64, 512:1024],
                ALU.mult),
              waits=[stageb_free], inc="dve")
            t_avn_b = tok("dve")
            av_free[2 * tp2 + 1] = t_avn_b
            rlb_free = t_avn_b
            stageb_free = QS.dma(lambda e, ao_b=ao_b: e.dma_start(
                out=ao_b, in_=b2["stageb"][0:64, :]),
                waits=[t_avn_b])
            return stageb_free

        LAG = 3
        pending_1b = None

        def run_pending(n):
            nonlocal pending_1b, pending_tail
            if pending_1b is not None and n >= pending_1b[0]:
                _, tp, tqt, t_avm, t_rl = pending_1b
                t_bc = emit_tail1b(t_rl)
                pending_tail = (pending_1b[0] + 3, tp, tqt, t_avm, t_bc)
                pending_1b = None
            if pending_tail is not None and n >= pending_tail[0]:
                _, tp, tqt, t_avm, t_bc = pending_tail
                emit_tail2(tp, tqt, t_avm, t_bc)
                pending_tail = None

        for n, (p, qt, j) in enumerate(items):
            if n % 3 == 1:
                wq_pump()
            run_pending(n)
            if n >= LAG:
                pp, pq, pj = items[n - LAG]
                emit_av(n - LAG, pp, pq, pj)
                if pj == KC - 1:
                    t_avm, t_rl = emit_tail1(pp, pq)
                    pending_1b = (n + 3, pp, pq, t_avm, t_rl)
            emit_qk_band_exp(n, p, qt, j)
        for m in range(len(items) - LAG, len(items)):
            run_pending(m + len(items))
            pp, pq, pj = items[m]
            emit_av(m, pp, pq, pj)
            if pj == KC - 1:
                t_avm, t_rl = emit_tail1(pp, pq)
                pending_1b = (0, pp, pq, t_avm, t_rl)
        run_pending(10 ** 9)
        run_pending(10 ** 9)

        T.barrier()

        for c in range(FC):
            QG.dma(ld(p3["woT"][:, c * D:(c + 1) * D], woT[128 * c:128 * (c + 1), :]))
        t_wo = QG.total_tokens()

        def quant_steps(src_f32, dst_bf16, par, tab=None, blk=None):
            BLK = blk if blk is not None else D // 32
            B = tab if tab is not None else {
                k: p3[f"{k}_{par}"] for k in
                ("qy", "qt", "qv", "qc", "qr", "qs", "qe", "qvi",
                 "amax", "scale", "rscale")}
            src3 = src_f32.rearrange("p (b k) -> p b k", k=32)
            dst3 = dst_bf16.rearrange("p (b k) -> p b k", k=32)
            rsb = B["rscale"].unsqueeze(2).to_broadcast([128, BLK, 32])
            ssb = B["scale"].unsqueeze(2).to_broadcast([128, BLK, 32])
            qy3 = B["qy"].rearrange("p (b k) -> p b k", k=32)
            qv3 = B["qv"].rearrange("p (b k) -> p b k", k=32)
            tt = "vector"
            return [
                ("vector", lambda e: e.reduce_max(
                    B["amax"].unsqueeze(2), src3, axis=AX.X, apply_absolute_value=True)),
                ("vector", lambda e: e.tensor_scalar(
                    B["scale"][:], B["amax"][:], 1.0 / FP6_MAX, MX_EPS, ALU.mult, ALU.max)),
                ("vector", lambda e: e.reciprocal(B["rscale"][:], B["scale"][:])),
                (tt, lambda e: e.tensor_tensor(qy3, src3, rsb, ALU.mult)),
                ("scalar", lambda e: e.activation(B["qt"][:], B["qy"][:], AF.Square)),
                ("vector", lambda e: e.tensor_single_scalar(
                    B["qc"][:], B["qt"][:], 4.0, ALU.is_ge)),
                ("vector", lambda e: e.tensor_single_scalar(
                    B["qs"][:], B["qt"][:], 16.0, ALU.is_ge)),
                ("scalar", lambda e: e.activation(B["qe"][:], B["qs"][:], AF.Copy,
                                                  scale=-2.0, bias=8.0)),
                ("vector", lambda e: e.scalar_tensor_tensor(
                    B["qr"][:], B["qc"][:], -4.0, B["qe"][:], ALU.mult, ALU.add)),
                ("scalar", lambda e: e.activation(B["qe"][:], B["qs"][:], AF.Copy,
                                                  scale=0.25, bias=0.125)),
                ("vector", lambda e: e.scalar_tensor_tensor(
                    B["qs"][:], B["qc"][:], 0.125, B["qe"][:], ALU.mult, ALU.add)),
                ("vector", lambda e: e.tensor_mul(B["qt"][:], B["qy"][:], B["qr"][:])),
                ("scalar", lambda e: e.activation(B["qt"][:], B["qt"][:], AF.Copy,
                                                  scale=1.0, bias=-0.5)),
                ("vector", lambda e: e.tensor_copy(B["qvi"][:], B["qt"][:])),
                ("vector", lambda e: e.tensor_copy(B["qv"][:], B["qvi"][:])),
                ("vector", lambda e: e.tensor_tensor(B["qc"][:], B["qt"][:], B["qv"][:],
                                                     ALU.is_gt)),
                (tt, lambda e: e.tensor_add(B["qv"][:], B["qv"][:], B["qc"][:])),
                (tt, lambda e: e.tensor_mul(B["qv"][:], B["qv"][:], B["qs"][:])),
                ("vector", lambda e: e.tensor_scalar(
                    B["qv"][:], B["qv"][:], FP6_MAX, -FP6_MAX, ALU.min, ALU.max)),
                (tt, lambda e: e.tensor_tensor(dst3, qv3, ssb, ALU.mult)),
            ]

        qbuf_free = [NONE_TOK, NONE_TOK]

        def run_quant(tiles):
            chains = []
            for tt_ in tiles:
                src_, dst_, par, pre = tt_[:4]
                tab = tt_[4] if len(tt_) > 4 else None
                chains.append([list(pre) + ([qbuf_free[par]] if tab is None else []),
                               quant_steps(src_, dst_, par, tab)])
            nsteps = max(len(c[1]) for c in chains)
            finals = [None] * len(tiles)
            s3toks = [None] * len(tiles)
            for k in range(nsteps):
                for t, (wt, steps) in enumerate(chains):
                    if k >= len(steps):
                        continue
                    eng, fn = steps[k]
                    sem = {"scalar": "act", "pool": "pool"}.get(eng, "dve")
                    E(eng, fn, waits=wt + [tok(sem)], inc=sem)
                    chains[t][0] = [tok(sem)]
                    finals[t] = tok(sem)
                    if k == 3:
                        s3toks[t] = tok(sem)
            for (tt, tile) in enumerate(tiles):
                if len(tile) <= 4:
                    qbuf_free[tile[2]] = finals[tt]
            return finals, s3toks

        while wq_queue or wq_state["h"] < NWH:
            wq_pump()
        wf_free = [NONE_TOK, NONE_TOK]
        wd_trbank_free = [NONE_TOK, NONE_TOK]
        wdl = {}
        wdl_free = [NONE_TOK, NONE_TOK]
        for jt in range(FC):
            par = jt % 2
            wstg = wq[f"wstg_{par}"][:]
            t_ld = QS.dma(ld(wstg, wd_dram[jt]),
                          waits=wq_state["st"][-4:] + [wdl_free[par]])
            tr_out = bankb(4 + par)
            for c in range(FC):
                w = [t_ld, wd_trbank_free[par]] if c == 0 else []
                E("tensor", lambda e, c=c, tr_out=tr_out, wstg=wstg: e.transpose(
                    tr_out[:, 128 * c:128 * (c + 1)],
                    wstg[:, 128 * c:128 * (c + 1)],
                    cident.ap()[:]),
                  waits=w, inc="pe" if c == FC - 1 else None)
            t_tr = tok("pe")
            wdl_free[par] = t_tr
            wd_v = p3["wd"].rearrange("p (c j) -> p c j", c=FC)[:, :, 128 * jt:128 * (jt + 1)]
            tr_v = tr_out[:, 0:FC * 128].rearrange("p (c j) -> p c j", c=FC)
            E("vector", lambda e, wd_v=wd_v, tr_v=tr_v: e.tensor_copy(wd_v, tr_v),
              waits=[t_tr], inc="dve")
            wd_trbank_free[par] = tok("dve")
        t_wd = tok("dve")

        T.barrier()

        p3buf = {"dummy": b1["xstage_0"], "lsum": p3["l3sum"], "lvar": p3["l3var"],
                 "lrstd": p3["l3rstd"], "lnmr": p3["l3nmr"], "lmusq": p3["l3musq"]}
        x3_free = list(wf_free)
        ffnbank_free = NONE_TOK
        xdtr_free = [NONE_TOK, NONE_TOK]
        x1_free = [NONE_TOK] * 4
        xd_free = [NONE_TOK, NONE_TOK]
        xdT_free = [NONE_TOK, NONE_TOK]
        ln3_reader = [("act", 0), ("act", 0)]
        out_dma_par = [NONE_TOK, NONE_TOK]
        out_toks = []

        t_op_toks = {}
        t_x_toks = {}
        x1_toks = {}
        opbank_tok = {}
        tr_toks = {}
        xd_ready = {}
        apply_tok = {}

        def p3_stage1(i):
            ib = i % 2
            xst = p3[f"wfstage_{ib}"][:]
            t_x_toks[i] = QS.dma(ld(xst, x_kv[128 * i:128 * (i + 1), :]),
                                 waits=[x3_free[ib]])
            qt, s4 = i // 4, i % 4
            ops = bank(2 * ib, 2)[:, 0:D]
            for jh in range(JH):
                w = ([opbank_tok.pop(i - 2)] if (i - 2) in opbank_tok else []) \
                    if jh == 0 else []
                if i == 0 and jh == 0:
                    w += t_wo
                E("tensor", lambda e, ops=ops, jh=jh: e.matmul(
                    ops[:, 512 * jh:512 * (jh + 1)],
                    cones1x128.ap()[32:33, :], bo_row[:, 512 * jh:512 * (jh + 1)],
                    start=True, stop=False),
                  waits=w, inc=None)
                for c in range(FC):
                    ao = b2["attnout"][:, (c * QT + qt) * 512 + 128 * s4:
                                       (c * QT + qt) * 512 + 128 * (s4 + 1)]
                    E("tensor", lambda e, ao=ao, c=c, jh=jh, ops=ops: e.matmul(
                        ops[:, 512 * jh:512 * (jh + 1)],
                        ao, p3["woT"][:, c * D + 512 * jh: c * D + 512 * (jh + 1)],
                        start=False, stop=(c == FC - 1)),
                      inc="pe" if (c == FC - 1 and jh == JH - 1) else None)
            t_op_toks[i] = tok("pe")

        def p3_tr(i):
            ib = i % 2
            tr_out = bankb(4 + ib)
            t_xd = xd_ready.pop(i)
            for c in range(FC):
                w = [t_xd, xdtr_free[ib]] if c == 0 else []
                E("tensor", lambda e, c=c, tr_out=tr_out, ib=ib: e.transpose(
                    tr_out[:, 128 * c:128 * (c + 1)],
                    p3[f"xd_{ib}"][:, 128 * c:128 * (c + 1)], cident.ap()[:]),
                  waits=w, inc="pe" if c == FC - 1 else None)
            t_tr = tok("pe")
            xd_free[ib] = t_tr
            tr_toks[i] = t_tr

        def mk_quant_chains(i0):
            mm = quant_steps(p3m["src"], p3m["dst"], 0, tab=p3m,
                             blk=2 * (D // 32))
            head = {"st8": [apply_tok.pop(i0, ("act", 0)),
                            apply_tok.pop(i0 + 1, ("act", 0)),
                            qbuf_free[0], qbuf_free[1]],
                    "steps": [(e, f, [], "qy" if k_ == 3 else None)
                              for k_, (e, f) in enumerate(mm[0:4])],
                    "rec": {}}
            mids_ = []
            for par in range(2):
                i = i0 + par
                ib = i % 2
                ms = quant_steps(p3[f"xhat2_{ib}"][:], p3[f"xd_{ib}"][:], ib)
                steps = []
                for k_ in range(4, 13):
                    eng, fn = ms[k_]
                    extras = [out_dma_par[ib]] if k_ == 7 else []
                    if k_ == 4:
                        extras = extras + [(lambda: head["rec"]["qy"])]
                    key = "qelast" if k_ == 10 else ("mfin" if k_ == 12 else None)
                    steps.append((eng, fn, extras, key))
                mids_.append({"st8": [], "steps": [None] * 4 + steps,
                              "rec": {}})
            tail = {"st8": [], "rec": {}, "steps": [None] * 13}
            for k_ in range(13, 20):
                eng, fn = mm[k_]
                extras = []
                if k_ == 13:
                    extras = [(lambda: mids_[0]["rec"]["mfin"]),
                              (lambda: mids_[1]["rec"]["mfin"]),
                              xd_free[0], xd_free[1]]
                tail["steps"].append((eng, fn, extras,
                                      "final" if k_ == 19 else None))
            return head, mids_[0], mids_[1], tail

        def mk_copy_chain(pair_lo):
            steps = []
            for par in range(2):
                i = pair_lo + par
                ib = i % 2
                tr_out = bankb(4 + ib)
                xdt = p3[f"xdT_{ib}"][:]
                steps.append(("scalar",
                              lambda e, xdt=xdt, tr_out=tr_out: e.activation(
                                  xdt, tr_out[:, 0:FC * 128], AF.Copy),
                              [tr_toks.pop(i), xdT_free[ib]], ("xdt", i)))
            return {"st8": [], "steps": steps, "rec": {}}

        def mk_prep_chain(i0p, chA, chB):
            steps = [None] * 19
            for par, pad in ((0, 0), (1, 1)):
                i = i0p + par
                ib, i4 = i % 2, i % 4
                xst = p3[f"wfstage_{ib}"][:]
                ops = bank(2 * ib, 2)[:, 0:D]
                x1 = p3[f"x1_{i4}"][:]
                steps += [None] * pad
                steps.append(("vector",
                              lambda e, x1=x1, ops=ops, xst=xst: e.tensor_tensor(
                                  x1, ops, xst, ALU.add),
                              [t_op_toks.pop(i), t_x_toks.pop(i),
                               (lambda i4=i4: x1_free[i4])], ("x1", par)))
                x1v = p3[f"x1_{i4}"][:]
                steps.append(("scalar",
                              lambda e, x1v=x1v, ib=ib: e.activation(
                                  p3buf["dummy"][:], x1v, AF.Copy,
                                  accum_out=p3buf["lsum"][:, ib:ib + 1]),
                              [ln3_reader[ib]], None))
                steps.append(("scalar",
                              lambda e, x1v=x1v, ib=ib: e.activation(
                                  p3buf["dummy"][:], x1v, AF.Square,
                                  accum_out=p3buf["lvar"][:, ib:ib + 1]),
                              [], None))
            buf = p3buf
            sl = slice(0, 2)
            steps += [
                ("vector", lambda e: e.tensor_single_scalar(
                    buf["lsum"][:, sl], buf["lsum"][:, sl], 1.0 / D, ALU.mult),
                 [], None),
                ("vector", lambda e: e.tensor_mul(
                    buf["lmusq"][:, sl], buf["lsum"][:, sl], buf["lsum"][:, sl]),
                 [], None),
                ("vector", lambda e: e.tensor_scalar(
                    buf["lvar"][:, sl], buf["lvar"][:, sl], 1.0 / D, LN_EPS,
                    ALU.mult, ALU.add), [], None),
                ("vector", lambda e: e.tensor_sub(
                    buf["lvar"][:, sl], buf["lvar"][:, sl], buf["lmusq"][:, sl]),
                 [], None),
                ("scalar", lambda e: e.activation(
                    buf["lrstd"][:, sl], buf["lvar"][:, sl], AF.Ln), [], None),
                ("scalar", lambda e: e.activation(
                    buf["lrstd"][:, sl], buf["lrstd"][:, sl], AF.Exp,
                    scale=-0.5), [], None),
                ("vector", lambda e: e.scalar_tensor_tensor(
                    buf["lnmr"][:, sl], buf["lsum"][:, sl], -1.0,
                    buf["lrstd"][:, sl], ALU.mult, ALU.mult), [], None),
            ]
            for par in range(2):
                i = i0p + par
                ib, i4 = i % 2, i % 4
                ch = chA if par == 0 else chB
                steps.append(("scalar",
                              lambda e, i4=i4, ib=ib: e.activation(
                                  p3[f"xhat2_{ib}"][:], p3[f"x1_{i4}"][:],
                                  AF.Identity,
                                  bias=p3buf["lnmr"][:, ib:ib + 1],
                                  scale=p3buf["lrstd"][:, ib:ib + 1]),
                              [(lambda ch=ch: ch["rec"]["qy"])], ("apply", par)))
            return {"st8": [], "steps": steps, "rec": {}}

        def run_zip(chains, mids=()):
            midmap = dict(mids)
            nsteps = max(len(c["steps"]) for c in chains)
            for r in range(nsteps):
                if r in midmap:
                    midmap.pop(r)()
                for c in chains:
                    if r >= len(c["steps"]) or c["steps"][r] is None:
                        continue
                    eng, fn, extras, key = c["steps"][r]
                    sem = {"scalar": "act", "pool": "pool"}.get(eng, "dve")
                    ext = [x() if callable(x) else x for x in extras]
                    E(eng, fn, waits=c["st8"] + ext + [tok(sem)], inc=sem)
                    c["st8"] = [tok(sem)]
                    if key is not None:
                        c["rec"][key] = tok(sem)
            for fn in midmap.values():
                fn()

        def p3_ffn(i, chC, qch):
            nonlocal ffnbank_free
            ib = i % 2
            i4 = i % 4
            x1 = p3[f"x1_{i4}"][:]
            xdt = p3[f"xdT_{ib}"][:]
            fps = bank(6, 2)[:, 0:D]
            for jh in range(JH):
                w = ([ffnbank_free, chC["rec"][("xdt", i)]] if jh == 0 else [])
                if i == 0 and jh == 0:
                    w += [t_wd]
                E("tensor", lambda e, fps=fps, jh=jh: e.matmul(
                    fps[:, 512 * jh:512 * (jh + 1)],
                    cones1x128.ap()[64:65, :], fb_row[:, 512 * jh:512 * (jh + 1)],
                    start=True, stop=False),
                  waits=w, inc=None)
                for c in range(FC):
                    E("tensor", lambda e, c=c, jh=jh, fps=fps, xdt=xdt: e.matmul(
                        fps[:, 512 * jh:512 * (jh + 1)],
                        xdt[:, 128 * c:128 * (c + 1)],
                        p3["wd"][:, c * D + 512 * jh: c * D + 512 * (jh + 1)],
                        start=False, stop=(c == FC - 1)),
                      inc="pe" if (c == FC - 1 and jh == JH - 1) else None)
            t_ffn = tok("pe")
            outst = p3[f"outst_{ib}"][:]
            w = [t_ffn, x1_toks.pop(i)]
            if qch is not None:
                w.append(qch["rec"]["qelast"])
            if len(out_toks) >= 2:
                w.append(out_toks[-2])
            E("vector", lambda e, outst=outst, fps=fps, x1=x1: e.tensor_tensor(
                outst, fps, x1, ALU.add),
              waits=w, inc="dve")
            t_out = tok("dve")
            x1_free[i4] = t_out
            ffnbank_free = t_out
            xdT_free[ib] = t_ffn
            out_toks.append(QS.dma(lambda e, outst=outst, i=i: e.dma_start(
                out=out_d[128 * i:128 * (i + 1), :], in_=outst),
                waits=[t_out]))
            out_dma_par[ib] = out_toks[-1]

        def resolve_recs(i0, chM, chC, chD):
            for par in range(2):
                qbuf_free[par] = chM["rec"]["final"]
                xd_ready[i0 + par] = chM["rec"]["final"]
            if chC is not None:
                for par in range(2):
                    xdtr_free[(i0 - 2 + par) % 2] = chC["rec"][("xdt", i0 - 2 + par)]
            if chD is not None:
                for par in range(2):
                    i = i0 + 2 + par
                    t_x1 = chD["rec"][("x1", par)]
                    x1_toks[i] = t_x1
                    x3_free[i % 2] = t_x1
                    opbank_tok[i] = t_x1
                    t_ap = chD["rec"][("apply", par)]
                    ln3_reader[i % 2] = t_ap
                    apply_tok[i] = t_ap

        p3_stage1(0)
        p3_stage1(1)
        t_x1_0 = []
        for par in range(2):
            ib = par
            xst = p3[f"wfstage_{ib}"][:]
            ops = bank(2 * ib, 2)[:, 0:D]
            x1 = p3[f"x1_{par}"][:]
            E("vector", lambda e, x1=x1, ops=ops, xst=xst: e.tensor_tensor(
                x1, ops, xst, ALU.add),
              waits=[t_op_toks.pop(par), t_x_toks.pop(par)], inc="dve")
            x1_toks[par] = tok("dve")
            x3_free[ib] = tok("dve")
            opbank_tok[par] = tok("dve")
            t_x1_0.append(tok("dve"))
        t_sts0 = [ln_stats(p3[f"x1_{par}"][:], p3buf, par, pre=[t_x1_0[par]])
                  for par in range(2)]
        t_ch_dve0, _ = ln_chain(p3buf, 0, 2, pre=[t_sts0[-1]])
        for par in range(2):
            apply_tok[par] = ln_apply(p3[f"x1_{par}"][:], p3[f"xhat2_{par}"][:],
                                      p3buf, par, pre=[t_ch_dve0])
            ln3_reader[par] = apply_tok[par]

        NPAIR = cfg.NTQ // 2
        for kw in range(NPAIR):
            i0 = 2 * kw
            if kw >= 1:
                p3_tr(i0 - 2)
                p3_tr(i0 - 1)
            if kw + 1 < NPAIR:
                p3_stage1(i0 + 2)
                p3_stage1(i0 + 3)
            chH, chMA, chMB, chT = mk_quant_chains(i0)
            chC = mk_copy_chain(i0 - 2) if kw >= 1 else None
            chD = mk_prep_chain(i0 + 2, chH, chH) if kw + 1 < NPAIR else None
            chains = [c for c in (chH, chMA, chMB, chT, chC, chD)
                      if c is not None]
            mids = []
            if kw >= 1:
                mids = [(18, lambda i=i0 - 2, chC=chC, q=chMA: p3_ffn(i, chC, q)),
                        (20, lambda i=i0 - 1, chC=chC, q=chMB: p3_ffn(i, chC, q))]
            run_zip(chains, mids)
            resolve_recs(i0, chT, chC, chD)

        i0 = cfg.NTQ - 2
        p3_tr(i0)
        p3_tr(i0 + 1)
        chC = mk_copy_chain(i0)
        run_zip([chC])
        p3_ffn(i0, chC, None)
        p3_ffn(i0 + 1, chC, None)

        E("gpsimd", lambda e: e.engine_nop(),
          waits=QG.total_tokens() + QS.total_tokens(), inc=None)

        for name, filt in (("sync", ("sync",)), ("tensor", ("tensor",)),
                           ("scalar", ("scalar",)), ("vector", ("vector",)),
                           ("gpsimd", ("gpsimd", "pool"))):
            def body(eng, filt=filt):
                T.replay(filt, eng, sems)
            getattr(blk, name)(body)

    return nc



def _make_in_maps(cfg, x, Wq, Wk, Wv, Wo, bq, bk, bv, bo, ocr, Wf, fbias, n_cores):
    import ml_dtypes
    bft = ml_dtypes.bfloat16
    B, S, D = x.shape
    per_b = n_cores // B
    S_q = cfg.S_q
    FC = cfg.FC
    shared = {
        "wqT": np.ascontiguousarray(Wq.T), "wkT": np.ascontiguousarray(Wk.T),
        "wvT": np.ascontiguousarray(Wv.T), "woT": np.ascontiguousarray(Wo.T),
        "wffn": Wf,
        "bq2": np.ascontiguousarray(bq.reshape(FC, 128).T),
        "bk2": np.ascontiguousarray(bk.reshape(FC, 128).T),
        "ocr2": np.ascontiguousarray(ocr.reshape(FC, 128).T),
        "bv_row": bv.reshape(1, D).copy(), "bo_row": bo.reshape(1, D).copy(),
        "fb_row": fbias.reshape(1, D).copy(),
        "ident": np.eye(128, dtype=bft),
        "ones128": np.ones((128, 1), dtype=bft),
        "ones1x128": np.ones((65, 128), dtype=bft),
        "ones1x128f": np.ones((65, 128), dtype=np.float32),
        "band_main": band_main_np().astype(bft),
    }
    in_maps = []
    for c in range(n_cores):
        b, half = c // per_b, c % per_b
        q_off = half * S_q
        m = dict(shared)
        m["x_kv"] = np.ascontiguousarray(np.roll(x[b], -q_off, axis=0))
        corners = np.zeros((128, 8), dtype=np.float32)
        corners[:, 0:2] = corner_pred_np(True)
        corners[:, 2:4] = corner_succ_np(True)
        corners[:, 4:6] = corner_pred_np(q_off > 0)
        corners[:, 6:8] = corner_succ_np(q_off + S_q < S)
        m["corners"] = corners.astype(bft)
        in_maps.append(m)
    return in_maps


def kernel(L0_attn_x, L0_attn_ocr_context, L0_attn_Wq, L0_attn_bq, L0_attn_Wk,
           L0_attn_bk, L0_attn_Wv, L0_attn_bv, L0_attn_Wo, L0_attn_bo,
           L0_attn_ocr_bias, L0_ffn_weight, L0_ffn_bias):
    x = np.asarray(L0_attn_x, dtype=np.float32)
    B, S, D = x.shape
    n_cores = 8
    per_b = n_cores // B
    S_q = S // per_b
    cfg = Cfg(D, S_q, S)

    f32 = lambda a: np.ascontiguousarray(np.asarray(a, dtype=np.float32))
    in_maps = _make_in_maps(
        cfg, x, f32(L0_attn_Wq), f32(L0_attn_Wk), f32(L0_attn_Wv), f32(L0_attn_Wo),
        f32(L0_attn_bq), f32(L0_attn_bk), f32(L0_attn_bv), f32(L0_attn_bo),
        f32(L0_attn_ocr_bias), f32(L0_ffn_weight), f32(L0_ffn_bias), n_cores)

    nc = build_nc(cfg)
    res = run_bass_kernel_spmd(nc, in_maps, core_ids=list(range(n_cores)))
    out = np.empty((B, S, D), dtype=np.float32)
    for c in range(n_cores):
        b, half = c // per_b, c % per_b
        out[b, half * S_q:(half + 1) * S_q, :] = res.results[c]["out"]
    return out


# version 50
# speedup vs baseline: 1.0255x; 1.0109x over previous
import numpy as np
from contextlib import ExitStack

import concourse.bass as bass
import concourse.mybir as mybir
from concourse.bass_utils import run_bass_kernel_spmd

F32 = mybir.dt.float32
BF16 = mybir.dt.bfloat16
I32 = mybir.dt.int32
U8 = mybir.dt.uint8
AF = mybir.ActivationFunctionType
ALU = mybir.AluOpType
AX = mybir.AxisListType

HD = 64
LN_EPS = 1e-5
FP6_MAX = 7.5
MX_EPS = 1e-30
DTB = {F32: 4, BF16: 2, I32: 4}
NONE_TOK = ("dve", 0)


class Cfg:
    def __init__(self, D, S_q, S_kv):
        self.D, self.S_q, self.S_kv = D, S_q, S_kv
        self.H = D // HD
        self.NP = self.H // 2
        self.FC = D // 128
        self.KC = S_kv // 128
        self.QT = S_q // 512
        self.NG = S_kv // 512
        self.NGQ = S_q // 512
        self.NTQ = S_q // 128
        self.JH = D // 512
        assert self.NP == self.FC


class Trace:
    def __init__(self):
        self.ops = []
        self.cnt = {}
        self.pending = {}

    def tok(self, sem):
        return (sem, self.cnt.get(sem, 0))

    def barrier(self):
        toks = [self.tok(s) for s in
                ("pe", "act", "dve", "pool",
                 "dxa", "dxb", "dxc", "dxd", "dwa", "dwb", "dwc", "dwd",
                 "dva", "dvb")]
        toks = [t for t in toks if t[1] > 0]
        for e in ("sync", "tensor", "scalar", "vector", "gpsimd"):
            self.pending.setdefault(e, []).extend(toks)

    def emit(self, eng, fn, waits=(), inc=None, inc_by=1):
        ws = list(self.pending.pop(eng, []))
        for t in waits:
            if t is None or t[0] is None or t[1] <= 0:
                continue
            s, v = t
            assert v <= self.cnt.get(s, 0), f"wait on future {s}={v} (now {self.cnt.get(s, 0)})"
            ws.append((s, v))
        if inc is not None:
            self.cnt[inc] = self.cnt.get(inc, 0) + inc_by
        self.ops.append((eng, fn, ws, inc, inc_by))
        return self.tok(inc) if inc else None

    def replay(self, engine_names, eng, sems):
        seen = {}
        for (e, fn, ws, inc, inc_by) in self.ops:
            if e not in engine_names:
                continue
            for (s, v) in ws:
                if seen.get(s, 0) >= v:
                    continue
                eng.wait_ge(sems[s], v)
                seen[s] = v
            inst = fn(eng)
            if inc is not None:
                assert inst is not None, f"op on {e} incrementing {inc} returned None"
                inst.then_inc(sems[inc], inc_by)


class DmaQ:

    def __init__(self, trace, engine, lane_sems):
        self.T = trace
        self.engine = engine
        self.lanes = lane_sems
        self.uses = [0] * len(lane_sems)
        self.i = 0

    def dma(self, fn, waits=()):
        ln = self.i % len(self.lanes)
        self.i += 1
        sem = self.lanes[ln]
        issue_waits = list(waits) + [(sem, 16 * self.uses[ln])]
        self.uses[ln] += 1
        self.T.emit(self.engine, fn, waits=issue_waits, inc=sem, inc_by=16)
        return (sem, 16 * self.uses[ln])

    def total_tokens(self):
        return [(s, 16 * u) for s, u in zip(self.lanes, self.uses) if u > 0]


BAND_MULT = float(np.exp(np.float64(0.1)))


def band_main_np():
    b = np.ones((128, 132), dtype=np.float32)
    for p in range(128):
        b[p, p:p + 5] = BAND_MULT
    return b


def corner_pred_np(on):
    c = np.ones((128, 2), dtype=np.float32)
    if on:
        c[126, 0] = BAND_MULT
        c[127, 0] = BAND_MULT
        c[127, 1] = BAND_MULT
    return c


def corner_succ_np(on):
    c = np.ones((128, 2), dtype=np.float32)
    if on:
        c[0, 0] = BAND_MULT
        c[0, 1] = BAND_MULT
        c[1, 1] = BAND_MULT
    return c


def build_nc(cfg: Cfg):
    D, FC, KC, QT, NP, JH = cfg.D, cfg.FC, cfg.KC, cfg.QT, cfg.NP, cfg.JH
    S_q, S_kv = cfg.S_q, cfg.S_kv

    nc = bass.Bass("TRN2", target_bir_lowering=False)
    ap = lambda t: t.ap()

    x_kv = ap(nc.dram_tensor("x_kv", [S_kv, D], F32, kind="ExternalInput"))
    wqT = ap(nc.dram_tensor("wqT", [D, D], F32, kind="ExternalInput"))
    wkT = ap(nc.dram_tensor("wkT", [D, D], F32, kind="ExternalInput"))
    wvT = ap(nc.dram_tensor("wvT", [D, D], F32, kind="ExternalInput"))
    woT = ap(nc.dram_tensor("woT", [D, D], F32, kind="ExternalInput"))
    wffn = ap(nc.dram_tensor("wffn", [D, D], F32, kind="ExternalInput"))
    bq2_d = ap(nc.dram_tensor("bq2", [128, FC], F32, kind="ExternalInput"))
    bk2_d = ap(nc.dram_tensor("bk2", [128, FC], F32, kind="ExternalInput"))
    ocr2_d = ap(nc.dram_tensor("ocr2", [128, FC], F32, kind="ExternalInput"))
    bv_d = ap(nc.dram_tensor("bv_row", [1, D], F32, kind="ExternalInput"))
    bo_d = ap(nc.dram_tensor("bo_row", [1, D], F32, kind="ExternalInput"))
    fb_d = ap(nc.dram_tensor("fb_row", [1, D], F32, kind="ExternalInput"))
    ident_d = ap(nc.dram_tensor("ident", [128, 128], BF16, kind="ExternalInput"))
    ones128_d = ap(nc.dram_tensor("ones128", [128, 1], BF16, kind="ExternalInput"))
    ones1x128_d = ap(nc.dram_tensor("ones1x128", [65, 128], BF16, kind="ExternalInput"))
    ones1x128f_d = ap(nc.dram_tensor("ones1x128f", [65, 128], F32, kind="ExternalInput"))
    band_d = ap(nc.dram_tensor("band_main", [128, 132], BF16, kind="ExternalInput"))
    corn_d = ap(nc.dram_tensor("corners", [128, 8], BF16, kind="ExternalInput"))
    out_d = ap(nc.dram_tensor("out", [S_q, D], F32, kind="ExternalOutput"))
    q_dram = ap(nc.dram_tensor("q_scratch", [FC, 128, S_q], BF16))
    wd_dram = ap(nc.dram_tensor("wd_scratch", [FC, 128, D], BF16))
    rl_dram = ap(nc.dram_tensor("rl_scratch", [1, 1024], F32))

    T = Trace()
    E, tok = T.emit, T.tok

    with ExitStack() as st:
        def layout(specs):
            out, o = {}, 0
            for name, dt, n in specs:
                o = (o + 63) // 64 * 64
                out[name] = (o, dt, n)
                o += n * DTB[dt]
            return out, o

        p3spec, p3_bytes = layout(
            [(nm, F32, D) for nm in
             ("x1_0", "x1_1", "xhat2_0", "xhat2_1",
              "wfstage_0", "wfstage_1",
              "qy_0", "qt_0", "qv_0", "qc_0", "qr_0", "qs_0", "qe_0",
              "qy_1", "qt_1", "qv_1", "qc_1", "qr_1", "qs_1", "qe_1",
              "x1_2", "x1_3")]
            + [(nm, F32, 2) for nm in
               ("l3sum", "l3var", "l3rstd", "l3nmr", "l3musq")]
            + [(nm, F32, D // 32) for nm in
               ("amax_0", "scale_0", "rscale_0", "amax_1", "scale_1", "rscale_1")]
            + [(nm, BF16, D) for nm in ("xd_0", "xd_1", "xdT_0", "xdT_1")]
            + [("woT", BF16, FC * D), ("wd", BF16, FC * D)])
        VW = (D // HD) * 65
        A_BYTES = max(FC * S_kv * 2 + KC * VW * 2, p3_bytes)

        b1spec, b1_bytes = layout(
            [("w1", BF16, FC * D), ("w2", BF16, FC * D), ("w3", BF16, FC * D),
             ("xTg", BF16, FC * 512),
             ("xstage_0", BF16, D),
             ("xhat_0", BF16, D), ("xhat_1", BF16, D),
             ("xhat_2", BF16, D), ("xhat_3", BF16, D),
             ("qst_0", BF16, 512), ("qst_1", BF16, 512),
             ("qqs_0", BF16, 512), ("qqs_1", BF16, 512),
             ("qqs_2", BF16, 512), ("lstats", F32, 32)])

        b2spec, b2_bytes = layout(
            [("attnout", BF16, NP * QT * 512), ("pt_0", BF16, 1024),
             ("pt_1", BF16, 1024), ("pt_2", BF16, 1024),
             ("lsb", F32, 1024), ("stageb", BF16, 512),
             ("q2_0", BF16, 512), ("q2_1", BF16, 512),
             ("negmh", F32, NP), ("nrm", F32, 8)])
        WH = 512
        wq_bytes = ((7 * WH * 4 + 63) // 64 * 64 + WH * 4 + 3 * 64 +
                    2 * WH * 4 + 2 * WH * 2 + 16 * 64)
        B_BYTES = max(b1_bytes, b2_bytes,
                      (b2_bytes + 63) // 64 * 64 + wq_bytes)

        regA = st.enter_context(nc.sbuf_tensor("regA", [128, A_BYTES], U8))
        regB = st.enter_context(nc.sbuf_tensor("regB", [128, B_BYTES], U8))

        def mkview(reg, off, dt, n):
            return reg.ap()[:, off:off + n * DTB[dt]].bitcast(dt)

        K_sb = mkview(regA, 0, BF16, FC * S_kv)
        V_sb = mkview(regA, FC * S_kv * 2, BF16, KC * VW)
        p3 = {nm: mkview(regA, o, dt, n) for nm, (o, dt, n) in p3spec.items()}
        p3["outst_0"] = p3["qe_0"]
        p3["outst_1"] = p3["qe_1"]
        for _par in range(2):
            p3[f"qvi_{_par}"] = mkview(regA, p3spec[f"qr_{_par}"][0], I32, D)

        b1 = {nm: mkview(regB, o, dt, n) for nm, (o, dt, n) in b1spec.items()}
        _ls = b1spec["lstats"][0]
        for _i, _nm in enumerate(("lsum", "lvar", "lrstd", "lnmr", "lmusq")):
            b1[_nm] = mkview(regB, _ls + 16 * _i, F32, 4)
        _f0 = (b2_bytes + 63) // 64 * 64
        _o2 = [_f0]
        def wqv(dt, n):
            o_al = (_o2[0] + 63) // 64 * 64
            _o2[0] = o_al + n * DTB[dt]
            return mkview(regB, o_al, dt, n)
        wq = {}
        wq_off = {}
        for k_nm in ("qy", "qt", "qv", "qc", "qr", "qs", "qe"):
            wq_off[k_nm] = (_o2[0] + 63) // 64 * 64
            wq[k_nm] = wqv(F32, WH)
        wq["qvi"] = wqv(I32, WH)
        wq["amax"] = wqv(F32, WH // 32)
        wq["scale"] = wqv(F32, WH // 32)
        wq["rscale"] = wqv(F32, WH // 32)
        wq["src_0"] = wqv(F32, WH)
        wq["src_1"] = wqv(F32, WH)
        wq["dst_0"] = wqv(BF16, WH)
        wq["dst_1"] = wqv(BF16, WH)
        assert _o2[0] <= B_BYTES, (_o2[0], B_BYTES)
        wq["wstg_0"] = mkview(regB, wq_off["qy"], BF16, D)
        wq["wstg_1"] = mkview(regB, wq_off["qv"], BF16, D)
        b2 = {nm: mkview(regB, o, dt, n) for nm, (o, dt, n) in b2spec.items()}
        b2["qq"] = b2["pt_0"]
        b2["rlb"] = b2["lsb"]

        cband = st.enter_context(nc.sbuf_tensor("cband", [128, 132], BF16))
        ccorn = st.enter_context(nc.sbuf_tensor("ccorn", [128, 8], BF16))
        cident = st.enter_context(nc.sbuf_tensor("cident", [128, 128], BF16))
        cones128 = st.enter_context(nc.sbuf_tensor("cones128", [128, 1], BF16))
        cones1x128 = st.enter_context(nc.sbuf_tensor("cones1x128", [65, 128], BF16))
        cones1x128f = st.enter_context(nc.sbuf_tensor("cones1x128f", [65, 128], F32))
        cbq2 = st.enter_context(nc.sbuf_tensor("cbq2", [128, FC], F32))
        cbk2 = st.enter_context(nc.sbuf_tensor("cbk2", [128, FC], F32))
        cocr2 = st.enter_context(nc.sbuf_tensor("cocr2", [128, FC], F32))
        cqbias = st.enter_context(nc.sbuf_tensor("cqbias", [128, FC], F32))
        crows = st.enter_context(nc.sbuf_tensor("crows", [65, D], BF16))
        cnrm = st.enter_context(nc.sbuf_tensor("cnrm", [128, 32], F32))

        psum = st.enter_context(nc.psum_tensor("psA", [128, 4096], F32))
        psum_ap = psum.ap()
        psum_b = psum_ap.bitcast(BF16)

        def bank(k, w=1):
            return psum_ap[:, 512 * k:512 * (k + w)]

        def bankb(k, w=1):
            return psum_b[:, 1024 * k:1024 * (k + w)]

        sems = {}
        for s in ("pe", "act", "dve", "pool",
                  "dxa", "dxb", "dxc", "dxd", "dwa", "dwb", "dwc", "dwd",
                  "dva", "dvb"):
            sems[s] = st.enter_context(nc.semaphore(s))
        QS = DmaQ(T, "sync", ["dxa", "dxb", "dxc", "dxd"])
        QG = DmaQ(T, "gpsimd", ["dwa", "dwb", "dwc", "dwd"])
        QV = DmaQ(T, "vector", ["dva", "dvb"])

        blk = st.enter_context(nc.Block())

        def ld(dst, src):
            return lambda e: e.dma_start(out=dst, in_=src)

        t_consts = []
        for dst, src in [
            (cband.ap()[:], band_d), (ccorn.ap()[:], corn_d),
            (cident.ap()[:], ident_d), (cones128.ap()[:], ones128_d),
            (cones1x128.ap()[:], ones1x128_d), (cones1x128f.ap()[:], ones1x128f_d),
            (cbq2.ap()[:], bq2_d), (cbk2.ap()[:], bk2_d), (cocr2.ap()[:], ocr2_d),
        ]:
            t_consts.append(QS.dma(ld(dst, src)))

        E("vector", lambda e: e.tensor_add(cqbias.ap()[:], cbq2.ap()[:], cocr2.ap()[:]),
          waits=t_consts, inc="dve")

        vones = V_sb.rearrange("p (j h c) -> p j h c", j=KC, c=65)[:, :, :, 64:65]
        E("vector", lambda e: e.memset(vones, 1.0), inc="dve")
        t_vones = tok("dve")
        vxs = [V_sb[:, (KC - 4 + i) * VW:(KC - 4 + i) * VW + D] for i in range(4)]
        vones_tail = vones[:, KC - 4:KC, :, :]
        xpre0 = [QG.dma(ld(vxs[sub], x_kv[128 * sub:128 * (sub + 1), :]),
                        waits=[t_vones]) for sub in range(4)]

        t_crows = [QG.dma(ld(crows.ap()[0:1, :], bv_d)),
                   QG.dma(ld(crows.ap()[32:33, :], bo_d)),
                   QG.dma(ld(crows.ap()[64:65, :], fb_d))]
        bv_row = crows.ap()[0:1, :]
        bo_row = crows.ap()[32:33, :]
        fb_row = crows.ap()[64:65, :]

        for c in range(FC):
            QG.dma(ld(b1["w1"][:, c * D:(c + 1) * D], wkT[128 * c:128 * (c + 1), :]))
        t_w1 = QG.total_tokens()
        for c in range(FC):
            QG.dma(ld(b1["w2"][:, c * D:(c + 1) * D], wvT[128 * c:128 * (c + 1), :]))
        t_w2 = QG.total_tokens()
        for c in range(FC):
            QG.dma(ld(b1["w3"][:, c * D:(c + 1) * D], wqT[128 * c:128 * (c + 1), :]))
        t_w3 = QG.total_tokens()

        def ln_stats(xst, buf, col, pre=(), dummy=None):
            dmy = buf["dummy"][:] if dummy is None else dummy
            E("scalar", lambda e: e.activation(dmy, xst, AF.Copy,
                                               accum_out=buf["lsum"][:, col:col + 1]),
              waits=list(pre) + [tok("act")], inc="act")
            E("scalar", lambda e: e.activation(dmy, xst, AF.Square,
                                               accum_out=buf["lvar"][:, col:col + 1]),
              waits=[tok("act")], inc="act")
            return tok("act")

        def ln_chain(buf, c0, c1, pre=()):
            sl = slice(c0, c1)
            E("vector", lambda e: e.tensor_single_scalar(
                buf["lsum"][:, sl], buf["lsum"][:, sl], 1.0 / D, ALU.mult),
              waits=list(pre) + [tok("dve")], inc="dve")
            E("vector", lambda e: e.tensor_mul(
                buf["lmusq"][:, sl], buf["lsum"][:, sl], buf["lsum"][:, sl]),
              waits=[tok("dve")], inc="dve")
            E("vector", lambda e: e.tensor_scalar(
                buf["lvar"][:, sl], buf["lvar"][:, sl], 1.0 / D, LN_EPS,
                ALU.mult, ALU.add),
              waits=[tok("dve")], inc="dve")
            E("vector", lambda e: e.tensor_sub(
                buf["lvar"][:, sl], buf["lvar"][:, sl], buf["lmusq"][:, sl]),
              waits=[tok("dve")], inc="dve")
            t_var = tok("dve")
            E("scalar", lambda e: e.activation(buf["lrstd"][:, sl], buf["lvar"][:, sl],
                                               AF.Ln),
              waits=[t_var, tok("act")], inc="act")
            E("scalar", lambda e: e.activation(buf["lrstd"][:, sl], buf["lrstd"][:, sl],
                                               AF.Exp, scale=-0.5),
              waits=[tok("act")], inc="act")
            t_rstd = tok("act")
            E("vector", lambda e: e.scalar_tensor_tensor(
                buf["lnmr"][:, sl], buf["lsum"][:, sl], -1.0, buf["lrstd"][:, sl],
                ALU.mult, ALU.mult),
              waits=[t_rstd, tok("dve")], inc="dve")
            return tok("dve"), t_rstd

        def ln_apply(xst, xhat, buf, col, pre=()):
            E("scalar", lambda e: e.activation(
                xhat, xst, AF.Identity, bias=buf["lnmr"][:, col:col + 1],
                scale=buf["lrstd"][:, col:col + 1]),
              waits=list(pre) + [tok("act")], inc="act")
            return tok("act")

        xst_free = [t_vones] * 4
        xhat_free = [NONE_TOK, NONE_TOK, NONE_TOK, NONE_TOK]
        trbank_free = [NONE_TOK, NONE_TOK]
        kqbank_free = [NONE_TOK, NONE_TOK]
        vbank_free = [NONE_TOK, NONE_TOK]
        xTg_free = [NONE_TOK]
        ln_reader = [("act", 0), NONE_TOK]
        qst_free = [[], []]
        qqs_pe_free = [("pe", 0), ("pe", 0), ("pe", 0)]
        nrm_scr_free = NONE_TOK
        pending_norm = []
        qqs_flip = [0]
        bank_norm_free = [NONE_TOK, NONE_TOK]

        def emit_norm(src_sl, col, first, t_src, par):
            qf = qqs_flip[0]
            qqs_flip[0] = (qf + 1) % 3
            E("scalar", lambda e, src_sl=src_sl, qf=qf: e.activation(
                b1[f"qqs_{qf}"][:], src_sl, AF.Square),
              waits=[t_src, qqs_pe_free[qf], tok("act")], inc="act")
            t_sq = tok("act")
            pending_norm.append((col, first, t_sq, par, qf))

        def flush_norm(force=False):
            nonlocal nrm_scr_free
            while pending_norm and (force or len(pending_norm) >= 3):
                col, first, t_sq, par, qf = pending_norm.pop(0)
                nbank = bank(par)
                E("tensor", lambda e, nbank=nbank, qf=qf: e.matmul(
                    nbank[0:1, :], cones128.ap()[:], b1[f"qqs_{qf}"][:],
                    start=True, stop=True),
                  waits=[t_sq, bank_norm_free[par], trbank_free[par]], inc="pe")
                t_mm = tok("pe")
                qqs_pe_free[qf] = t_mm
                E("vector", lambda e, nbank=nbank: e.reduce_max(
                    cnrm.ap()[0:1, 30:31], nbank[0:1, :], axis=AX.X),
                  waits=[t_mm, nrm_scr_free], inc="dve")
                if first:
                    E("vector", lambda e, col=col: e.tensor_copy(
                        cnrm.ap()[0:1, col:col + 1], cnrm.ap()[0:1, 30:31]),
                      waits=[tok("dve")], inc="dve")
                else:
                    E("vector", lambda e, col=col: e.tensor_max(
                        cnrm.ap()[0:1, col:col + 1], cnrm.ap()[0:1, col:col + 1],
                        cnrm.ap()[0:1, 30:31]),
                      waits=[tok("dve")], inc="dve")
                nrm_scr_free = tok("dve")
                bank_norm_free[par] = tok("dve")

        def p1_front(g, pre_x=None):
            nonlocal ln_reader
            applies = []
            t_stats = []
            for sub in range(4):
                i = 4 * g + sub
                xst = vxs[sub]
                t_x = pre_x[sub] if pre_x is not None else QG.dma(
                    ld(xst, x_kv[128 * i:128 * (i + 1), :]),
                    waits=[xst_free[sub]])
                t_stats.append(ln_stats(
                    xst, b1, sub, pre=[t_x, ln_reader[1], xhat_free[sub]],
                    dummy=b1[f"xhat_{sub}"][:]))
            t_ch_dve, t_ch_act = ln_chain(b1, 0, 4,
                                          pre=[t_stats[-1], ln_reader[0]])
            ln_reader = [tok("act"), t_ch_dve]
            for sub in range(4):
                xst = vxs[sub]
                xhat = b1[f"xhat_{sub}"][:]
                t_apply = ln_apply(xst, xhat, b1, sub,
                                   pre=[t_ch_dve, xhat_free[sub]])
                ln_reader[0] = t_apply
                xst_free[sub] = t_apply
                applies.append(t_apply)
            return applies

        def p1_tr(g, applies):
            tr_evac = []
            for sub in range(4):
                xhat = b1[f"xhat_{sub}"][:]
                pb = sub % 2
                tr_out = bankb(pb)
                for c in range(FC):
                    w = ([applies[sub], trbank_free[pb], bank_norm_free[pb]]
                         + t_consts) if c == 0 else []
                    E("tensor", lambda e, c=c, tr_out=tr_out, xhat=xhat: e.transpose(
                        tr_out[:, 128 * c:128 * (c + 1)],
                        xhat[:, 128 * c:128 * (c + 1)], cident.ap()[:]),
                      waits=w, inc="pe" if c == FC - 1 else None)
                t_tr = tok("pe")
                xhat_free[sub] = t_tr
                xTg_v = b1["xTg"].rearrange("p (c t) -> p c t", c=FC)[:, :, 128 * sub:128 * (sub + 1)]
                tr_v = tr_out[:, 0:FC * 128].rearrange("p (c t) -> p c t", c=FC)
                E("vector", lambda e, xTg_v=xTg_v, tr_v=tr_v: e.tensor_copy(xTg_v, tr_v),
                  waits=[t_tr] + (xTg_free if sub == 0 else []), inc="dve")
                tr_evac.append(tok("dve"))
                trbank_free[pb] = tok("dve")
            return tr_evac[-1]

        def p1_proj(g, t_xTg):
            nonlocal xTg_free
            for jp in range(FC):
                pb = jp % 2
                kps = bank(2 + pb)
                for c in range(FC):
                    w = ([t_xTg, kqbank_free[pb], bank_norm_free[pb]] + t_w1) if c == 0 else []
                    E("tensor", lambda e, c=c, jp=jp, kps=kps: e.matmul(
                        kps,
                        b1["w1"][:, c * D + 128 * jp: c * D + 128 * (jp + 1)],
                        b1["xTg"][:, 512 * c:512 * (c + 1)],
                        start=(c == 0), stop=(c == FC - 1)),
                      waits=w, inc="pe" if c == FC - 1 else None)
                t_kmm = tok("pe")
                dst = K_sb[:, jp * S_kv + 512 * g: jp * S_kv + 512 * (g + 1)]
                E("vector", lambda e, dst=dst, kps=kps, jp=jp: e.tensor_scalar(
                    dst, kps, cbk2.ap()[:, jp:jp + 1], None, ALU.add),
                  waits=[t_kmm] + t_consts, inc="dve")
                kqbank_free[pb] = tok("dve")
                emit_norm(dst, jp, g == 0, tok("dve"), pb)
                flush_norm()
            for sub in range(4):
                j = 4 * g + sub
                pb = sub % 2
                vps = bank(4 + 2 * pb, 2)[:, 0:D]
                for jh in range(JH):
                    w = []
                    if jh == 0:
                        w = [t_xTg, vbank_free[pb]]
                        if g == 0 and sub == 0:
                            w += t_crows + t_w2
                    E("tensor", lambda e, vps=vps, jh=jh: e.matmul(
                        vps[:, 512 * jh:512 * (jh + 1)],
                        cones1x128.ap()[0:1, :], bv_row[:, 512 * jh:512 * (jh + 1)],
                        start=True, stop=False),
                      waits=w, inc=None)
                    for c in range(FC):
                        E("tensor", lambda e, c=c, sub=sub, jh=jh, vps=vps: e.matmul(
                            vps[:, 512 * jh:512 * (jh + 1)],
                            b1["xTg"][:, 512 * c + 128 * sub: 512 * c + 128 * (sub + 1)],
                            b1["w2"][:, c * D + 512 * jh: c * D + 512 * (jh + 1)],
                            start=False, stop=(c == FC - 1)),
                          inc="pe" if (c == FC - 1 and jh == JH - 1) else None)
                t_vmm = tok("pe")
                dst = V_sb[:, j * VW:(j + 1) * VW].rearrange(
                    "p (h c) -> p h c", c=65)[:, :, 0:64]
                vsrc = vps.rearrange("p (h c) -> p h c", c=64)
                E("vector", lambda e, dst=dst, vsrc=vsrc: e.tensor_copy(dst, vsrc),
                  waits=[t_vmm], inc="dve")
                vbank_free[pb] = tok("dve")
            if g < cfg.NGQ:
                for jp in range(FC):
                    pb = jp % 2
                    qps = bank(2 + pb)
                    for c in range(FC):
                        w = ([kqbank_free[pb], bank_norm_free[pb], t_xTg] + t_w3) if c == 0 else []
                        E("tensor", lambda e, c=c, jp=jp, qps=qps: e.matmul(
                            qps,
                            b1["w3"][:, c * D + 128 * jp: c * D + 128 * (jp + 1)],
                            b1["xTg"][:, 512 * c:512 * (c + 1)],
                            start=(c == 0), stop=(c == FC - 1)),
                          waits=w, inc="pe" if c == FC - 1 else None)
                    t_qmm = tok("pe")
                    qb = jp % 2
                    qdst = b1[f"qst_{qb}"][:]
                    E("vector", lambda e, qdst=qdst, qps=qps, jp=jp: e.tensor_scalar(
                        qdst, qps, cqbias.ap()[:, jp:jp + 1], None, ALU.add),
                      waits=[t_qmm] + qst_free[qb], inc="dve")
                    kqbank_free[pb] = tok("dve")
                    t_qev = tok("dve")
                    t_qs = QS.dma(lambda e, qdst=qdst, jp=jp, g=g: e.dma_start(
                        out=q_dram[jp, :, 512 * g:512 * (g + 1)], in_=qdst),
                        waits=[t_qev])
                    emit_norm(qdst, 8 + jp, g == 0, t_qev, pb)
                    flush_norm()
                    qst_free[qb] = [t_qs, tok("act")]
            if g == cfg.NG - 1:
                flush_norm(force=True)
                flush_norm(force=True)
            xTg_free = [tok("pe")]

        ap_cur = p1_front(0)
        for g in range(cfg.NG):
            t_xTg = p1_tr(g, ap_cur)
            if g + 1 < cfg.NG:
                ap_cur = p1_front(g + 1)
            p1_proj(g, t_xTg)
        flush_norm()
        E("vector", lambda e: e.memset(vones_tail, 1.0),
          waits=list(ap_cur) + [tok("dve")], inc="dve")
        t_P1_pe = tok("pe")

        T.barrier()

        nrm = cnrm.ap()
        t_negmh = []
        for p in range(NP):
            E("vector", lambda e, p=p: e.tensor_mul(
                nrm[0:1, 16 + p:17 + p], nrm[0:1, p:p + 1], nrm[0:1, 8 + p:9 + p]),
              waits=[tok("dve")], inc="dve")
            E("scalar", lambda e, p=p: e.activation(
                nrm[0:1, 24:25], nrm[0:1, 16 + p:17 + p], AF.Ln),
              waits=[tok("dve"), tok("act")], inc="act")
            E("scalar", lambda e: e.activation(
                nrm[0:1, 24:25], nrm[0:1, 24:25], AF.Exp, scale=0.5),
              waits=[tok("act")], inc="act")
            E("vector", lambda e: e.tensor_scalar(
                nrm[0:1, 24:25], nrm[0:1, 24:25], -0.125, 7.9, ALU.mult, ALU.add),
              waits=[tok("act")], inc="dve")
            t_val = tok("dve")
            E("tensor", lambda e: e.matmul(
                bank(6)[:, 0:1], cones1x128f.ap()[0:1, :], nrm[0:1, 24:25],
                start=True, stop=True),
              waits=[t_val], inc="pe")
            E("vector", lambda e, p=p: e.tensor_copy(b2["negmh"][:, p:p + 1],
                                                     bank(6)[:, 0:1]),
              waits=[tok("pe")], inc="dve")
            t_negmh.append(tok("dve"))

        T.barrier()


        stbank_free = [NONE_TOK, NONE_TOK]
        pt_free = [NONE_TOK, NONE_TOK, NONE_TOK]
        av_free = NONE_TOK
        sums_free = NONE_TOK
        b6_free = tok("dve")


        def band_ops(qt, j):
            res = []
            e = j - 4 * qt
            if 0 <= e <= 3:
                lo = 128 * e - 2
                if e == 0:
                    res.append((0, 130, cband.ap()[:, 2:132], 0, 128))
                elif lo + 132 > 512:
                    res.append((lo, 512, cband.ap()[:, 0:512 - lo], 0, 128))
                else:
                    res.append((lo, lo + 132, cband.ap()[:, 0:132], 0, 128))
            if j == (4 * qt - 1) % KC:
                cap = ccorn.ap()[64:128, 4:6] if qt == 0 else ccorn.ap()[64:128, 0:2]
                res.append((0, 2, cap, 64, 128))
            if j == 4 * qt + 4:
                cap = ccorn.ap()[0:64, 6:8] if qt == QT - 1 else ccorn.ap()[0:64, 2:4]
                res.append((510, 512, cap, 0, 64))
            return res

        items = [(p, qt, j) for p in range(NP) for qt in range(QT) for j in range(KC)]
        av_free = [NONE_TOK, NONE_TOK, NONE_TOK, NONE_TOK]
        exp_toks = {}
        pending_tail = None
        NT2 = NP * QT
        q2_tok = [None] * NT2
        q2_qkdone = [("pe", 0), ("pe", 0)]

        def emit_qload(t2):
            p_, qt_ = t2 // QT, t2 % QT
            q2_tok[t2] = QS.dma(
                lambda e, p_=p_, qt_=qt_, t2=t2: e.dma_start(
                    out=b2[f"q2_{t2 % 2}"][:],
                    in_=q_dram[p_, :, 512 * qt_:512 * (qt_ + 1)]),
                waits=[q2_qkdone[t2 % 2]])

        emit_qload(0)
        emit_qload(1)

        def wq_steps(hidx):
            BLK = WH // 32
            B = wq
            par = hidx % 2
            src3 = wq[f"src_{par}"].rearrange("p (b k) -> p b k", k=32)
            dst = wq[f"dst_{par}"]
            dst3 = dst.rearrange("p (b k) -> p b k", k=32)
            rsb = B["rscale"].unsqueeze(2).to_broadcast([128, BLK, 32])
            ssb = B["scale"].unsqueeze(2).to_broadcast([128, BLK, 32])
            qy3 = B["qy"].rearrange("p (b k) -> p b k", k=32)
            qv3 = B["qv"].rearrange("p (b k) -> p b k", k=32)
            return [
                ("vector", lambda e: e.reduce_max(
                    B["amax"].unsqueeze(2), src3, axis=AX.X, apply_absolute_value=True)),
                ("vector", lambda e: e.tensor_scalar(
                    B["scale"][:], B["amax"][:], 1.0 / FP6_MAX, MX_EPS, ALU.mult, ALU.max)),
                ("vector", lambda e: e.reciprocal(B["rscale"][:], B["scale"][:])),
                ("pool", lambda e: e.tensor_tensor(qy3, src3, rsb, ALU.mult)),
                ("pool", lambda e: e.tensor_mul(B["qt"][:], B["qy"][:], B["qy"][:])),
                ("vector", lambda e: e.tensor_single_scalar(
                    B["qc"][:], B["qt"][:], 4.0, ALU.is_ge)),
                ("vector", lambda e: e.tensor_single_scalar(
                    B["qs"][:], B["qt"][:], 16.0, ALU.is_ge)),
                ("vector", lambda e: e.tensor_scalar(
                    B["qe"][:], B["qs"][:], -2.0, 8.0, ALU.mult, ALU.add)),
                ("vector", lambda e: e.scalar_tensor_tensor(
                    B["qr"][:], B["qc"][:], -4.0, B["qe"][:], ALU.mult, ALU.add)),
                ("vector", lambda e: e.tensor_scalar(
                    B["qe"][:], B["qs"][:], 0.25, 0.125, ALU.mult, ALU.add)),
                ("vector", lambda e: e.scalar_tensor_tensor(
                    B["qs"][:], B["qc"][:], 0.125, B["qe"][:], ALU.mult, ALU.add)),
                ("pool", lambda e: e.tensor_mul(B["qt"][:], B["qy"][:], B["qr"][:])),
                ("vector", lambda e: e.tensor_single_scalar(
                    B["qt"][:], B["qt"][:], -0.5, ALU.add)),
                ("vector", lambda e: e.tensor_copy(B["qvi"][:], B["qt"][:])),
                ("vector", lambda e: e.tensor_copy(B["qv"][:], B["qvi"][:])),
                ("vector", lambda e: e.tensor_tensor(B["qc"][:], B["qt"][:], B["qv"][:],
                                                     ALU.is_gt)),
                ("vector", lambda e: e.tensor_add(B["qv"][:], B["qv"][:], B["qc"][:])),
                ("pool", lambda e: e.tensor_mul(B["qv"][:], B["qv"][:], B["qs"][:])),
                ("vector", lambda e: e.tensor_scalar(
                    B["qv"][:], B["qv"][:], FP6_MAX, -FP6_MAX, ALU.min, ALU.max)),
                ("pool", lambda e: e.tensor_tensor(dst3, qv3, ssb, ALU.mult)),
            ]

        wq_queue = []
        NWH = FC * (D // WH)
        wq_state = {"tok": None, "h": 0, "dst_free": [None, None], "st": []}

        def wq_pump():
            if not wq_queue:
                if wq_state["h"] >= NWH:
                    return
                h = wq_state["h"]
                wq_state["h"] += 1
                jt, hh = h // (D // WH), h % (D // WH)
                par = h % 2
                w = [wq_state["tok"]] if wq_state["tok"] else []
                if wq_state["dst_free"][par] is not None:
                    w.append(wq_state["dst_free"][par])
                t_l = QS.dma(ld(wq[f"src_{par}"][:],
                                wffn[128 * jt:128 * (jt + 1), WH * hh:WH * (hh + 1)]),
                             waits=w)
                wq_state["tok"] = t_l
                wq_queue.extend(wq_steps(h))
                wq_queue.append(("dma", h))
                return
            item = wq_queue.pop(0)
            if item[0] == "dma":
                h = item[1]
                jt, hh = h // (D // WH), h % (D // WH)
                par = h % 2
                t_s = QS.dma(lambda e, jt=jt, hh=hh, par=par: e.dma_start(
                    out=wd_dram[jt, :, WH * hh:WH * (hh + 1)],
                    in_=wq[f"dst_{par}"][:]),
                    waits=[wq_state["tok"]])
                wq_state["dst_free"][par] = t_s
                wq_state["st"].append(t_s)
                return
            eng, fn = item
            sem = {"scalar": "act", "pool": "pool"}.get(eng, "dve")
            E(eng, fn, waits=[wq_state["tok"], tok(sem)], inc=sem)
            wq_state["tok"] = tok(sem)

        def emit_qk_band_exp(n, p, qt, j):
            u = n % 2
            stv = bank(2 * u, 2)
            t2 = p * QT + qt
            qsl = b2[f"q2_{t2 % 2}"]
            w0 = [stbank_free[u]]
            if j == 0:
                w0 = w0 + [q2_tok[t2]]
            E("tensor", lambda e, j=j, p=p, stv=stv, qsl=qsl: e.matmul(
                stv[:, 0:512],
                K_sb[0:64, p * S_kv + 128 * j: p * S_kv + 128 * (j + 1)],
                qsl[0:64, :],
                start=True, stop=True, tile_position=(0, 0)),
              waits=w0, inc=None)
            E("tensor", lambda e, j=j, p=p, stv=stv, qsl=qsl: e.matmul(
                stv[:, 512:1024],
                K_sb[64:128, p * S_kv + 128 * j: p * S_kv + 128 * (j + 1)],
                qsl[64:128, :],
                start=True, stop=True, tile_position=(64, 0)),
              inc="pe")
            t_qk = tok("pe")
            if j == KC - 1:
                q2_qkdone[t2 % 2] = t_qk
                if t2 + 2 < NT2:
                    emit_qload(t2 + 2)
            pt = b2[f"pt_{n % 3}"]
            w = [t_qk, pt_free[n % 3]] + ([t_negmh[p]] if n == 0 else [])
            E("scalar", lambda e, stv=stv, pt=pt, p=p: e.activation(
                pt[:], stv, AF.Exp, bias=b2["negmh"][:, p:p + 1], scale=0.125),
              waits=w, inc="act")
            t_exp = tok("act")
            stbank_free[u] = t_exp
            for (c0, c1, cap, r0, r1) in band_ops(qt, j):
                W = c1 - c0
                pt3 = pt.rearrange("p (h w) -> p h w", h=2)[r0:r1, :, c0:c1]
                cap3 = cap.unsqueeze(1).to_broadcast([r1 - r0, 2, W])
                E("pool", lambda e, pt3=pt3, cap3=cap3: e.tensor_tensor(
                    pt3, pt3, cap3, ALU.mult),
                  waits=[t_exp, tok("pool")], inc="pool")
                t_exp = tok("pool")
            exp_toks[n] = t_exp

        def emit_av(n, p, qt, j):
            pt = b2[f"pt_{n % 3}"]
            tp2 = (p * QT + qt) % 2
            ava, avb = bank(4 + 2 * tp2), bank(5 + 2 * tp2)
            w = [exp_toks.pop(n)]
            if j == 0:
                w += [av_free[2 * tp2], av_free[2 * tp2 + 1], t_vones]
            E("tensor", lambda e, p=p, j=j, pt=pt, ava=ava: e.matmul(
                ava[0:65, :],
                V_sb[:, j * VW + 65 * (2 * p): j * VW + 65 * (2 * p) + 65],
                pt[:, 0:512],
                start=(j == 0), stop=(j == KC - 1)),
              waits=w, inc=None)
            E("tensor", lambda e, p=p, j=j, pt=pt, avb=avb: e.matmul(
                avb[0:65, :],
                V_sb[:, j * VW + 65 * (2 * p + 1): j * VW + 65 * (2 * p + 2)],
                pt[:, 512:1024],
                start=(j == 0), stop=(j == KC - 1)),
              inc="pe")
            pt_free[n % 3] = tok("pe")

        lsb_free = NONE_TOK
        rlb_free = NONE_TOK
        stageb_free = NONE_TOK

        def emit_tail1(p, qt):
            nonlocal lsb_free
            t_avmm = tok("pe")
            tp2 = (p * QT + qt) % 2
            ava, avb = bank(4 + 2 * tp2), bank(5 + 2 * tp2)
            E("vector", lambda e, ava=ava: e.tensor_copy(
                b2["lsb"][64:65, 0:512], ava[64:65, :]),
              waits=[t_avmm, lsb_free, rlb_free], inc="dve")
            E("vector", lambda e, avb=avb: e.tensor_copy(
                b2["lsb"][64:65, 512:1024], avb[64:65, :]),
              waits=[tok("dve")], inc="dve")
            E("vector", lambda e: e.reciprocal(b2["lsb"][64:65, :],
                                               b2["lsb"][64:65, :]),
              waits=[tok("dve")], inc="dve")
            t_rl = tok("dve")
            lsb_free = t_rl
            return t_avmm, t_rl

        def emit_tail1b(t_rl):
            t_st = QS.dma(lambda e: e.dma_start(out=rl_dram, in_=b2["lsb"][64:65, :]),
                          waits=[t_rl])
            t_bc = QS.dma(lambda e: e.dma_start(
                out=b2["rlb"][:], in_=rl_dram.to_broadcast([128, 1024])),
                waits=[t_st])
            return t_bc

        def emit_tail2(p, qt, t_avmm_tile, t_bc):
            nonlocal rlb_free, stageb_free
            tp2 = (p * QT + qt) % 2
            ava, avb = bank(4 + 2 * tp2), bank(5 + 2 * tp2)
            ao_a = b2["attnout"][0:64, (p * QT + qt) * 512: (p * QT + qt + 1) * 512]
            ao_b = b2["attnout"][64:128, (p * QT + qt) * 512: (p * QT + qt + 1) * 512]
            E("vector", lambda e, ao_a=ao_a, ava=ava: e.tensor_tensor(
                ao_a, ava[0:64, :], b2["rlb"][0:64, 0:512], ALU.mult),
              waits=[t_bc, t_avmm_tile], inc="dve")
            av_free[2 * tp2] = tok("dve")
            E("vector", lambda e, avb=avb: e.tensor_tensor(
                b2["stageb"][0:64, :], avb[0:64, :], b2["rlb"][0:64, 512:1024],
                ALU.mult),
              waits=[stageb_free], inc="dve")
            t_avn_b = tok("dve")
            av_free[2 * tp2 + 1] = t_avn_b
            rlb_free = t_avn_b
            stageb_free = QS.dma(lambda e, ao_b=ao_b: e.dma_start(
                out=ao_b, in_=b2["stageb"][0:64, :]),
                waits=[t_avn_b])
            return stageb_free

        LAG = 3
        pending_1b = None

        def run_pending(n):
            nonlocal pending_1b, pending_tail
            if pending_1b is not None and n >= pending_1b[0]:
                _, tp, tqt, t_avm, t_rl = pending_1b
                t_bc = emit_tail1b(t_rl)
                pending_tail = (pending_1b[0] + 3, tp, tqt, t_avm, t_bc)
                pending_1b = None
            if pending_tail is not None and n >= pending_tail[0]:
                _, tp, tqt, t_avm, t_bc = pending_tail
                emit_tail2(tp, tqt, t_avm, t_bc)
                pending_tail = None

        for n, (p, qt, j) in enumerate(items):
            if n % 3 == 1:
                wq_pump()
            run_pending(n)
            if n >= LAG:
                pp, pq, pj = items[n - LAG]
                emit_av(n - LAG, pp, pq, pj)
                if pj == KC - 1:
                    t_avm, t_rl = emit_tail1(pp, pq)
                    pending_1b = (n + 3, pp, pq, t_avm, t_rl)
            emit_qk_band_exp(n, p, qt, j)
        for m in range(len(items) - LAG, len(items)):
            run_pending(m + len(items))
            pp, pq, pj = items[m]
            emit_av(m, pp, pq, pj)
            if pj == KC - 1:
                t_avm, t_rl = emit_tail1(pp, pq)
                pending_1b = (0, pp, pq, t_avm, t_rl)
        run_pending(10 ** 9)
        run_pending(10 ** 9)

        T.barrier()

        for c in range(FC):
            QG.dma(ld(p3["woT"][:, c * D:(c + 1) * D], woT[128 * c:128 * (c + 1), :]))
        t_wo = QG.total_tokens()

        def quant_steps(src_f32, dst_bf16, par, tab=None):
            BLK = D // 32
            B = tab if tab is not None else {
                k: p3[f"{k}_{par}"] for k in
                ("qy", "qt", "qv", "qc", "qr", "qs", "qe", "qvi",
                 "amax", "scale", "rscale")}
            src3 = src_f32.rearrange("p (b k) -> p b k", k=32)
            dst3 = dst_bf16.rearrange("p (b k) -> p b k", k=32)
            rsb = B["rscale"].unsqueeze(2).to_broadcast([128, BLK, 32])
            ssb = B["scale"].unsqueeze(2).to_broadcast([128, BLK, 32])
            qy3 = B["qy"].rearrange("p (b k) -> p b k", k=32)
            qv3 = B["qv"].rearrange("p (b k) -> p b k", k=32)
            tt = "vector"
            return [
                ("vector", lambda e: e.reduce_max(
                    B["amax"].unsqueeze(2), src3, axis=AX.X, apply_absolute_value=True)),
                ("vector", lambda e: e.tensor_scalar(
                    B["scale"][:], B["amax"][:], 1.0 / FP6_MAX, MX_EPS, ALU.mult, ALU.max)),
                ("vector", lambda e: e.reciprocal(B["rscale"][:], B["scale"][:])),
                (tt, lambda e: e.tensor_tensor(qy3, src3, rsb, ALU.mult)),
                ("scalar", lambda e: e.activation(B["qt"][:], B["qy"][:], AF.Square)),
                ("vector", lambda e: e.tensor_single_scalar(
                    B["qc"][:], B["qt"][:], 4.0, ALU.is_ge)),
                ("vector", lambda e: e.tensor_single_scalar(
                    B["qs"][:], B["qt"][:], 16.0, ALU.is_ge)),
                ("scalar", lambda e: e.activation(B["qe"][:], B["qs"][:], AF.Copy,
                                                  scale=-2.0, bias=8.0)),
                ("vector", lambda e: e.scalar_tensor_tensor(
                    B["qr"][:], B["qc"][:], -4.0, B["qe"][:], ALU.mult, ALU.add)),
                ("scalar", lambda e: e.activation(B["qe"][:], B["qs"][:], AF.Copy,
                                                  scale=0.25, bias=0.125)),
                ("vector", lambda e: e.scalar_tensor_tensor(
                    B["qs"][:], B["qc"][:], 0.125, B["qe"][:], ALU.mult, ALU.add)),
                ("vector", lambda e: e.tensor_mul(B["qt"][:], B["qy"][:], B["qr"][:])),
                ("scalar", lambda e: e.activation(B["qt"][:], B["qt"][:], AF.Copy,
                                                  scale=1.0, bias=-0.5)),
                ("vector", lambda e: e.tensor_copy(B["qvi"][:], B["qt"][:])),
                ("vector", lambda e: e.tensor_copy(B["qv"][:], B["qvi"][:])),
                ("vector", lambda e: e.tensor_tensor(B["qc"][:], B["qt"][:], B["qv"][:],
                                                     ALU.is_gt)),
                (tt, lambda e: e.tensor_add(B["qv"][:], B["qv"][:], B["qc"][:])),
                (tt, lambda e: e.tensor_mul(B["qv"][:], B["qv"][:], B["qs"][:])),
                ("vector", lambda e: e.tensor_scalar(
                    B["qv"][:], B["qv"][:], FP6_MAX, -FP6_MAX, ALU.min, ALU.max)),
                (tt, lambda e: e.tensor_tensor(dst3, qv3, ssb, ALU.mult)),
            ]

        qbuf_free = [NONE_TOK, NONE_TOK]

        def run_quant(tiles):
            chains = []
            for tt_ in tiles:
                src_, dst_, par, pre = tt_[:4]
                tab = tt_[4] if len(tt_) > 4 else None
                chains.append([list(pre) + ([qbuf_free[par]] if tab is None else []),
                               quant_steps(src_, dst_, par, tab)])
            nsteps = max(len(c[1]) for c in chains)
            finals = [None] * len(tiles)
            s3toks = [None] * len(tiles)
            for k in range(nsteps):
                for t, (wt, steps) in enumerate(chains):
                    if k >= len(steps):
                        continue
                    eng, fn = steps[k]
                    sem = {"scalar": "act", "pool": "pool"}.get(eng, "dve")
                    E(eng, fn, waits=wt + [tok(sem)], inc=sem)
                    chains[t][0] = [tok(sem)]
                    finals[t] = tok(sem)
                    if k == 3:
                        s3toks[t] = tok(sem)
            for (tt, tile) in enumerate(tiles):
                if len(tile) <= 4:
                    qbuf_free[tile[2]] = finals[tt]
            return finals, s3toks

        while wq_queue or wq_state["h"] < NWH:
            wq_pump()
        wf_free = [NONE_TOK, NONE_TOK]
        wd_trbank_free = [NONE_TOK, NONE_TOK]
        wdl = {}
        wdl_free = [NONE_TOK, NONE_TOK]
        for jt in range(FC):
            par = jt % 2
            wstg = wq[f"wstg_{par}"][:]
            t_ld = QS.dma(ld(wstg, wd_dram[jt]),
                          waits=wq_state["st"][-4:] + [wdl_free[par]])
            tr_out = bankb(4 + par)
            for c in range(FC):
                w = [t_ld, wd_trbank_free[par]] if c == 0 else []
                E("tensor", lambda e, c=c, tr_out=tr_out, wstg=wstg: e.transpose(
                    tr_out[:, 128 * c:128 * (c + 1)],
                    wstg[:, 128 * c:128 * (c + 1)],
                    cident.ap()[:]),
                  waits=w, inc="pe" if c == FC - 1 else None)
            t_tr = tok("pe")
            wdl_free[par] = t_tr
            wd_v = p3["wd"].rearrange("p (c j) -> p c j", c=FC)[:, :, 128 * jt:128 * (jt + 1)]
            tr_v = tr_out[:, 0:FC * 128].rearrange("p (c j) -> p c j", c=FC)
            E("vector", lambda e, wd_v=wd_v, tr_v=tr_v: e.tensor_copy(wd_v, tr_v),
              waits=[t_tr], inc="dve")
            wd_trbank_free[par] = tok("dve")
        t_wd = tok("dve")

        T.barrier()

        p3buf = {"dummy": b1["xstage_0"], "lsum": p3["l3sum"], "lvar": p3["l3var"],
                 "lrstd": p3["l3rstd"], "lnmr": p3["l3nmr"], "lmusq": p3["l3musq"]}
        x3_free = list(wf_free)
        ffnbank_free = NONE_TOK
        xdtr_free = [NONE_TOK, NONE_TOK]
        x1_free = [NONE_TOK] * 4
        xd_free = [NONE_TOK, NONE_TOK]
        xdT_free = [NONE_TOK, NONE_TOK]
        ln3_reader = [("act", 0), ("act", 0)]
        out_dma_par = [NONE_TOK, NONE_TOK]
        out_toks = []

        t_op_toks = {}
        t_x_toks = {}
        x1_toks = {}
        opbank_tok = {}
        tr_toks = {}
        xd_ready = {}
        apply_tok = {}

        def p3_stage1(i):
            ib = i % 2
            xst = p3[f"wfstage_{ib}"][:]
            t_x_toks[i] = QS.dma(ld(xst, x_kv[128 * i:128 * (i + 1), :]),
                                 waits=[x3_free[ib]])
            qt, s4 = i // 4, i % 4
            ops = bank(2 * ib, 2)[:, 0:D]
            for jh in range(JH):
                w = ([opbank_tok.pop(i - 2)] if (i - 2) in opbank_tok else []) \
                    if jh == 0 else []
                if i == 0 and jh == 0:
                    w += t_wo
                E("tensor", lambda e, ops=ops, jh=jh: e.matmul(
                    ops[:, 512 * jh:512 * (jh + 1)],
                    cones1x128.ap()[32:33, :], bo_row[:, 512 * jh:512 * (jh + 1)],
                    start=True, stop=False),
                  waits=w, inc=None)
                for c in range(FC):
                    ao = b2["attnout"][:, (c * QT + qt) * 512 + 128 * s4:
                                       (c * QT + qt) * 512 + 128 * (s4 + 1)]
                    E("tensor", lambda e, ao=ao, c=c, jh=jh, ops=ops: e.matmul(
                        ops[:, 512 * jh:512 * (jh + 1)],
                        ao, p3["woT"][:, c * D + 512 * jh: c * D + 512 * (jh + 1)],
                        start=False, stop=(c == FC - 1)),
                      inc="pe" if (c == FC - 1 and jh == JH - 1) else None)
            t_op_toks[i] = tok("pe")

        def p3_tr(i):
            ib = i % 2
            tr_out = bankb(4 + ib)
            t_xd = xd_ready.pop(i)
            for c in range(FC):
                w = [t_xd, xdtr_free[ib]] if c == 0 else []
                E("tensor", lambda e, c=c, tr_out=tr_out, ib=ib: e.transpose(
                    tr_out[:, 128 * c:128 * (c + 1)],
                    p3[f"xd_{ib}"][:, 128 * c:128 * (c + 1)], cident.ap()[:]),
                  waits=w, inc="pe" if c == FC - 1 else None)
            t_tr = tok("pe")
            xd_free[ib] = t_tr
            tr_toks[i] = t_tr

        def mk_quant_chain(i):
            ib = i % 2
            steps = []
            for k_, (eng, fn) in enumerate(quant_steps(
                    p3[f"xhat2_{ib}"][:], p3[f"xd_{ib}"][:], ib)):
                extras = []
                key = None
                if k_ == 3:
                    key = "qy"
                elif k_ == 7:
                    extras = [out_dma_par[ib]]
                elif k_ == 10:
                    key = "qelast"
                elif k_ == 19:
                    key = "final"
                steps.append((eng, fn, extras, key))
            return {"st8": [apply_tok.pop(i, ("act", 0)), xd_free[ib],
                            qbuf_free[ib]],
                    "steps": steps, "rec": {}}

        def mk_copy_chain(pair_lo):
            steps = []
            for par in range(2):
                i = pair_lo + par
                ib = i % 2
                tr_out = bankb(4 + ib)
                xdt = p3[f"xdT_{ib}"][:]
                steps.append(("scalar",
                              lambda e, xdt=xdt, tr_out=tr_out: e.activation(
                                  xdt, tr_out[:, 0:FC * 128], AF.Copy),
                              [tr_toks.pop(i), xdT_free[ib]], ("xdt", i)))
            return {"st8": [], "steps": steps, "rec": {}}

        def mk_prep_chain(i0p, chA, chB):
            steps = [None] * 12
            for par, pad in ((0, 0), (1, 2)):
                i = i0p + par
                ib, i4 = i % 2, i % 4
                xst = p3[f"wfstage_{ib}"][:]
                ops = bank(2 * ib, 2)[:, 0:D]
                x1 = p3[f"x1_{i4}"][:]
                steps += [None] * pad
                steps.append(("vector",
                              lambda e, x1=x1, ops=ops, xst=xst: e.tensor_tensor(
                                  x1, ops, xst, ALU.add),
                              [t_op_toks.pop(i), t_x_toks.pop(i),
                               (lambda i4=i4: x1_free[i4])], ("x1", par)))
                x1v = p3[f"x1_{i4}"][:]
                steps.append(("scalar",
                              lambda e, x1v=x1v, ib=ib: e.activation(
                                  p3buf["dummy"][:], x1v, AF.Copy,
                                  accum_out=p3buf["lsum"][:, ib:ib + 1]),
                              [ln3_reader[ib]], None))
                steps.append(("scalar",
                              lambda e, x1v=x1v, ib=ib: e.activation(
                                  p3buf["dummy"][:], x1v, AF.Square,
                                  accum_out=p3buf["lvar"][:, ib:ib + 1]),
                              [], None))
            buf = p3buf
            sl = slice(0, 2)
            steps += [
                ("vector", lambda e: e.tensor_single_scalar(
                    buf["lsum"][:, sl], buf["lsum"][:, sl], 1.0 / D, ALU.mult),
                 [], None),
                ("vector", lambda e: e.tensor_mul(
                    buf["lmusq"][:, sl], buf["lsum"][:, sl], buf["lsum"][:, sl]),
                 [], None),
                ("vector", lambda e: e.tensor_scalar(
                    buf["lvar"][:, sl], buf["lvar"][:, sl], 1.0 / D, LN_EPS,
                    ALU.mult, ALU.add), [], None),
                ("vector", lambda e: e.tensor_sub(
                    buf["lvar"][:, sl], buf["lvar"][:, sl], buf["lmusq"][:, sl]),
                 [], None),
                ("scalar", lambda e: e.activation(
                    buf["lrstd"][:, sl], buf["lvar"][:, sl], AF.Ln), [], None),
                ("scalar", lambda e: e.activation(
                    buf["lrstd"][:, sl], buf["lrstd"][:, sl], AF.Exp,
                    scale=-0.5), [], None),
                ("vector", lambda e: e.scalar_tensor_tensor(
                    buf["lnmr"][:, sl], buf["lsum"][:, sl], -1.0,
                    buf["lrstd"][:, sl], ALU.mult, ALU.mult), [], None),
            ]
            for par in range(2):
                i = i0p + par
                ib, i4 = i % 2, i % 4
                ch = chA if par == 0 else chB
                steps.append(("scalar",
                              lambda e, i4=i4, ib=ib: e.activation(
                                  p3[f"xhat2_{ib}"][:], p3[f"x1_{i4}"][:],
                                  AF.Identity,
                                  bias=p3buf["lnmr"][:, ib:ib + 1],
                                  scale=p3buf["lrstd"][:, ib:ib + 1]),
                              [(lambda ch=ch: ch["rec"]["qy"])], ("apply", par)))
            return {"st8": [], "steps": steps, "rec": {}}

        def run_zip(chains, mids=()):
            midmap = dict(mids)
            nsteps = max(len(c["steps"]) for c in chains)
            for r in range(nsteps):
                if r in midmap:
                    midmap.pop(r)()
                for c in chains:
                    if r >= len(c["steps"]) or c["steps"][r] is None:
                        continue
                    eng, fn, extras, key = c["steps"][r]
                    sem = {"scalar": "act", "pool": "pool"}.get(eng, "dve")
                    ext = [x() if callable(x) else x for x in extras]
                    E(eng, fn, waits=c["st8"] + ext + [tok(sem)], inc=sem)
                    c["st8"] = [tok(sem)]
                    if key is not None:
                        c["rec"][key] = tok(sem)
            for fn in midmap.values():
                fn()

        def p3_ffn(i, chC, qch):
            nonlocal ffnbank_free
            ib = i % 2
            i4 = i % 4
            x1 = p3[f"x1_{i4}"][:]
            xdt = p3[f"xdT_{ib}"][:]
            fps = bank(6, 2)[:, 0:D]
            for jh in range(JH):
                w = ([ffnbank_free, chC["rec"][("xdt", i)]] if jh == 0 else [])
                if i == 0 and jh == 0:
                    w += [t_wd]
                E("tensor", lambda e, fps=fps, jh=jh: e.matmul(
                    fps[:, 512 * jh:512 * (jh + 1)],
                    cones1x128.ap()[64:65, :], fb_row[:, 512 * jh:512 * (jh + 1)],
                    start=True, stop=False),
                  waits=w, inc=None)
                for c in range(FC):
                    E("tensor", lambda e, c=c, jh=jh, fps=fps, xdt=xdt: e.matmul(
                        fps[:, 512 * jh:512 * (jh + 1)],
                        xdt[:, 128 * c:128 * (c + 1)],
                        p3["wd"][:, c * D + 512 * jh: c * D + 512 * (jh + 1)],
                        start=False, stop=(c == FC - 1)),
                      inc="pe" if (c == FC - 1 and jh == JH - 1) else None)
            t_ffn = tok("pe")
            outst = p3[f"outst_{ib}"][:]
            w = [t_ffn, x1_toks.pop(i)]
            if qch is not None:
                w.append(qch["rec"]["qelast"])
            if len(out_toks) >= 2:
                w.append(out_toks[-2])
            E("vector", lambda e, outst=outst, fps=fps, x1=x1: e.tensor_tensor(
                outst, fps, x1, ALU.add),
              waits=w, inc="dve")
            t_out = tok("dve")
            x1_free[i4] = t_out
            ffnbank_free = t_out
            xdT_free[ib] = t_ffn
            out_toks.append(QS.dma(lambda e, outst=outst, i=i: e.dma_start(
                out=out_d[128 * i:128 * (i + 1), :], in_=outst),
                waits=[t_out]))
            out_dma_par[ib] = out_toks[-1]

        def resolve_recs(i0, chA, chB, chC, chD):
            for par, ch in ((0, chA), (1, chB)):
                qbuf_free[(i0 + par) % 2] = ch["rec"]["final"]
                xd_ready[i0 + par] = ch["rec"]["final"]
            if chC is not None:
                for par in range(2):
                    xdtr_free[(i0 - 2 + par) % 2] = chC["rec"][("xdt", i0 - 2 + par)]
            if chD is not None:
                for par in range(2):
                    i = i0 + 2 + par
                    t_x1 = chD["rec"][("x1", par)]
                    x1_toks[i] = t_x1
                    x3_free[i % 2] = t_x1
                    opbank_tok[i] = t_x1
                    t_ap = chD["rec"][("apply", par)]
                    ln3_reader[i % 2] = t_ap
                    apply_tok[i] = t_ap

        p3_stage1(0)
        p3_stage1(1)
        t_x1_0 = []
        for par in range(2):
            ib = par
            xst = p3[f"wfstage_{ib}"][:]
            ops = bank(2 * ib, 2)[:, 0:D]
            x1 = p3[f"x1_{par}"][:]
            E("vector", lambda e, x1=x1, ops=ops, xst=xst: e.tensor_tensor(
                x1, ops, xst, ALU.add),
              waits=[t_op_toks.pop(par), t_x_toks.pop(par)], inc="dve")
            x1_toks[par] = tok("dve")
            x3_free[ib] = tok("dve")
            opbank_tok[par] = tok("dve")
            t_x1_0.append(tok("dve"))
        t_sts0 = [ln_stats(p3[f"x1_{par}"][:], p3buf, par, pre=[t_x1_0[par]])
                  for par in range(2)]
        t_ch_dve0, _ = ln_chain(p3buf, 0, 2, pre=[t_sts0[-1]])
        for par in range(2):
            apply_tok[par] = ln_apply(p3[f"x1_{par}"][:], p3[f"xhat2_{par}"][:],
                                      p3buf, par, pre=[t_ch_dve0])
            ln3_reader[par] = apply_tok[par]

        NPAIR = cfg.NTQ // 2
        for kw in range(NPAIR):
            i0 = 2 * kw
            if kw >= 1:
                p3_tr(i0 - 2)
                p3_tr(i0 - 1)
            if kw + 1 < NPAIR:
                p3_stage1(i0 + 2)
                p3_stage1(i0 + 3)
            chA = mk_quant_chain(i0)
            chB = mk_quant_chain(i0 + 1)
            chC = mk_copy_chain(i0 - 2) if kw >= 1 else None
            chD = mk_prep_chain(i0 + 2, chA, chB) if kw + 1 < NPAIR else None
            chains = [c for c in (chA, chB, chC, chD) if c is not None]
            mids = []
            if kw >= 1:
                mids = [(12, lambda i=i0 - 2, chC=chC, chA=chA: p3_ffn(i, chC, chA)),
                        (16, lambda i=i0 - 1, chC=chC, chB=chB: p3_ffn(i, chC, chB))]
            run_zip(chains, mids)
            resolve_recs(i0, chA, chB, chC, chD)

        i0 = cfg.NTQ - 2
        p3_tr(i0)
        p3_tr(i0 + 1)
        chC = mk_copy_chain(i0)
        run_zip([chC])
        p3_ffn(i0, chC, None)
        p3_ffn(i0 + 1, chC, None)

        E("gpsimd", lambda e: e.engine_nop(),
          waits=QG.total_tokens() + QS.total_tokens(), inc=None)

        for name, filt in (("sync", ("sync",)), ("tensor", ("tensor",)),
                           ("scalar", ("scalar",)), ("vector", ("vector",)),
                           ("gpsimd", ("gpsimd", "pool"))):
            def body(eng, filt=filt):
                T.replay(filt, eng, sems)
            getattr(blk, name)(body)

    return nc



def _make_in_maps(cfg, x, Wq, Wk, Wv, Wo, bq, bk, bv, bo, ocr, Wf, fbias, n_cores):
    import ml_dtypes
    bft = ml_dtypes.bfloat16
    B, S, D = x.shape
    per_b = n_cores // B
    S_q = cfg.S_q
    FC = cfg.FC
    shared = {
        "wqT": np.ascontiguousarray(Wq.T), "wkT": np.ascontiguousarray(Wk.T),
        "wvT": np.ascontiguousarray(Wv.T), "woT": np.ascontiguousarray(Wo.T),
        "wffn": Wf,
        "bq2": np.ascontiguousarray(bq.reshape(FC, 128).T),
        "bk2": np.ascontiguousarray(bk.reshape(FC, 128).T),
        "ocr2": np.ascontiguousarray(ocr.reshape(FC, 128).T),
        "bv_row": bv.reshape(1, D).copy(), "bo_row": bo.reshape(1, D).copy(),
        "fb_row": fbias.reshape(1, D).copy(),
        "ident": np.eye(128, dtype=bft),
        "ones128": np.ones((128, 1), dtype=bft),
        "ones1x128": np.ones((65, 128), dtype=bft),
        "ones1x128f": np.ones((65, 128), dtype=np.float32),
        "band_main": band_main_np().astype(bft),
    }
    in_maps = []
    for c in range(n_cores):
        b, half = c // per_b, c % per_b
        q_off = half * S_q
        m = dict(shared)
        m["x_kv"] = np.ascontiguousarray(np.roll(x[b], -q_off, axis=0))
        corners = np.zeros((128, 8), dtype=np.float32)
        corners[:, 0:2] = corner_pred_np(True)
        corners[:, 2:4] = corner_succ_np(True)
        corners[:, 4:6] = corner_pred_np(q_off > 0)
        corners[:, 6:8] = corner_succ_np(q_off + S_q < S)
        m["corners"] = corners.astype(bft)
        in_maps.append(m)
    return in_maps


def kernel(L0_attn_x, L0_attn_ocr_context, L0_attn_Wq, L0_attn_bq, L0_attn_Wk,
           L0_attn_bk, L0_attn_Wv, L0_attn_bv, L0_attn_Wo, L0_attn_bo,
           L0_attn_ocr_bias, L0_ffn_weight, L0_ffn_bias):
    x = np.asarray(L0_attn_x, dtype=np.float32)
    B, S, D = x.shape
    n_cores = 8
    per_b = n_cores // B
    S_q = S // per_b
    cfg = Cfg(D, S_q, S)

    f32 = lambda a: np.ascontiguousarray(np.asarray(a, dtype=np.float32))
    in_maps = _make_in_maps(
        cfg, x, f32(L0_attn_Wq), f32(L0_attn_Wk), f32(L0_attn_Wv), f32(L0_attn_Wo),
        f32(L0_attn_bq), f32(L0_attn_bk), f32(L0_attn_bv), f32(L0_attn_bo),
        f32(L0_attn_ocr_bias), f32(L0_ffn_weight), f32(L0_ffn_bias), n_cores)

    nc = build_nc(cfg)
    res = run_bass_kernel_spmd(nc, in_maps, core_ids=list(range(n_cores)))
    out = np.empty((B, S, D), dtype=np.float32)
    for c in range(n_cores):
        b, half = c // per_b, c % per_b
        out[b, half * S_q:(half + 1) * S_q, :] = res.results[c]["out"]
    return out


# version 51
# speedup vs baseline: 1.0434x; 1.0174x over previous
import numpy as np
from contextlib import ExitStack

import concourse.bass as bass
import concourse.mybir as mybir
from concourse.bass_utils import run_bass_kernel_spmd

F32 = mybir.dt.float32
BF16 = mybir.dt.bfloat16
I32 = mybir.dt.int32
U8 = mybir.dt.uint8
AF = mybir.ActivationFunctionType
ALU = mybir.AluOpType
AX = mybir.AxisListType

HD = 64
LN_EPS = 1e-5
FP6_MAX = 7.5
MX_EPS = 1e-30
DTB = {F32: 4, BF16: 2, I32: 4}
NONE_TOK = ("dve", 0)


class Cfg:
    def __init__(self, D, S_q, S_kv):
        self.D, self.S_q, self.S_kv = D, S_q, S_kv
        self.H = D // HD
        self.NP = self.H // 2
        self.FC = D // 128
        self.KC = S_kv // 128
        self.QT = S_q // 512
        self.NG = S_kv // 512
        self.NGQ = S_q // 512
        self.NTQ = S_q // 128
        self.JH = D // 512
        assert self.NP == self.FC


class Trace:
    def __init__(self):
        self.ops = []
        self.cnt = {}
        self.pending = {}

    def tok(self, sem):
        return (sem, self.cnt.get(sem, 0))

    def barrier(self):
        toks = [self.tok(s) for s in
                ("pe", "act", "dve", "pool",
                 "dxa", "dxb", "dxc", "dxd", "dwa", "dwb", "dwc", "dwd",
                 "dva", "dvb")]
        toks = [t for t in toks if t[1] > 0]
        for e in ("sync", "tensor", "scalar", "vector", "gpsimd"):
            self.pending.setdefault(e, []).extend(toks)

    def emit(self, eng, fn, waits=(), inc=None, inc_by=1):
        ws = list(self.pending.pop(eng, []))
        for t in waits:
            if t is None or t[0] is None or t[1] <= 0:
                continue
            s, v = t
            assert v <= self.cnt.get(s, 0), f"wait on future {s}={v} (now {self.cnt.get(s, 0)})"
            ws.append((s, v))
        if inc is not None:
            self.cnt[inc] = self.cnt.get(inc, 0) + inc_by
        self.ops.append((eng, fn, ws, inc, inc_by))
        return self.tok(inc) if inc else None

    def replay(self, engine_names, eng, sems):
        seen = {}
        for (e, fn, ws, inc, inc_by) in self.ops:
            if e not in engine_names:
                continue
            for (s, v) in ws:
                if seen.get(s, 0) >= v:
                    continue
                eng.wait_ge(sems[s], v)
                seen[s] = v
            inst = fn(eng)
            if inc is not None:
                assert inst is not None, f"op on {e} incrementing {inc} returned None"
                inst.then_inc(sems[inc], inc_by)


class DmaQ:

    def __init__(self, trace, engine, lane_sems):
        self.T = trace
        self.engine = engine
        self.lanes = lane_sems
        self.uses = [0] * len(lane_sems)
        self.i = 0

    def dma(self, fn, waits=()):
        ln = self.i % len(self.lanes)
        self.i += 1
        sem = self.lanes[ln]
        issue_waits = list(waits) + [(sem, 16 * self.uses[ln])]
        self.uses[ln] += 1
        self.T.emit(self.engine, fn, waits=issue_waits, inc=sem, inc_by=16)
        return (sem, 16 * self.uses[ln])

    def total_tokens(self):
        return [(s, 16 * u) for s, u in zip(self.lanes, self.uses) if u > 0]


BAND_MULT = float(np.exp(np.float64(0.1)))


def band_main_np():
    b = np.ones((128, 132), dtype=np.float32)
    for p in range(128):
        b[p, p:p + 5] = BAND_MULT
    return b


def corner_pred_np(on):
    c = np.ones((128, 2), dtype=np.float32)
    if on:
        c[126, 0] = BAND_MULT
        c[127, 0] = BAND_MULT
        c[127, 1] = BAND_MULT
    return c


def corner_succ_np(on):
    c = np.ones((128, 2), dtype=np.float32)
    if on:
        c[0, 0] = BAND_MULT
        c[0, 1] = BAND_MULT
        c[1, 1] = BAND_MULT
    return c


def build_nc(cfg: Cfg):
    D, FC, KC, QT, NP, JH = cfg.D, cfg.FC, cfg.KC, cfg.QT, cfg.NP, cfg.JH
    S_q, S_kv = cfg.S_q, cfg.S_kv

    nc = bass.Bass("TRN2", target_bir_lowering=False)
    ap = lambda t: t.ap()

    x_kv = ap(nc.dram_tensor("x_kv", [S_kv, D], F32, kind="ExternalInput"))
    wqT = ap(nc.dram_tensor("wqT", [D, D], F32, kind="ExternalInput"))
    wkT = ap(nc.dram_tensor("wkT", [D, D], F32, kind="ExternalInput"))
    wvT = ap(nc.dram_tensor("wvT", [D, D], F32, kind="ExternalInput"))
    woT = ap(nc.dram_tensor("woT", [D, D], F32, kind="ExternalInput"))
    wffn = ap(nc.dram_tensor("wffn", [D, D], F32, kind="ExternalInput"))
    bq2_d = ap(nc.dram_tensor("bq2", [128, FC], F32, kind="ExternalInput"))
    bk2_d = ap(nc.dram_tensor("bk2", [128, FC], F32, kind="ExternalInput"))
    ocr2_d = ap(nc.dram_tensor("ocr2", [128, FC], F32, kind="ExternalInput"))
    bv_d = ap(nc.dram_tensor("bv_row", [1, D], F32, kind="ExternalInput"))
    bo_d = ap(nc.dram_tensor("bo_row", [1, D], F32, kind="ExternalInput"))
    fb_d = ap(nc.dram_tensor("fb_row", [1, D], F32, kind="ExternalInput"))
    ident_d = ap(nc.dram_tensor("ident", [128, 128], BF16, kind="ExternalInput"))
    ones128_d = ap(nc.dram_tensor("ones128", [128, 1], BF16, kind="ExternalInput"))
    ones1x128_d = ap(nc.dram_tensor("ones1x128", [65, 128], BF16, kind="ExternalInput"))
    ones1x128f_d = ap(nc.dram_tensor("ones1x128f", [65, 128], F32, kind="ExternalInput"))
    band_d = ap(nc.dram_tensor("band_main", [128, 132], BF16, kind="ExternalInput"))
    corn_d = ap(nc.dram_tensor("corners", [128, 8], BF16, kind="ExternalInput"))
    out_d = ap(nc.dram_tensor("out", [S_q, D], F32, kind="ExternalOutput"))
    q_dram = ap(nc.dram_tensor("q_scratch", [FC, 128, S_q], BF16))
    wd_dram = ap(nc.dram_tensor("wd_scratch", [FC, 128, D], BF16))
    rl_dram = ap(nc.dram_tensor("rl_scratch", [1, 1024], F32))

    T = Trace()
    E, tok = T.emit, T.tok

    with ExitStack() as st:
        def layout(specs):
            out, o = {}, 0
            for name, dt, n in specs:
                o = (o + 63) // 64 * 64
                out[name] = (o, dt, n)
                o += n * DTB[dt]
            return out, o

        p3spec, p3_bytes = layout(
            [(nm, F32, D) for nm in
             ("x1_0", "x1_1", "xhat2_0", "xhat2_1",
              "wfstage_0", "wfstage_1",
              "qy_0", "qt_0", "qv_0", "qc_0", "qr_0", "qs_0", "qe_0",
              "qy_1", "qt_1", "qv_1", "qc_1", "qr_1", "qs_1", "qe_1",
              "x1_2", "x1_3")]
            + [(nm, F32, 2) for nm in
               ("l3sum", "l3var", "l3rstd", "l3nmr", "l3musq")]
            + [(nm, F32, D // 32) for nm in
               ("amax_0", "scale_0", "rscale_0", "amax_1", "scale_1", "rscale_1")]
            + [(nm, BF16, D) for nm in ("xd_0", "xd_1", "xdT_0", "xdT_1")]
            + [("woT", BF16, FC * D), ("wd", BF16, FC * D)])
        VW = (D // HD) * 65
        A_BYTES = max(FC * S_kv * 2 + KC * VW * 2, p3_bytes)

        b1spec, b1_bytes = layout(
            [("w1", BF16, FC * D), ("w2", BF16, FC * D), ("w3", BF16, FC * D),
             ("xTg", BF16, FC * 512),
             ("xstage_0", BF16, D),
             ("xhat_0", BF16, D), ("xhat_1", BF16, D),
             ("xhat_2", BF16, D), ("xhat_3", BF16, D),
             ("qst_0", BF16, 512), ("qst_1", BF16, 512),
             ("qqs_0", BF16, 512), ("qqs_1", BF16, 512),
             ("qqs_2", BF16, 512), ("lstats", F32, 32)])

        b2spec, b2_bytes = layout(
            [("attnout", BF16, NP * QT * 512), ("pt_0", BF16, 1024),
             ("pt_1", BF16, 1024), ("pt_2", BF16, 1024), ("pt_3", BF16, 1024),
             ("lsb", F32, 1024), ("stageb", BF16, 512),
             ("q2_0", BF16, 512), ("q2_1", BF16, 512),
             ("negmh", F32, NP), ("nrm", F32, 8)])
        WH = 512
        wq_bytes = ((7 * WH * 4 + 63) // 64 * 64 + WH * 4 + 3 * 64 +
                    2 * WH * 4 + 2 * WH * 2 + 16 * 64)
        B_BYTES = max(b1_bytes, b2_bytes,
                      (b2_bytes + 63) // 64 * 64 + wq_bytes)

        regA = st.enter_context(nc.sbuf_tensor("regA", [128, A_BYTES], U8))
        regB = st.enter_context(nc.sbuf_tensor("regB", [128, B_BYTES], U8))

        def mkview(reg, off, dt, n):
            return reg.ap()[:, off:off + n * DTB[dt]].bitcast(dt)

        K_sb = mkview(regA, 0, BF16, FC * S_kv)
        V_sb = mkview(regA, FC * S_kv * 2, BF16, KC * VW)
        p3 = {nm: mkview(regA, o, dt, n) for nm, (o, dt, n) in p3spec.items()}
        p3["outst_0"] = p3["qe_0"]
        p3["outst_1"] = p3["qe_1"]
        for _par in range(2):
            p3[f"qvi_{_par}"] = mkview(regA, p3spec[f"qr_{_par}"][0], I32, D)

        b1 = {nm: mkview(regB, o, dt, n) for nm, (o, dt, n) in b1spec.items()}
        _ls = b1spec["lstats"][0]
        for _i, _nm in enumerate(("lsum", "lvar", "lrstd", "lnmr", "lmusq")):
            b1[_nm] = mkview(regB, _ls + 16 * _i, F32, 4)
        _f0 = (b2_bytes + 63) // 64 * 64
        _o2 = [_f0]
        def wqv(dt, n):
            o_al = (_o2[0] + 63) // 64 * 64
            _o2[0] = o_al + n * DTB[dt]
            return mkview(regB, o_al, dt, n)
        wq = {}
        wq_off = {}
        for k_nm in ("qy", "qt", "qv", "qc", "qr", "qs", "qe"):
            wq_off[k_nm] = (_o2[0] + 63) // 64 * 64
            wq[k_nm] = wqv(F32, WH)
        wq["qvi"] = wqv(I32, WH)
        wq["amax"] = wqv(F32, WH // 32)
        wq["scale"] = wqv(F32, WH // 32)
        wq["rscale"] = wqv(F32, WH // 32)
        wq["src_0"] = wqv(F32, WH)
        wq["src_1"] = wqv(F32, WH)
        wq["dst_0"] = wqv(BF16, WH)
        wq["dst_1"] = wqv(BF16, WH)
        assert _o2[0] <= B_BYTES, (_o2[0], B_BYTES)
        wq["wstg_0"] = mkview(regB, wq_off["qy"], BF16, D)
        wq["wstg_1"] = mkview(regB, wq_off["qv"], BF16, D)
        b2 = {nm: mkview(regB, o, dt, n) for nm, (o, dt, n) in b2spec.items()}
        b2["qq"] = b2["pt_0"]
        b2["rlb"] = b2["lsb"]

        cband = st.enter_context(nc.sbuf_tensor("cband", [128, 132], BF16))
        ccorn = st.enter_context(nc.sbuf_tensor("ccorn", [128, 8], BF16))
        cident = st.enter_context(nc.sbuf_tensor("cident", [128, 128], BF16))
        cones128 = st.enter_context(nc.sbuf_tensor("cones128", [128, 1], BF16))
        cones1x128 = st.enter_context(nc.sbuf_tensor("cones1x128", [65, 128], BF16))
        cones1x128f = st.enter_context(nc.sbuf_tensor("cones1x128f", [65, 128], F32))
        cbq2 = st.enter_context(nc.sbuf_tensor("cbq2", [128, FC], F32))
        cbk2 = st.enter_context(nc.sbuf_tensor("cbk2", [128, FC], F32))
        cocr2 = st.enter_context(nc.sbuf_tensor("cocr2", [128, FC], F32))
        cqbias = st.enter_context(nc.sbuf_tensor("cqbias", [128, FC], F32))
        crows = st.enter_context(nc.sbuf_tensor("crows", [65, D], BF16))
        cnrm = st.enter_context(nc.sbuf_tensor("cnrm", [128, 32], F32))

        psum = st.enter_context(nc.psum_tensor("psA", [128, 4096], F32))
        psum_ap = psum.ap()
        psum_b = psum_ap.bitcast(BF16)

        def bank(k, w=1):
            return psum_ap[:, 512 * k:512 * (k + w)]

        def bankb(k, w=1):
            return psum_b[:, 1024 * k:1024 * (k + w)]

        sems = {}
        for s in ("pe", "act", "dve", "pool",
                  "dxa", "dxb", "dxc", "dxd", "dwa", "dwb", "dwc", "dwd",
                  "dva", "dvb"):
            sems[s] = st.enter_context(nc.semaphore(s))
        QS = DmaQ(T, "sync", ["dxa", "dxb", "dxc", "dxd"])
        QG = DmaQ(T, "gpsimd", ["dwa", "dwb", "dwc", "dwd"])
        QV = DmaQ(T, "vector", ["dva", "dvb"])

        blk = st.enter_context(nc.Block())

        def ld(dst, src):
            return lambda e: e.dma_start(out=dst, in_=src)

        t_consts = []
        for dst, src in [
            (cband.ap()[:], band_d), (ccorn.ap()[:], corn_d),
            (cident.ap()[:], ident_d), (cones128.ap()[:], ones128_d),
            (cones1x128.ap()[:], ones1x128_d), (cones1x128f.ap()[:], ones1x128f_d),
            (cbq2.ap()[:], bq2_d), (cbk2.ap()[:], bk2_d), (cocr2.ap()[:], ocr2_d),
        ]:
            t_consts.append(QS.dma(ld(dst, src)))

        E("vector", lambda e: e.tensor_add(cqbias.ap()[:], cbq2.ap()[:], cocr2.ap()[:]),
          waits=t_consts, inc="dve")

        t_crows = [QG.dma(ld(crows.ap()[0:1, :], bv_d)),
                   QG.dma(ld(crows.ap()[32:33, :], bo_d)),
                   QG.dma(ld(crows.ap()[64:65, :], fb_d))]
        bv_row = crows.ap()[0:1, :]
        bo_row = crows.ap()[32:33, :]
        fb_row = crows.ap()[64:65, :]

        for c in range(FC):
            QG.dma(ld(b1["w1"][:, c * D:(c + 1) * D], wkT[128 * c:128 * (c + 1), :]))
        t_w1 = QG.total_tokens()
        for c in range(FC):
            QG.dma(ld(b1["w2"][:, c * D:(c + 1) * D], wvT[128 * c:128 * (c + 1), :]))
        t_w2 = QG.total_tokens()
        for c in range(FC):
            QG.dma(ld(b1["w3"][:, c * D:(c + 1) * D], wqT[128 * c:128 * (c + 1), :]))
        t_w3 = QG.total_tokens()

        vones = V_sb.rearrange("p (j h c) -> p j h c", j=KC, c=65)[:, :, :, 64:65]
        E("vector", lambda e: e.memset(vones, 1.0), inc="dve")
        t_vones = tok("dve")
        vxs = [V_sb[:, (KC - 4 + i) * VW:(KC - 4 + i) * VW + D] for i in range(4)]
        vones_tail = vones[:, KC - 4:KC, :, :]

        def ln_stats(xst, buf, col, pre=(), dummy=None):
            dmy = buf["dummy"][:] if dummy is None else dummy
            E("scalar", lambda e: e.activation(dmy, xst, AF.Copy,
                                               accum_out=buf["lsum"][:, col:col + 1]),
              waits=list(pre) + [tok("act")], inc="act")
            E("scalar", lambda e: e.activation(dmy, xst, AF.Square,
                                               accum_out=buf["lvar"][:, col:col + 1]),
              waits=[tok("act")], inc="act")
            return tok("act")

        def ln_chain(buf, c0, c1, pre=()):
            sl = slice(c0, c1)
            E("vector", lambda e: e.tensor_single_scalar(
                buf["lsum"][:, sl], buf["lsum"][:, sl], 1.0 / D, ALU.mult),
              waits=list(pre) + [tok("dve")], inc="dve")
            E("vector", lambda e: e.tensor_mul(
                buf["lmusq"][:, sl], buf["lsum"][:, sl], buf["lsum"][:, sl]),
              waits=[tok("dve")], inc="dve")
            E("vector", lambda e: e.tensor_scalar(
                buf["lvar"][:, sl], buf["lvar"][:, sl], 1.0 / D, LN_EPS,
                ALU.mult, ALU.add),
              waits=[tok("dve")], inc="dve")
            E("vector", lambda e: e.tensor_sub(
                buf["lvar"][:, sl], buf["lvar"][:, sl], buf["lmusq"][:, sl]),
              waits=[tok("dve")], inc="dve")
            t_var = tok("dve")
            E("scalar", lambda e: e.activation(buf["lrstd"][:, sl], buf["lvar"][:, sl],
                                               AF.Ln),
              waits=[t_var, tok("act")], inc="act")
            E("scalar", lambda e: e.activation(buf["lrstd"][:, sl], buf["lrstd"][:, sl],
                                               AF.Exp, scale=-0.5),
              waits=[tok("act")], inc="act")
            t_rstd = tok("act")
            E("vector", lambda e: e.scalar_tensor_tensor(
                buf["lnmr"][:, sl], buf["lsum"][:, sl], -1.0, buf["lrstd"][:, sl],
                ALU.mult, ALU.mult),
              waits=[t_rstd, tok("dve")], inc="dve")
            return tok("dve"), t_rstd

        def ln_apply(xst, xhat, buf, col, pre=()):
            E("scalar", lambda e: e.activation(
                xhat, xst, AF.Identity, bias=buf["lnmr"][:, col:col + 1],
                scale=buf["lrstd"][:, col:col + 1]),
              waits=list(pre) + [tok("act")], inc="act")
            return tok("act")

        xst_free = [t_vones] * 4
        xhat_free = [NONE_TOK, NONE_TOK, NONE_TOK, NONE_TOK]
        trbank_free = [NONE_TOK, NONE_TOK]
        kqbank_free = [NONE_TOK, NONE_TOK]
        vbank_free = [NONE_TOK, NONE_TOK]
        xTg_free = [NONE_TOK]
        ln_reader = [("act", 0), NONE_TOK]
        qst_free = [[], []]
        qqs_pe_free = [("pe", 0), ("pe", 0), ("pe", 0)]
        nrm_scr_free = NONE_TOK
        pending_norm = []
        qqs_flip = [0]
        bank_norm_free = [NONE_TOK, NONE_TOK]

        def emit_norm(src_sl, col, first, t_src, par):
            qf = qqs_flip[0]
            qqs_flip[0] = (qf + 1) % 3
            E("scalar", lambda e, src_sl=src_sl, qf=qf: e.activation(
                b1[f"qqs_{qf}"][:], src_sl, AF.Square),
              waits=[t_src, qqs_pe_free[qf], tok("act")], inc="act")
            t_sq = tok("act")
            pending_norm.append((col, first, t_sq, par, qf))

        def flush_norm(force=False):
            nonlocal nrm_scr_free
            while pending_norm and (force or len(pending_norm) >= 3):
                col, first, t_sq, par, qf = pending_norm.pop(0)
                nbank = bank(par)
                E("tensor", lambda e, nbank=nbank, qf=qf: e.matmul(
                    nbank[0:1, :], cones128.ap()[:], b1[f"qqs_{qf}"][:],
                    start=True, stop=True),
                  waits=[t_sq, bank_norm_free[par], trbank_free[par]], inc="pe")
                t_mm = tok("pe")
                qqs_pe_free[qf] = t_mm
                E("vector", lambda e, nbank=nbank: e.reduce_max(
                    cnrm.ap()[0:1, 30:31], nbank[0:1, :], axis=AX.X),
                  waits=[t_mm, nrm_scr_free], inc="dve")
                if first:
                    E("vector", lambda e, col=col: e.tensor_copy(
                        cnrm.ap()[0:1, col:col + 1], cnrm.ap()[0:1, 30:31]),
                      waits=[tok("dve")], inc="dve")
                else:
                    E("vector", lambda e, col=col: e.tensor_max(
                        cnrm.ap()[0:1, col:col + 1], cnrm.ap()[0:1, col:col + 1],
                        cnrm.ap()[0:1, 30:31]),
                      waits=[tok("dve")], inc="dve")
                nrm_scr_free = tok("dve")
                bank_norm_free[par] = tok("dve")

        def p1_front(g):
            nonlocal ln_reader
            applies = []
            t_stats = []
            for sub in range(4):
                i = 4 * g + sub
                xst = vxs[sub]
                t_x = QG.dma(ld(xst, x_kv[128 * i:128 * (i + 1), :]),
                             waits=[xst_free[sub]])
                t_stats.append(ln_stats(
                    xst, b1, sub, pre=[t_x, ln_reader[1], xhat_free[sub]],
                    dummy=b1[f"xhat_{sub}"][:]))
            t_ch_dve, t_ch_act = ln_chain(b1, 0, 4,
                                          pre=[t_stats[-1], ln_reader[0]])
            ln_reader = [tok("act"), t_ch_dve]
            for sub in range(4):
                xst = vxs[sub]
                xhat = b1[f"xhat_{sub}"][:]
                t_apply = ln_apply(xst, xhat, b1, sub,
                                   pre=[t_ch_dve, xhat_free[sub]])
                ln_reader[0] = t_apply
                xst_free[sub] = t_apply
                applies.append(t_apply)
            return applies

        def p1_tr(g, applies):
            tr_evac = []
            for sub in range(4):
                xhat = b1[f"xhat_{sub}"][:]
                pb = sub % 2
                tr_out = bankb(pb)
                for c in range(FC):
                    w = ([applies[sub], trbank_free[pb], bank_norm_free[pb]]
                         + t_consts) if c == 0 else []
                    E("tensor", lambda e, c=c, tr_out=tr_out, xhat=xhat: e.transpose(
                        tr_out[:, 128 * c:128 * (c + 1)],
                        xhat[:, 128 * c:128 * (c + 1)], cident.ap()[:]),
                      waits=w, inc="pe" if c == FC - 1 else None)
                t_tr = tok("pe")
                xhat_free[sub] = t_tr
                xTg_v = b1["xTg"].rearrange("p (c t) -> p c t", c=FC)[:, :, 128 * sub:128 * (sub + 1)]
                tr_v = tr_out[:, 0:FC * 128].rearrange("p (c t) -> p c t", c=FC)
                E("vector", lambda e, xTg_v=xTg_v, tr_v=tr_v: e.tensor_copy(xTg_v, tr_v),
                  waits=[t_tr] + (xTg_free if sub == 0 else []), inc="dve")
                tr_evac.append(tok("dve"))
                trbank_free[pb] = tok("dve")
            return tr_evac[-1]

        def p1_proj(g, t_xTg):
            nonlocal xTg_free
            for jp in range(FC):
                pb = jp % 2
                kps = bank(2 + pb)
                for c in range(FC):
                    w = ([t_xTg, kqbank_free[pb], bank_norm_free[pb]] + t_w1) if c == 0 else []
                    E("tensor", lambda e, c=c, jp=jp, kps=kps: e.matmul(
                        kps,
                        b1["w1"][:, c * D + 128 * jp: c * D + 128 * (jp + 1)],
                        b1["xTg"][:, 512 * c:512 * (c + 1)],
                        start=(c == 0), stop=(c == FC - 1)),
                      waits=w, inc="pe" if c == FC - 1 else None)
                t_kmm = tok("pe")
                dst = K_sb[:, jp * S_kv + 512 * g: jp * S_kv + 512 * (g + 1)]
                E("vector", lambda e, dst=dst, kps=kps, jp=jp: e.tensor_scalar(
                    dst, kps, cbk2.ap()[:, jp:jp + 1], None, ALU.add),
                  waits=[t_kmm] + t_consts, inc="dve")
                kqbank_free[pb] = tok("dve")
                emit_norm(dst, jp, g == 0, tok("dve"), pb)
                flush_norm()
            for sub in range(4):
                j = 4 * g + sub
                pb = sub % 2
                vps = bank(4 + 2 * pb, 2)[:, 0:D]
                for jh in range(JH):
                    w = []
                    if jh == 0:
                        w = [t_xTg, vbank_free[pb]]
                        if g == 0 and sub == 0:
                            w += t_crows + t_w2
                    E("tensor", lambda e, vps=vps, jh=jh: e.matmul(
                        vps[:, 512 * jh:512 * (jh + 1)],
                        cones1x128.ap()[0:1, :], bv_row[:, 512 * jh:512 * (jh + 1)],
                        start=True, stop=False),
                      waits=w, inc=None)
                    for c in range(FC):
                        E("tensor", lambda e, c=c, sub=sub, jh=jh, vps=vps: e.matmul(
                            vps[:, 512 * jh:512 * (jh + 1)],
                            b1["xTg"][:, 512 * c + 128 * sub: 512 * c + 128 * (sub + 1)],
                            b1["w2"][:, c * D + 512 * jh: c * D + 512 * (jh + 1)],
                            start=False, stop=(c == FC - 1)),
                          inc="pe" if (c == FC - 1 and jh == JH - 1) else None)
                t_vmm = tok("pe")
                dst = V_sb[:, j * VW:(j + 1) * VW].rearrange(
                    "p (h c) -> p h c", c=65)[:, :, 0:64]
                vsrc = vps.rearrange("p (h c) -> p h c", c=64)
                E("vector", lambda e, dst=dst, vsrc=vsrc: e.tensor_copy(dst, vsrc),
                  waits=[t_vmm], inc="dve")
                vbank_free[pb] = tok("dve")
            if g < cfg.NGQ:
                for jp in range(FC):
                    pb = jp % 2
                    qps = bank(2 + pb)
                    for c in range(FC):
                        w = ([kqbank_free[pb], bank_norm_free[pb], t_xTg] + t_w3) if c == 0 else []
                        E("tensor", lambda e, c=c, jp=jp, qps=qps: e.matmul(
                            qps,
                            b1["w3"][:, c * D + 128 * jp: c * D + 128 * (jp + 1)],
                            b1["xTg"][:, 512 * c:512 * (c + 1)],
                            start=(c == 0), stop=(c == FC - 1)),
                          waits=w, inc="pe" if c == FC - 1 else None)
                    t_qmm = tok("pe")
                    qb = jp % 2
                    qdst = b1[f"qst_{qb}"][:]
                    E("vector", lambda e, qdst=qdst, qps=qps, jp=jp: e.tensor_scalar(
                        qdst, qps, cqbias.ap()[:, jp:jp + 1], None, ALU.add),
                      waits=[t_qmm] + qst_free[qb], inc="dve")
                    kqbank_free[pb] = tok("dve")
                    t_qev = tok("dve")
                    t_qs = QS.dma(lambda e, qdst=qdst, jp=jp, g=g: e.dma_start(
                        out=q_dram[jp, :, 512 * g:512 * (g + 1)], in_=qdst),
                        waits=[t_qev])
                    emit_norm(qdst, 8 + jp, g == 0, t_qev, pb)
                    flush_norm()
                    qst_free[qb] = [t_qs, tok("act")]
            if g == cfg.NG - 1:
                flush_norm(force=True)
                flush_norm(force=True)
            xTg_free = [tok("pe")]

        ap_cur = p1_front(0)
        for g in range(cfg.NG):
            t_xTg = p1_tr(g, ap_cur)
            if g + 1 < cfg.NG:
                ap_cur = p1_front(g + 1)
            p1_proj(g, t_xTg)
        flush_norm()
        E("vector", lambda e: e.memset(vones_tail, 1.0),
          waits=list(ap_cur) + [tok("dve")], inc="dve")
        t_P1_pe = tok("pe")

        T.barrier()

        nrm = cnrm.ap()
        t_negmh = []
        for p in range(NP):
            E("vector", lambda e, p=p: e.tensor_mul(
                nrm[0:1, 16 + p:17 + p], nrm[0:1, p:p + 1], nrm[0:1, 8 + p:9 + p]),
              waits=[tok("dve")], inc="dve")
            E("scalar", lambda e, p=p: e.activation(
                nrm[0:1, 24:25], nrm[0:1, 16 + p:17 + p], AF.Ln),
              waits=[tok("dve"), tok("act")], inc="act")
            E("scalar", lambda e: e.activation(
                nrm[0:1, 24:25], nrm[0:1, 24:25], AF.Exp, scale=0.5),
              waits=[tok("act")], inc="act")
            E("vector", lambda e: e.tensor_scalar(
                nrm[0:1, 24:25], nrm[0:1, 24:25], -0.125, 7.9, ALU.mult, ALU.add),
              waits=[tok("act")], inc="dve")
            t_val = tok("dve")
            E("tensor", lambda e: e.matmul(
                bank(6)[:, 0:1], cones1x128f.ap()[0:1, :], nrm[0:1, 24:25],
                start=True, stop=True),
              waits=[t_val], inc="pe")
            E("vector", lambda e, p=p: e.tensor_copy(b2["negmh"][:, p:p + 1],
                                                     bank(6)[:, 0:1]),
              waits=[tok("pe")], inc="dve")
            t_negmh.append(tok("dve"))

        T.barrier()


        stbank_free = [NONE_TOK, NONE_TOK]
        pt_free = [NONE_TOK, NONE_TOK, NONE_TOK, NONE_TOK]
        av_free = NONE_TOK
        sums_free = NONE_TOK
        b6_free = tok("dve")


        def band_ops(qt, j):
            res = []
            e = j - 4 * qt
            if 0 <= e <= 3:
                lo = 128 * e - 2
                if e == 0:
                    res.append((0, 130, cband.ap()[:, 2:132], 0, 128))
                elif lo + 132 > 512:
                    res.append((lo, 512, cband.ap()[:, 0:512 - lo], 0, 128))
                else:
                    res.append((lo, lo + 132, cband.ap()[:, 0:132], 0, 128))
            if j == (4 * qt - 1) % KC:
                cap = ccorn.ap()[64:128, 4:6] if qt == 0 else ccorn.ap()[64:128, 0:2]
                res.append((0, 2, cap, 64, 128))
            if j == 4 * qt + 4:
                cap = ccorn.ap()[0:64, 6:8] if qt == QT - 1 else ccorn.ap()[0:64, 2:4]
                res.append((510, 512, cap, 0, 64))
            return res

        items = [(p, qt, j) for p in range(NP) for qt in range(QT) for j in range(KC)]
        av_free = [NONE_TOK, NONE_TOK, NONE_TOK, NONE_TOK]
        exp_toks = {}
        pending_tail = None
        NT2 = NP * QT
        q2_tok = [None] * NT2
        q2_qkdone = [("pe", 0), ("pe", 0)]

        def emit_qload(t2):
            p_, qt_ = t2 // QT, t2 % QT
            q2_tok[t2] = QS.dma(
                lambda e, p_=p_, qt_=qt_, t2=t2: e.dma_start(
                    out=b2[f"q2_{t2 % 2}"][:],
                    in_=q_dram[p_, :, 512 * qt_:512 * (qt_ + 1)]),
                waits=[q2_qkdone[t2 % 2]])

        emit_qload(0)
        emit_qload(1)

        def wq_steps(hidx):
            BLK = WH // 32
            B = wq
            par = hidx % 2
            src3 = wq[f"src_{par}"].rearrange("p (b k) -> p b k", k=32)
            dst = wq[f"dst_{par}"]
            dst3 = dst.rearrange("p (b k) -> p b k", k=32)
            rsb = B["rscale"].unsqueeze(2).to_broadcast([128, BLK, 32])
            ssb = B["scale"].unsqueeze(2).to_broadcast([128, BLK, 32])
            qy3 = B["qy"].rearrange("p (b k) -> p b k", k=32)
            qv3 = B["qv"].rearrange("p (b k) -> p b k", k=32)
            return [
                ("vector", lambda e: e.reduce_max(
                    B["amax"].unsqueeze(2), src3, axis=AX.X, apply_absolute_value=True)),
                ("vector", lambda e: e.tensor_scalar(
                    B["scale"][:], B["amax"][:], 1.0 / FP6_MAX, MX_EPS, ALU.mult, ALU.max)),
                ("vector", lambda e: e.reciprocal(B["rscale"][:], B["scale"][:])),
                ("pool", lambda e: e.tensor_tensor(qy3, src3, rsb, ALU.mult)),
                ("pool", lambda e: e.tensor_mul(B["qt"][:], B["qy"][:], B["qy"][:])),
                ("vector", lambda e: e.tensor_single_scalar(
                    B["qc"][:], B["qt"][:], 4.0, ALU.is_ge)),
                ("vector", lambda e: e.tensor_single_scalar(
                    B["qs"][:], B["qt"][:], 16.0, ALU.is_ge)),
                ("vector", lambda e: e.tensor_scalar(
                    B["qe"][:], B["qs"][:], -2.0, 8.0, ALU.mult, ALU.add)),
                ("vector", lambda e: e.scalar_tensor_tensor(
                    B["qr"][:], B["qc"][:], -4.0, B["qe"][:], ALU.mult, ALU.add)),
                ("vector", lambda e: e.tensor_scalar(
                    B["qe"][:], B["qs"][:], 0.25, 0.125, ALU.mult, ALU.add)),
                ("vector", lambda e: e.scalar_tensor_tensor(
                    B["qs"][:], B["qc"][:], 0.125, B["qe"][:], ALU.mult, ALU.add)),
                ("pool", lambda e: e.tensor_mul(B["qt"][:], B["qy"][:], B["qr"][:])),
                ("vector", lambda e: e.tensor_single_scalar(
                    B["qt"][:], B["qt"][:], -0.5, ALU.add)),
                ("vector", lambda e: e.tensor_copy(B["qvi"][:], B["qt"][:])),
                ("vector", lambda e: e.tensor_copy(B["qv"][:], B["qvi"][:])),
                ("vector", lambda e: e.tensor_tensor(B["qc"][:], B["qt"][:], B["qv"][:],
                                                     ALU.is_gt)),
                ("vector", lambda e: e.tensor_add(B["qv"][:], B["qv"][:], B["qc"][:])),
                ("pool", lambda e: e.tensor_mul(B["qv"][:], B["qv"][:], B["qs"][:])),
                ("vector", lambda e: e.tensor_scalar(
                    B["qv"][:], B["qv"][:], FP6_MAX, -FP6_MAX, ALU.min, ALU.max)),
                ("pool", lambda e: e.tensor_tensor(dst3, qv3, ssb, ALU.mult)),
            ]

        wq_queue = []
        NWH = FC * (D // WH)
        wq_state = {"tok": None, "h": 0, "dst_free": [None, None], "st": []}

        def wq_pump():
            if not wq_queue:
                if wq_state["h"] >= NWH:
                    return
                h = wq_state["h"]
                wq_state["h"] += 1
                jt, hh = h // (D // WH), h % (D // WH)
                par = h % 2
                w = [wq_state["tok"]] if wq_state["tok"] else []
                if wq_state["dst_free"][par] is not None:
                    w.append(wq_state["dst_free"][par])
                t_l = QS.dma(ld(wq[f"src_{par}"][:],
                                wffn[128 * jt:128 * (jt + 1), WH * hh:WH * (hh + 1)]),
                             waits=w)
                wq_state["tok"] = t_l
                wq_queue.extend(wq_steps(h))
                wq_queue.append(("dma", h))
                return
            item = wq_queue.pop(0)
            if item[0] == "dma":
                h = item[1]
                jt, hh = h // (D // WH), h % (D // WH)
                par = h % 2
                t_s = QS.dma(lambda e, jt=jt, hh=hh, par=par: e.dma_start(
                    out=wd_dram[jt, :, WH * hh:WH * (hh + 1)],
                    in_=wq[f"dst_{par}"][:]),
                    waits=[wq_state["tok"]])
                wq_state["dst_free"][par] = t_s
                wq_state["st"].append(t_s)
                return
            eng, fn = item
            sem = {"scalar": "act", "pool": "pool"}.get(eng, "dve")
            E(eng, fn, waits=[wq_state["tok"], tok(sem)], inc=sem)
            wq_state["tok"] = tok(sem)

        def emit_qk_band_exp(n, p, qt, j):
            u = n % 2
            stv = bank(2 * u, 2)
            t2 = p * QT + qt
            qsl = b2[f"q2_{t2 % 2}"]
            w0 = [stbank_free[u]]
            if j == 0:
                w0 = w0 + [q2_tok[t2]]
            E("tensor", lambda e, j=j, p=p, stv=stv, qsl=qsl: e.matmul(
                stv[:, 0:512],
                K_sb[0:64, p * S_kv + 128 * j: p * S_kv + 128 * (j + 1)],
                qsl[0:64, :],
                start=True, stop=True, tile_position=(0, 0)),
              waits=w0, inc=None)
            E("tensor", lambda e, j=j, p=p, stv=stv, qsl=qsl: e.matmul(
                stv[:, 512:1024],
                K_sb[64:128, p * S_kv + 128 * j: p * S_kv + 128 * (j + 1)],
                qsl[64:128, :],
                start=True, stop=True, tile_position=(64, 0)),
              inc="pe")
            t_qk = tok("pe")
            if j == KC - 1:
                q2_qkdone[t2 % 2] = t_qk
                if t2 + 2 < NT2:
                    emit_qload(t2 + 2)
            pt = b2[f"pt_{n % 4}"]
            w = [t_qk, pt_free[n % 4]] + ([t_negmh[p]] if n == 0 else [])
            E("scalar", lambda e, stv=stv, pt=pt, p=p: e.activation(
                pt[:], stv, AF.Exp, bias=b2["negmh"][:, p:p + 1], scale=0.125),
              waits=w, inc="act")
            t_exp = tok("act")
            stbank_free[u] = t_exp
            for (c0, c1, cap, r0, r1) in band_ops(qt, j):
                W = c1 - c0
                pt3 = pt.rearrange("p (h w) -> p h w", h=2)[r0:r1, :, c0:c1]
                cap3 = cap.unsqueeze(1).to_broadcast([r1 - r0, 2, W])
                E("pool", lambda e, pt3=pt3, cap3=cap3: e.tensor_tensor(
                    pt3, pt3, cap3, ALU.mult),
                  waits=[t_exp, tok("pool")], inc="pool")
                t_exp = tok("pool")
            exp_toks[n] = t_exp

        def emit_av(n, p, qt, j):
            pt = b2[f"pt_{n % 4}"]
            tp2 = (p * QT + qt) % 2
            ava, avb = bank(4 + 2 * tp2), bank(5 + 2 * tp2)
            w = [exp_toks.pop(n)]
            if j == 0:
                w += [av_free[2 * tp2], av_free[2 * tp2 + 1], t_vones]
            E("tensor", lambda e, p=p, j=j, pt=pt, ava=ava: e.matmul(
                ava[0:65, :],
                V_sb[:, j * VW + 65 * (2 * p): j * VW + 65 * (2 * p) + 65],
                pt[:, 0:512],
                start=(j == 0), stop=(j == KC - 1)),
              waits=w, inc=None)
            E("tensor", lambda e, p=p, j=j, pt=pt, avb=avb: e.matmul(
                avb[0:65, :],
                V_sb[:, j * VW + 65 * (2 * p + 1): j * VW + 65 * (2 * p + 2)],
                pt[:, 512:1024],
                start=(j == 0), stop=(j == KC - 1)),
              inc="pe")
            pt_free[n % 4] = tok("pe")

        lsb_free = NONE_TOK
        rlb_free = NONE_TOK
        stageb_free = NONE_TOK

        def emit_tail1(p, qt):
            nonlocal lsb_free
            t_avmm = tok("pe")
            tp2 = (p * QT + qt) % 2
            ava, avb = bank(4 + 2 * tp2), bank(5 + 2 * tp2)
            E("vector", lambda e, ava=ava: e.tensor_copy(
                b2["lsb"][64:65, 0:512], ava[64:65, :]),
              waits=[t_avmm, lsb_free, rlb_free], inc="dve")
            E("vector", lambda e, avb=avb: e.tensor_copy(
                b2["lsb"][64:65, 512:1024], avb[64:65, :]),
              waits=[tok("dve")], inc="dve")
            E("vector", lambda e: e.reciprocal(b2["lsb"][64:65, :],
                                               b2["lsb"][64:65, :]),
              waits=[tok("dve")], inc="dve")
            t_rl = tok("dve")
            lsb_free = t_rl
            return t_avmm, t_rl

        def emit_tail1b(t_rl):
            t_st = QS.dma(lambda e: e.dma_start(out=rl_dram, in_=b2["lsb"][64:65, :]),
                          waits=[t_rl])
            t_bc = QS.dma(lambda e: e.dma_start(
                out=b2["rlb"][:], in_=rl_dram.to_broadcast([128, 1024])),
                waits=[t_st])
            return t_bc

        def emit_tail2(p, qt, t_avmm_tile, t_bc):
            nonlocal rlb_free, stageb_free
            tp2 = (p * QT + qt) % 2
            ava, avb = bank(4 + 2 * tp2), bank(5 + 2 * tp2)
            ao_a = b2["attnout"][0:64, (p * QT + qt) * 512: (p * QT + qt + 1) * 512]
            ao_b = b2["attnout"][64:128, (p * QT + qt) * 512: (p * QT + qt + 1) * 512]
            E("vector", lambda e, ao_a=ao_a, ava=ava: e.tensor_tensor(
                ao_a, ava[0:64, :], b2["rlb"][0:64, 0:512], ALU.mult),
              waits=[t_bc, t_avmm_tile], inc="dve")
            av_free[2 * tp2] = tok("dve")
            E("vector", lambda e, avb=avb: e.tensor_tensor(
                b2["stageb"][0:64, :], avb[0:64, :], b2["rlb"][0:64, 512:1024],
                ALU.mult),
              waits=[stageb_free], inc="dve")
            t_avn_b = tok("dve")
            av_free[2 * tp2 + 1] = t_avn_b
            rlb_free = t_avn_b
            stageb_free = QS.dma(lambda e, ao_b=ao_b: e.dma_start(
                out=ao_b, in_=b2["stageb"][0:64, :]),
                waits=[t_avn_b])
            return stageb_free

        LAG = 4
        pending_1b = None

        def run_pending(n):
            nonlocal pending_1b, pending_tail
            if pending_1b is not None and n >= pending_1b[0]:
                _, tp, tqt, t_avm, t_rl = pending_1b
                t_bc = emit_tail1b(t_rl)
                pending_tail = (pending_1b[0] + 3, tp, tqt, t_avm, t_bc)
                pending_1b = None
            if pending_tail is not None and n >= pending_tail[0]:
                _, tp, tqt, t_avm, t_bc = pending_tail
                emit_tail2(tp, tqt, t_avm, t_bc)
                pending_tail = None

        for n, (p, qt, j) in enumerate(items):
            if n % 3 == 1:
                wq_pump()
            run_pending(n)
            if n >= LAG:
                pp, pq, pj = items[n - LAG]
                emit_av(n - LAG, pp, pq, pj)
                if pj == KC - 1:
                    t_avm, t_rl = emit_tail1(pp, pq)
                    pending_1b = (n + 3, pp, pq, t_avm, t_rl)
            emit_qk_band_exp(n, p, qt, j)
        for m in range(len(items) - LAG, len(items)):
            run_pending(m + len(items))
            pp, pq, pj = items[m]
            emit_av(m, pp, pq, pj)
            if pj == KC - 1:
                t_avm, t_rl = emit_tail1(pp, pq)
                pending_1b = (0, pp, pq, t_avm, t_rl)
        run_pending(10 ** 9)
        run_pending(10 ** 9)

        T.barrier()

        for c in range(FC):
            QG.dma(ld(p3["woT"][:, c * D:(c + 1) * D], woT[128 * c:128 * (c + 1), :]))
        t_wo = QG.total_tokens()

        def quant_steps(src_f32, dst_bf16, par, tab=None):
            BLK = D // 32
            B = tab if tab is not None else {
                k: p3[f"{k}_{par}"] for k in
                ("qy", "qt", "qv", "qc", "qr", "qs", "qe", "qvi",
                 "amax", "scale", "rscale")}
            src3 = src_f32.rearrange("p (b k) -> p b k", k=32)
            dst3 = dst_bf16.rearrange("p (b k) -> p b k", k=32)
            rsb = B["rscale"].unsqueeze(2).to_broadcast([128, BLK, 32])
            ssb = B["scale"].unsqueeze(2).to_broadcast([128, BLK, 32])
            qy3 = B["qy"].rearrange("p (b k) -> p b k", k=32)
            qv3 = B["qv"].rearrange("p (b k) -> p b k", k=32)
            tt = "vector"
            return [
                ("vector", lambda e: e.reduce_max(
                    B["amax"].unsqueeze(2), src3, axis=AX.X, apply_absolute_value=True)),
                ("vector", lambda e: e.tensor_scalar(
                    B["scale"][:], B["amax"][:], 1.0 / FP6_MAX, MX_EPS, ALU.mult, ALU.max)),
                ("vector", lambda e: e.reciprocal(B["rscale"][:], B["scale"][:])),
                (tt, lambda e: e.tensor_tensor(qy3, src3, rsb, ALU.mult)),
                ("scalar", lambda e: e.activation(B["qt"][:], B["qy"][:], AF.Square)),
                ("vector", lambda e: e.tensor_single_scalar(
                    B["qc"][:], B["qt"][:], 4.0, ALU.is_ge)),
                ("vector", lambda e: e.tensor_single_scalar(
                    B["qs"][:], B["qt"][:], 16.0, ALU.is_ge)),
                ("scalar", lambda e: e.activation(B["qe"][:], B["qs"][:], AF.Copy,
                                                  scale=-2.0, bias=8.0)),
                ("vector", lambda e: e.scalar_tensor_tensor(
                    B["qr"][:], B["qc"][:], -4.0, B["qe"][:], ALU.mult, ALU.add)),
                ("scalar", lambda e: e.activation(B["qe"][:], B["qs"][:], AF.Copy,
                                                  scale=0.25, bias=0.125)),
                ("vector", lambda e: e.scalar_tensor_tensor(
                    B["qs"][:], B["qc"][:], 0.125, B["qe"][:], ALU.mult, ALU.add)),
                ("vector", lambda e: e.tensor_mul(B["qt"][:], B["qy"][:], B["qr"][:])),
                ("scalar", lambda e: e.activation(B["qt"][:], B["qt"][:], AF.Copy,
                                                  scale=1.0, bias=-0.5)),
                ("vector", lambda e: e.tensor_copy(B["qvi"][:], B["qt"][:])),
                ("vector", lambda e: e.tensor_copy(B["qv"][:], B["qvi"][:])),
                ("vector", lambda e: e.tensor_tensor(B["qc"][:], B["qt"][:], B["qv"][:],
                                                     ALU.is_gt)),
                (tt, lambda e: e.tensor_add(B["qv"][:], B["qv"][:], B["qc"][:])),
                (tt, lambda e: e.tensor_mul(B["qv"][:], B["qv"][:], B["qs"][:])),
                ("vector", lambda e: e.tensor_scalar(
                    B["qv"][:], B["qv"][:], FP6_MAX, -FP6_MAX, ALU.min, ALU.max)),
                (tt, lambda e: e.tensor_tensor(dst3, qv3, ssb, ALU.mult)),
            ]

        qbuf_free = [NONE_TOK, NONE_TOK]

        def run_quant(tiles):
            chains = []
            for tt_ in tiles:
                src_, dst_, par, pre = tt_[:4]
                tab = tt_[4] if len(tt_) > 4 else None
                chains.append([list(pre) + ([qbuf_free[par]] if tab is None else []),
                               quant_steps(src_, dst_, par, tab)])
            nsteps = max(len(c[1]) for c in chains)
            finals = [None] * len(tiles)
            s3toks = [None] * len(tiles)
            for k in range(nsteps):
                for t, (wt, steps) in enumerate(chains):
                    if k >= len(steps):
                        continue
                    eng, fn = steps[k]
                    sem = {"scalar": "act", "pool": "pool"}.get(eng, "dve")
                    E(eng, fn, waits=wt + [tok(sem)], inc=sem)
                    chains[t][0] = [tok(sem)]
                    finals[t] = tok(sem)
                    if k == 3:
                        s3toks[t] = tok(sem)
            for (tt, tile) in enumerate(tiles):
                if len(tile) <= 4:
                    qbuf_free[tile[2]] = finals[tt]
            return finals, s3toks

        while wq_queue or wq_state["h"] < NWH:
            wq_pump()
        wf_free = [NONE_TOK, NONE_TOK]
        wd_trbank_free = [NONE_TOK, NONE_TOK]
        wdl = {}
        wdl_free = [NONE_TOK, NONE_TOK]
        for jt in range(FC):
            par = jt % 2
            wstg = wq[f"wstg_{par}"][:]
            t_ld = QS.dma(ld(wstg, wd_dram[jt]),
                          waits=wq_state["st"][-4:] + [wdl_free[par]])
            tr_out = bankb(4 + par)
            for c in range(FC):
                w = [t_ld, wd_trbank_free[par]] if c == 0 else []
                E("tensor", lambda e, c=c, tr_out=tr_out, wstg=wstg: e.transpose(
                    tr_out[:, 128 * c:128 * (c + 1)],
                    wstg[:, 128 * c:128 * (c + 1)],
                    cident.ap()[:]),
                  waits=w, inc="pe" if c == FC - 1 else None)
            t_tr = tok("pe")
            wdl_free[par] = t_tr
            wd_v = p3["wd"].rearrange("p (c j) -> p c j", c=FC)[:, :, 128 * jt:128 * (jt + 1)]
            tr_v = tr_out[:, 0:FC * 128].rearrange("p (c j) -> p c j", c=FC)
            E("vector", lambda e, wd_v=wd_v, tr_v=tr_v: e.tensor_copy(wd_v, tr_v),
              waits=[t_tr], inc="dve")
            wd_trbank_free[par] = tok("dve")
        t_wd = tok("dve")

        T.barrier()

        p3buf = {"dummy": b1["xstage_0"], "lsum": p3["l3sum"], "lvar": p3["l3var"],
                 "lrstd": p3["l3rstd"], "lnmr": p3["l3nmr"], "lmusq": p3["l3musq"]}
        x3_free = list(wf_free)
        ffnbank_free = NONE_TOK
        xdtr_free = [NONE_TOK, NONE_TOK]
        x1_free = [NONE_TOK] * 4
        xd_free = [NONE_TOK, NONE_TOK]
        xdT_free = [NONE_TOK, NONE_TOK]
        ln3_reader = [("act", 0), ("act", 0)]
        out_dma_par = [NONE_TOK, NONE_TOK]
        out_toks = []

        t_op_toks = {}
        t_x_toks = {}
        x1_toks = {}
        opbank_tok = {}
        tr_toks = {}
        xd_ready = {}
        apply_tok = {}

        def p3_stage1(i):
            ib = i % 2
            xst = p3[f"wfstage_{ib}"][:]
            t_x_toks[i] = QS.dma(ld(xst, x_kv[128 * i:128 * (i + 1), :]),
                                 waits=[x3_free[ib]])
            qt, s4 = i // 4, i % 4
            ops = bank(2 * ib, 2)[:, 0:D]
            for jh in range(JH):
                w = ([opbank_tok.pop(i - 2)] if (i - 2) in opbank_tok else []) \
                    if jh == 0 else []
                if i == 0 and jh == 0:
                    w += t_wo
                E("tensor", lambda e, ops=ops, jh=jh: e.matmul(
                    ops[:, 512 * jh:512 * (jh + 1)],
                    cones1x128.ap()[32:33, :], bo_row[:, 512 * jh:512 * (jh + 1)],
                    start=True, stop=False),
                  waits=w, inc=None)
                for c in range(FC):
                    ao = b2["attnout"][:, (c * QT + qt) * 512 + 128 * s4:
                                       (c * QT + qt) * 512 + 128 * (s4 + 1)]
                    E("tensor", lambda e, ao=ao, c=c, jh=jh, ops=ops: e.matmul(
                        ops[:, 512 * jh:512 * (jh + 1)],
                        ao, p3["woT"][:, c * D + 512 * jh: c * D + 512 * (jh + 1)],
                        start=False, stop=(c == FC - 1)),
                      inc="pe" if (c == FC - 1 and jh == JH - 1) else None)
            t_op_toks[i] = tok("pe")

        def p3_tr(i):
            ib = i % 2
            tr_out = bankb(4 + ib)
            t_xd = xd_ready.pop(i)
            for c in range(FC):
                w = [t_xd, xdtr_free[ib]] if c == 0 else []
                E("tensor", lambda e, c=c, tr_out=tr_out, ib=ib: e.transpose(
                    tr_out[:, 128 * c:128 * (c + 1)],
                    p3[f"xd_{ib}"][:, 128 * c:128 * (c + 1)], cident.ap()[:]),
                  waits=w, inc="pe" if c == FC - 1 else None)
            t_tr = tok("pe")
            xd_free[ib] = t_tr
            tr_toks[i] = t_tr

        def mk_quant_chain(i):
            ib = i % 2
            steps = []
            for k_, (eng, fn) in enumerate(quant_steps(
                    p3[f"xhat2_{ib}"][:], p3[f"xd_{ib}"][:], ib)):
                extras = []
                key = None
                if k_ == 3:
                    key = "qy"
                elif k_ == 7:
                    extras = [out_dma_par[ib]]
                elif k_ == 10:
                    key = "qelast"
                elif k_ == 19:
                    key = "final"
                steps.append((eng, fn, extras, key))
            return {"st8": [apply_tok.pop(i, ("act", 0)), xd_free[ib],
                            qbuf_free[ib]],
                    "steps": steps, "rec": {}}

        def mk_copy_chain(pair_lo):
            steps = []
            for par in range(2):
                i = pair_lo + par
                ib = i % 2
                tr_out = bankb(4 + ib)
                xdt = p3[f"xdT_{ib}"][:]
                steps.append(("scalar",
                              lambda e, xdt=xdt, tr_out=tr_out: e.activation(
                                  xdt, tr_out[:, 0:FC * 128], AF.Copy),
                              [tr_toks.pop(i), xdT_free[ib]], ("xdt", i)))
            return {"st8": [], "steps": steps, "rec": {}}

        def mk_prep_chain(i0p, chA, chB):
            steps = [None] * 12
            for par, pad in ((0, 0), (1, 2)):
                i = i0p + par
                ib, i4 = i % 2, i % 4
                xst = p3[f"wfstage_{ib}"][:]
                ops = bank(2 * ib, 2)[:, 0:D]
                x1 = p3[f"x1_{i4}"][:]
                steps += [None] * pad
                steps.append(("vector",
                              lambda e, x1=x1, ops=ops, xst=xst: e.tensor_tensor(
                                  x1, ops, xst, ALU.add),
                              [t_op_toks.pop(i), t_x_toks.pop(i),
                               (lambda i4=i4: x1_free[i4])], ("x1", par)))
                x1v = p3[f"x1_{i4}"][:]
                steps.append(("scalar",
                              lambda e, x1v=x1v, ib=ib: e.activation(
                                  p3buf["dummy"][:], x1v, AF.Copy,
                                  accum_out=p3buf["lsum"][:, ib:ib + 1]),
                              [ln3_reader[ib]], None))
                steps.append(("scalar",
                              lambda e, x1v=x1v, ib=ib: e.activation(
                                  p3buf["dummy"][:], x1v, AF.Square,
                                  accum_out=p3buf["lvar"][:, ib:ib + 1]),
                              [], None))
            buf = p3buf
            sl = slice(0, 2)
            steps += [
                ("vector", lambda e: e.tensor_single_scalar(
                    buf["lsum"][:, sl], buf["lsum"][:, sl], 1.0 / D, ALU.mult),
                 [], None),
                ("vector", lambda e: e.tensor_mul(
                    buf["lmusq"][:, sl], buf["lsum"][:, sl], buf["lsum"][:, sl]),
                 [], None),
                ("vector", lambda e: e.tensor_scalar(
                    buf["lvar"][:, sl], buf["lvar"][:, sl], 1.0 / D, LN_EPS,
                    ALU.mult, ALU.add), [], None),
                ("vector", lambda e: e.tensor_sub(
                    buf["lvar"][:, sl], buf["lvar"][:, sl], buf["lmusq"][:, sl]),
                 [], None),
                ("scalar", lambda e: e.activation(
                    buf["lrstd"][:, sl], buf["lvar"][:, sl], AF.Ln), [], None),
                ("scalar", lambda e: e.activation(
                    buf["lrstd"][:, sl], buf["lrstd"][:, sl], AF.Exp,
                    scale=-0.5), [], None),
                ("vector", lambda e: e.scalar_tensor_tensor(
                    buf["lnmr"][:, sl], buf["lsum"][:, sl], -1.0,
                    buf["lrstd"][:, sl], ALU.mult, ALU.mult), [], None),
            ]
            for par in range(2):
                i = i0p + par
                ib, i4 = i % 2, i % 4
                ch = chA if par == 0 else chB
                steps.append(("scalar",
                              lambda e, i4=i4, ib=ib: e.activation(
                                  p3[f"xhat2_{ib}"][:], p3[f"x1_{i4}"][:],
                                  AF.Identity,
                                  bias=p3buf["lnmr"][:, ib:ib + 1],
                                  scale=p3buf["lrstd"][:, ib:ib + 1]),
                              [(lambda ch=ch: ch["rec"]["qy"])], ("apply", par)))
            return {"st8": [], "steps": steps, "rec": {}}

        def run_zip(chains, mids=()):
            midmap = dict(mids)
            nsteps = max(len(c["steps"]) for c in chains)
            for r in range(nsteps):
                if r in midmap:
                    midmap.pop(r)()
                for c in chains:
                    if r >= len(c["steps"]) or c["steps"][r] is None:
                        continue
                    eng, fn, extras, key = c["steps"][r]
                    sem = {"scalar": "act", "pool": "pool"}.get(eng, "dve")
                    ext = [x() if callable(x) else x for x in extras]
                    E(eng, fn, waits=c["st8"] + ext + [tok(sem)], inc=sem)
                    c["st8"] = [tok(sem)]
                    if key is not None:
                        c["rec"][key] = tok(sem)
            for fn in midmap.values():
                fn()

        def p3_ffn(i, chC, qch):
            nonlocal ffnbank_free
            ib = i % 2
            i4 = i % 4
            x1 = p3[f"x1_{i4}"][:]
            xdt = p3[f"xdT_{ib}"][:]
            fps = bank(6, 2)[:, 0:D]
            for jh in range(JH):
                w = ([ffnbank_free, chC["rec"][("xdt", i)]] if jh == 0 else [])
                if i == 0 and jh == 0:
                    w += [t_wd]
                E("tensor", lambda e, fps=fps, jh=jh: e.matmul(
                    fps[:, 512 * jh:512 * (jh + 1)],
                    cones1x128.ap()[64:65, :], fb_row[:, 512 * jh:512 * (jh + 1)],
                    start=True, stop=False),
                  waits=w, inc=None)
                for c in range(FC):
                    E("tensor", lambda e, c=c, jh=jh, fps=fps, xdt=xdt: e.matmul(
                        fps[:, 512 * jh:512 * (jh + 1)],
                        xdt[:, 128 * c:128 * (c + 1)],
                        p3["wd"][:, c * D + 512 * jh: c * D + 512 * (jh + 1)],
                        start=False, stop=(c == FC - 1)),
                      inc="pe" if (c == FC - 1 and jh == JH - 1) else None)
            t_ffn = tok("pe")
            outst = p3[f"outst_{ib}"][:]
            w = [t_ffn, x1_toks.pop(i)]
            if qch is not None:
                w.append(qch["rec"]["qelast"])
            if len(out_toks) >= 2:
                w.append(out_toks[-2])
            E("vector", lambda e, outst=outst, fps=fps, x1=x1: e.tensor_tensor(
                outst, fps, x1, ALU.add),
              waits=w, inc="dve")
            t_out = tok("dve")
            x1_free[i4] = t_out
            ffnbank_free = t_out
            xdT_free[ib] = t_ffn
            out_toks.append(QS.dma(lambda e, outst=outst, i=i: e.dma_start(
                out=out_d[128 * i:128 * (i + 1), :], in_=outst),
                waits=[t_out]))
            out_dma_par[ib] = out_toks[-1]

        def resolve_recs(i0, chA, chB, chC, chD):
            for par, ch in ((0, chA), (1, chB)):
                qbuf_free[(i0 + par) % 2] = ch["rec"]["final"]
                xd_ready[i0 + par] = ch["rec"]["final"]
            if chC is not None:
                for par in range(2):
                    xdtr_free[(i0 - 2 + par) % 2] = chC["rec"][("xdt", i0 - 2 + par)]
            if chD is not None:
                for par in range(2):
                    i = i0 + 2 + par
                    t_x1 = chD["rec"][("x1", par)]
                    x1_toks[i] = t_x1
                    x3_free[i % 2] = t_x1
                    opbank_tok[i] = t_x1
                    t_ap = chD["rec"][("apply", par)]
                    ln3_reader[i % 2] = t_ap
                    apply_tok[i] = t_ap

        p3_stage1(0)
        p3_stage1(1)
        t_x1_0 = []
        for par in range(2):
            ib = par
            xst = p3[f"wfstage_{ib}"][:]
            ops = bank(2 * ib, 2)[:, 0:D]
            x1 = p3[f"x1_{par}"][:]
            E("vector", lambda e, x1=x1, ops=ops, xst=xst: e.tensor_tensor(
                x1, ops, xst, ALU.add),
              waits=[t_op_toks.pop(par), t_x_toks.pop(par)], inc="dve")
            x1_toks[par] = tok("dve")
            x3_free[ib] = tok("dve")
            opbank_tok[par] = tok("dve")
            t_x1_0.append(tok("dve"))
        t_sts0 = [ln_stats(p3[f"x1_{par}"][:], p3buf, par, pre=[t_x1_0[par]])
                  for par in range(2)]
        t_ch_dve0, _ = ln_chain(p3buf, 0, 2, pre=[t_sts0[-1]])
        for par in range(2):
            apply_tok[par] = ln_apply(p3[f"x1_{par}"][:], p3[f"xhat2_{par}"][:],
                                      p3buf, par, pre=[t_ch_dve0])
            ln3_reader[par] = apply_tok[par]

        NPAIR = cfg.NTQ // 2
        for kw in range(NPAIR):
            i0 = 2 * kw
            if kw >= 1:
                p3_tr(i0 - 2)
                p3_tr(i0 - 1)
            if kw + 1 < NPAIR:
                p3_stage1(i0 + 2)
                p3_stage1(i0 + 3)
            chA = mk_quant_chain(i0)
            chB = mk_quant_chain(i0 + 1)
            chC = mk_copy_chain(i0 - 2) if kw >= 1 else None
            chD = mk_prep_chain(i0 + 2, chA, chB) if kw + 1 < NPAIR else None
            chains = [c for c in (chA, chB, chC, chD) if c is not None]
            mids = []
            if kw >= 1:
                mids = [(12, lambda i=i0 - 2, chC=chC, chA=chA: p3_ffn(i, chC, chA)),
                        (16, lambda i=i0 - 1, chC=chC, chB=chB: p3_ffn(i, chC, chB))]
            run_zip(chains, mids)
            resolve_recs(i0, chA, chB, chC, chD)

        i0 = cfg.NTQ - 2
        p3_tr(i0)
        p3_tr(i0 + 1)
        chC = mk_copy_chain(i0)
        run_zip([chC])
        p3_ffn(i0, chC, None)
        p3_ffn(i0 + 1, chC, None)

        E("gpsimd", lambda e: e.engine_nop(),
          waits=QG.total_tokens() + QS.total_tokens(), inc=None)

        for name, filt in (("sync", ("sync",)), ("tensor", ("tensor",)),
                           ("scalar", ("scalar",)), ("vector", ("vector",)),
                           ("gpsimd", ("gpsimd", "pool"))):
            def body(eng, filt=filt):
                T.replay(filt, eng, sems)
            getattr(blk, name)(body)

    return nc



def _make_in_maps(cfg, x, Wq, Wk, Wv, Wo, bq, bk, bv, bo, ocr, Wf, fbias, n_cores):
    import ml_dtypes
    bft = ml_dtypes.bfloat16
    B, S, D = x.shape
    per_b = n_cores // B
    S_q = cfg.S_q
    FC = cfg.FC
    shared = {
        "wqT": np.ascontiguousarray(Wq.T), "wkT": np.ascontiguousarray(Wk.T),
        "wvT": np.ascontiguousarray(Wv.T), "woT": np.ascontiguousarray(Wo.T),
        "wffn": Wf,
        "bq2": np.ascontiguousarray(bq.reshape(FC, 128).T),
        "bk2": np.ascontiguousarray(bk.reshape(FC, 128).T),
        "ocr2": np.ascontiguousarray(ocr.reshape(FC, 128).T),
        "bv_row": bv.reshape(1, D).copy(), "bo_row": bo.reshape(1, D).copy(),
        "fb_row": fbias.reshape(1, D).copy(),
        "ident": np.eye(128, dtype=bft),
        "ones128": np.ones((128, 1), dtype=bft),
        "ones1x128": np.ones((65, 128), dtype=bft),
        "ones1x128f": np.ones((65, 128), dtype=np.float32),
        "band_main": band_main_np().astype(bft),
    }
    in_maps = []
    for c in range(n_cores):
        b, half = c // per_b, c % per_b
        q_off = half * S_q
        m = dict(shared)
        m["x_kv"] = np.ascontiguousarray(np.roll(x[b], -q_off, axis=0))
        corners = np.zeros((128, 8), dtype=np.float32)
        corners[:, 0:2] = corner_pred_np(True)
        corners[:, 2:4] = corner_succ_np(True)
        corners[:, 4:6] = corner_pred_np(q_off > 0)
        corners[:, 6:8] = corner_succ_np(q_off + S_q < S)
        m["corners"] = corners.astype(bft)
        in_maps.append(m)
    return in_maps


def kernel(L0_attn_x, L0_attn_ocr_context, L0_attn_Wq, L0_attn_bq, L0_attn_Wk,
           L0_attn_bk, L0_attn_Wv, L0_attn_bv, L0_attn_Wo, L0_attn_bo,
           L0_attn_ocr_bias, L0_ffn_weight, L0_ffn_bias):
    x = np.asarray(L0_attn_x, dtype=np.float32)
    B, S, D = x.shape
    n_cores = 8
    per_b = n_cores // B
    S_q = S // per_b
    cfg = Cfg(D, S_q, S)

    f32 = lambda a: np.ascontiguousarray(np.asarray(a, dtype=np.float32))
    in_maps = _make_in_maps(
        cfg, x, f32(L0_attn_Wq), f32(L0_attn_Wk), f32(L0_attn_Wv), f32(L0_attn_Wo),
        f32(L0_attn_bq), f32(L0_attn_bk), f32(L0_attn_bv), f32(L0_attn_bo),
        f32(L0_attn_ocr_bias), f32(L0_ffn_weight), f32(L0_ffn_bias), n_cores)

    nc = build_nc(cfg)
    res = run_bass_kernel_spmd(nc, in_maps, core_ids=list(range(n_cores)))
    out = np.empty((B, S, D), dtype=np.float32)
    for c in range(n_cores):
        b, half = c // per_b, c % per_b
        out[b, half * S_q:(half + 1) * S_q, :] = res.results[c]["out"]
    return out


# version 52 (final)
# speedup vs baseline: 1.0492x; 1.0056x over previous
import numpy as np
from contextlib import ExitStack

import concourse.bass as bass
import concourse.mybir as mybir
from concourse.bass_utils import run_bass_kernel_spmd

F32 = mybir.dt.float32
BF16 = mybir.dt.bfloat16
I32 = mybir.dt.int32
U8 = mybir.dt.uint8
AF = mybir.ActivationFunctionType
ALU = mybir.AluOpType
AX = mybir.AxisListType

HD = 64
LN_EPS = 1e-5
FP6_MAX = 7.5
MX_EPS = 1e-30
DTB = {F32: 4, BF16: 2, I32: 4}
NONE_TOK = ("dve", 0)


class Cfg:
    def __init__(self, D, S_q, S_kv):
        self.D, self.S_q, self.S_kv = D, S_q, S_kv
        self.H = D // HD
        self.NP = self.H // 2
        self.FC = D // 128
        self.KC = S_kv // 128
        self.QT = S_q // 512
        self.NG = S_kv // 512
        self.NGQ = S_q // 512
        self.NTQ = S_q // 128
        self.JH = D // 512
        assert self.NP == self.FC


class Trace:
    def __init__(self):
        self.ops = []
        self.cnt = {}
        self.pending = {}

    def tok(self, sem):
        return (sem, self.cnt.get(sem, 0))

    def barrier(self):
        toks = [self.tok(s) for s in
                ("pe", "act", "dve", "pool",
                 "dxa", "dxb", "dxc", "dxd", "dwa", "dwb", "dwc", "dwd",
                 "dva", "dvb")]
        toks = [t for t in toks if t[1] > 0]
        for e in ("sync", "tensor", "scalar", "vector", "gpsimd"):
            self.pending.setdefault(e, []).extend(toks)

    def emit(self, eng, fn, waits=(), inc=None, inc_by=1):
        ws = list(self.pending.pop(eng, []))
        for t in waits:
            if t is None or t[0] is None or t[1] <= 0:
                continue
            s, v = t
            assert v <= self.cnt.get(s, 0), f"wait on future {s}={v} (now {self.cnt.get(s, 0)})"
            ws.append((s, v))
        if inc is not None:
            self.cnt[inc] = self.cnt.get(inc, 0) + inc_by
        self.ops.append((eng, fn, ws, inc, inc_by))
        return self.tok(inc) if inc else None

    def replay(self, engine_names, eng, sems):
        seen = {}
        for (e, fn, ws, inc, inc_by) in self.ops:
            if e not in engine_names:
                continue
            for (s, v) in ws:
                if seen.get(s, 0) >= v:
                    continue
                eng.wait_ge(sems[s], v)
                seen[s] = v
            inst = fn(eng)
            if inc is not None:
                assert inst is not None, f"op on {e} incrementing {inc} returned None"
                inst.then_inc(sems[inc], inc_by)


class DmaQ:

    def __init__(self, trace, engine, lane_sems):
        self.T = trace
        self.engine = engine
        self.lanes = lane_sems
        self.uses = [0] * len(lane_sems)
        self.i = 0

    def dma(self, fn, waits=()):
        ln = self.i % len(self.lanes)
        self.i += 1
        sem = self.lanes[ln]
        issue_waits = list(waits) + [(sem, 16 * self.uses[ln])]
        self.uses[ln] += 1
        self.T.emit(self.engine, fn, waits=issue_waits, inc=sem, inc_by=16)
        return (sem, 16 * self.uses[ln])

    def total_tokens(self):
        return [(s, 16 * u) for s, u in zip(self.lanes, self.uses) if u > 0]


BAND_MULT = float(np.exp(np.float64(0.1)))


def band_main_np():
    b = np.ones((128, 132), dtype=np.float32)
    for p in range(128):
        b[p, p:p + 5] = BAND_MULT
    return b


def corner_pred_np(on):
    c = np.ones((128, 2), dtype=np.float32)
    if on:
        c[126, 0] = BAND_MULT
        c[127, 0] = BAND_MULT
        c[127, 1] = BAND_MULT
    return c


def corner_succ_np(on):
    c = np.ones((128, 2), dtype=np.float32)
    if on:
        c[0, 0] = BAND_MULT
        c[0, 1] = BAND_MULT
        c[1, 1] = BAND_MULT
    return c


def build_nc(cfg: Cfg):
    D, FC, KC, QT, NP, JH = cfg.D, cfg.FC, cfg.KC, cfg.QT, cfg.NP, cfg.JH
    S_q, S_kv = cfg.S_q, cfg.S_kv

    nc = bass.Bass("TRN2", target_bir_lowering=False)
    ap = lambda t: t.ap()

    x_kv = ap(nc.dram_tensor("x_kv", [S_kv, D], F32, kind="ExternalInput"))
    wqT = ap(nc.dram_tensor("wqT", [D, D], F32, kind="ExternalInput"))
    wkT = ap(nc.dram_tensor("wkT", [D, D], F32, kind="ExternalInput"))
    wvT = ap(nc.dram_tensor("wvT", [D, D], F32, kind="ExternalInput"))
    woT = ap(nc.dram_tensor("woT", [D, D], F32, kind="ExternalInput"))
    wffn = ap(nc.dram_tensor("wffn", [D, D], F32, kind="ExternalInput"))
    bq2_d = ap(nc.dram_tensor("bq2", [128, FC], F32, kind="ExternalInput"))
    bk2_d = ap(nc.dram_tensor("bk2", [128, FC], F32, kind="ExternalInput"))
    ocr2_d = ap(nc.dram_tensor("ocr2", [128, FC], F32, kind="ExternalInput"))
    bv_d = ap(nc.dram_tensor("bv_row", [1, D], F32, kind="ExternalInput"))
    bo_d = ap(nc.dram_tensor("bo_row", [1, D], F32, kind="ExternalInput"))
    fb_d = ap(nc.dram_tensor("fb_row", [1, D], F32, kind="ExternalInput"))
    ident_d = ap(nc.dram_tensor("ident", [128, 128], BF16, kind="ExternalInput"))
    ones128_d = ap(nc.dram_tensor("ones128", [128, 1], BF16, kind="ExternalInput"))
    ones1x128_d = ap(nc.dram_tensor("ones1x128", [65, 128], BF16, kind="ExternalInput"))
    ones1x128f_d = ap(nc.dram_tensor("ones1x128f", [65, 128], F32, kind="ExternalInput"))
    band_d = ap(nc.dram_tensor("band_main", [128, 132], BF16, kind="ExternalInput"))
    corn_d = ap(nc.dram_tensor("corners", [128, 8], BF16, kind="ExternalInput"))
    out_d = ap(nc.dram_tensor("out", [S_q, D], F32, kind="ExternalOutput"))
    q_dram = ap(nc.dram_tensor("q_scratch", [FC, 128, S_q], BF16))
    wd_dram = ap(nc.dram_tensor("wd_scratch", [FC, 128, D], BF16))
    rl_dram = ap(nc.dram_tensor("rl_scratch", [1, 1024], F32))

    T = Trace()
    E, tok = T.emit, T.tok

    with ExitStack() as st:
        def layout(specs):
            out, o = {}, 0
            for name, dt, n in specs:
                o = (o + 63) // 64 * 64
                out[name] = (o, dt, n)
                o += n * DTB[dt]
            return out, o

        p3spec, p3_bytes = layout(
            [(nm, F32, D) for nm in
             ("x1_0", "x1_1", "xhat2_0", "xhat2_1",
              "wfstage_0", "wfstage_1",
              "qy_0", "qt_0", "qv_0", "qc_0", "qr_0", "qs_0", "qe_0",
              "qy_1", "qt_1", "qv_1", "qc_1", "qr_1", "qs_1", "qe_1",
              "x1_2", "x1_3")]
            + [(nm, F32, 2) for nm in
               ("l3sum", "l3var", "l3rstd", "l3nmr", "l3musq")]
            + [(nm, F32, D // 32) for nm in
               ("amax_0", "scale_0", "rscale_0", "amax_1", "scale_1", "rscale_1")]
            + [(nm, BF16, D) for nm in ("xd_0", "xd_1", "xdT_0", "xdT_1")]
            + [("woT", BF16, FC * D), ("wd", BF16, FC * D)])
        VW = (D // HD) * 65
        A_BYTES = max(FC * S_kv * 2 + KC * VW * 2, p3_bytes)

        b1spec, b1_bytes = layout(
            [("w1", BF16, FC * D), ("w2", BF16, FC * D), ("w3", BF16, FC * D),
             ("xTg", BF16, FC * 512),
             ("xstage_0", BF16, D),
             ("xhat_0", BF16, D), ("xhat_1", BF16, D),
             ("xhat_2", BF16, D), ("xhat_3", BF16, D),
             ("qst_0", BF16, 512), ("qst_1", BF16, 512),
             ("qqs_0", BF16, 512), ("qqs_1", BF16, 512),
             ("qqs_2", BF16, 512), ("lstats", F32, 32)])

        b2spec, b2_bytes = layout(
            [("attnout", BF16, NP * QT * 512), ("pt_0", BF16, 1024),
             ("pt_1", BF16, 1024), ("pt_2", BF16, 1024), ("pt_3", BF16, 1024),
             ("pt_4", BF16, 1024),
             ("lsb", F32, 1024), ("stageb", BF16, 512),
             ("q2_0", BF16, 512), ("q2_1", BF16, 512),
             ("negmh", F32, NP), ("nrm", F32, 8)])
        WH = 512
        wq_bytes = ((7 * WH * 4 + 63) // 64 * 64 + WH * 4 + 3 * 64 +
                    2 * WH * 4 + 2 * WH * 2 + 16 * 64)
        B_BYTES = max(b1_bytes, b2_bytes,
                      (b2_bytes + 63) // 64 * 64 + wq_bytes)

        regA = st.enter_context(nc.sbuf_tensor("regA", [128, A_BYTES], U8))
        regB = st.enter_context(nc.sbuf_tensor("regB", [128, B_BYTES], U8))

        def mkview(reg, off, dt, n):
            return reg.ap()[:, off:off + n * DTB[dt]].bitcast(dt)

        K_sb = mkview(regA, 0, BF16, FC * S_kv)
        V_sb = mkview(regA, FC * S_kv * 2, BF16, KC * VW)
        p3 = {nm: mkview(regA, o, dt, n) for nm, (o, dt, n) in p3spec.items()}
        p3["outst_0"] = p3["qe_0"]
        p3["outst_1"] = p3["qe_1"]
        for _par in range(2):
            p3[f"qvi_{_par}"] = mkview(regA, p3spec[f"qr_{_par}"][0], I32, D)

        b1 = {nm: mkview(regB, o, dt, n) for nm, (o, dt, n) in b1spec.items()}
        _ls = b1spec["lstats"][0]
        for _i, _nm in enumerate(("lsum", "lvar", "lrstd", "lnmr", "lmusq")):
            b1[_nm] = mkview(regB, _ls + 16 * _i, F32, 4)
        _f0 = (b2_bytes + 63) // 64 * 64
        _o2 = [_f0]
        def wqv(dt, n):
            o_al = (_o2[0] + 63) // 64 * 64
            _o2[0] = o_al + n * DTB[dt]
            return mkview(regB, o_al, dt, n)
        wq = {}
        wq_off = {}
        for k_nm in ("qy", "qt", "qv", "qc", "qr", "qs", "qe"):
            wq_off[k_nm] = (_o2[0] + 63) // 64 * 64
            wq[k_nm] = wqv(F32, WH)
        wq["qvi"] = wqv(I32, WH)
        wq["amax"] = wqv(F32, WH // 32)
        wq["scale"] = wqv(F32, WH // 32)
        wq["rscale"] = wqv(F32, WH // 32)
        wq["src_0"] = wqv(F32, WH)
        wq["src_1"] = wqv(F32, WH)
        wq["dst_0"] = wqv(BF16, WH)
        wq["dst_1"] = wqv(BF16, WH)
        assert _o2[0] <= B_BYTES, (_o2[0], B_BYTES)
        wq["wstg_0"] = mkview(regB, wq_off["qy"], BF16, D)
        wq["wstg_1"] = mkview(regB, wq_off["qv"], BF16, D)
        b2 = {nm: mkview(regB, o, dt, n) for nm, (o, dt, n) in b2spec.items()}
        b2["qq"] = b2["pt_0"]
        b2["rlb"] = b2["lsb"]

        cband = st.enter_context(nc.sbuf_tensor("cband", [128, 132], BF16))
        ccorn = st.enter_context(nc.sbuf_tensor("ccorn", [128, 8], BF16))
        cident = st.enter_context(nc.sbuf_tensor("cident", [128, 128], BF16))
        cones128 = st.enter_context(nc.sbuf_tensor("cones128", [128, 1], BF16))
        cones1x128 = st.enter_context(nc.sbuf_tensor("cones1x128", [65, 128], BF16))
        cones1x128f = st.enter_context(nc.sbuf_tensor("cones1x128f", [65, 128], F32))
        cbq2 = st.enter_context(nc.sbuf_tensor("cbq2", [128, FC], F32))
        cbk2 = st.enter_context(nc.sbuf_tensor("cbk2", [128, FC], F32))
        cocr2 = st.enter_context(nc.sbuf_tensor("cocr2", [128, FC], F32))
        cqbias = st.enter_context(nc.sbuf_tensor("cqbias", [128, FC], F32))
        crows = st.enter_context(nc.sbuf_tensor("crows", [65, D], BF16))
        cnrm = st.enter_context(nc.sbuf_tensor("cnrm", [128, 32], F32))

        psum = st.enter_context(nc.psum_tensor("psA", [128, 4096], F32))
        psum_ap = psum.ap()
        psum_b = psum_ap.bitcast(BF16)

        def bank(k, w=1):
            return psum_ap[:, 512 * k:512 * (k + w)]

        def bankb(k, w=1):
            return psum_b[:, 1024 * k:1024 * (k + w)]

        sems = {}
        for s in ("pe", "act", "dve", "pool",
                  "dxa", "dxb", "dxc", "dxd", "dwa", "dwb", "dwc", "dwd",
                  "dva", "dvb"):
            sems[s] = st.enter_context(nc.semaphore(s))
        QS = DmaQ(T, "sync", ["dxa", "dxb", "dxc", "dxd"])
        QG = DmaQ(T, "gpsimd", ["dwa", "dwb", "dwc", "dwd"])
        QV = DmaQ(T, "vector", ["dva", "dvb"])

        blk = st.enter_context(nc.Block())

        def ld(dst, src):
            return lambda e: e.dma_start(out=dst, in_=src)

        t_consts = []
        for dst, src in [
            (cband.ap()[:], band_d), (ccorn.ap()[:], corn_d),
            (cident.ap()[:], ident_d), (cones128.ap()[:], ones128_d),
            (cones1x128.ap()[:], ones1x128_d), (cones1x128f.ap()[:], ones1x128f_d),
            (cbq2.ap()[:], bq2_d), (cbk2.ap()[:], bk2_d), (cocr2.ap()[:], ocr2_d),
        ]:
            t_consts.append(QS.dma(ld(dst, src)))

        E("vector", lambda e: e.tensor_add(cqbias.ap()[:], cbq2.ap()[:], cocr2.ap()[:]),
          waits=t_consts, inc="dve")

        t_crows = [QG.dma(ld(crows.ap()[0:1, :], bv_d)),
                   QG.dma(ld(crows.ap()[32:33, :], bo_d)),
                   QG.dma(ld(crows.ap()[64:65, :], fb_d))]
        bv_row = crows.ap()[0:1, :]
        bo_row = crows.ap()[32:33, :]
        fb_row = crows.ap()[64:65, :]

        for c in range(FC):
            QG.dma(ld(b1["w1"][:, c * D:(c + 1) * D], wkT[128 * c:128 * (c + 1), :]))
        t_w1 = QG.total_tokens()
        for c in range(FC):
            QG.dma(ld(b1["w2"][:, c * D:(c + 1) * D], wvT[128 * c:128 * (c + 1), :]))
        t_w2 = QG.total_tokens()
        for c in range(FC):
            QG.dma(ld(b1["w3"][:, c * D:(c + 1) * D], wqT[128 * c:128 * (c + 1), :]))
        t_w3 = QG.total_tokens()

        vones = V_sb.rearrange("p (j h c) -> p j h c", j=KC, c=65)[:, :, :, 64:65]
        E("vector", lambda e: e.memset(vones, 1.0), inc="dve")
        t_vones = tok("dve")
        vxs = [V_sb[:, (KC - 4 + i) * VW:(KC - 4 + i) * VW + D] for i in range(4)]
        vones_tail = vones[:, KC - 4:KC, :, :]

        def ln_stats(xst, buf, col, pre=(), dummy=None):
            dmy = buf["dummy"][:] if dummy is None else dummy
            E("scalar", lambda e: e.activation(dmy, xst, AF.Copy,
                                               accum_out=buf["lsum"][:, col:col + 1]),
              waits=list(pre) + [tok("act")], inc="act")
            E("scalar", lambda e: e.activation(dmy, xst, AF.Square,
                                               accum_out=buf["lvar"][:, col:col + 1]),
              waits=[tok("act")], inc="act")
            return tok("act")

        def ln_chain(buf, c0, c1, pre=()):
            sl = slice(c0, c1)
            E("vector", lambda e: e.tensor_single_scalar(
                buf["lsum"][:, sl], buf["lsum"][:, sl], 1.0 / D, ALU.mult),
              waits=list(pre) + [tok("dve")], inc="dve")
            E("vector", lambda e: e.tensor_mul(
                buf["lmusq"][:, sl], buf["lsum"][:, sl], buf["lsum"][:, sl]),
              waits=[tok("dve")], inc="dve")
            E("vector", lambda e: e.tensor_scalar(
                buf["lvar"][:, sl], buf["lvar"][:, sl], 1.0 / D, LN_EPS,
                ALU.mult, ALU.add),
              waits=[tok("dve")], inc="dve")
            E("vector", lambda e: e.tensor_sub(
                buf["lvar"][:, sl], buf["lvar"][:, sl], buf["lmusq"][:, sl]),
              waits=[tok("dve")], inc="dve")
            t_var = tok("dve")
            E("scalar", lambda e: e.activation(buf["lrstd"][:, sl], buf["lvar"][:, sl],
                                               AF.Ln),
              waits=[t_var, tok("act")], inc="act")
            E("scalar", lambda e: e.activation(buf["lrstd"][:, sl], buf["lrstd"][:, sl],
                                               AF.Exp, scale=-0.5),
              waits=[tok("act")], inc="act")
            t_rstd = tok("act")
            E("vector", lambda e: e.scalar_tensor_tensor(
                buf["lnmr"][:, sl], buf["lsum"][:, sl], -1.0, buf["lrstd"][:, sl],
                ALU.mult, ALU.mult),
              waits=[t_rstd, tok("dve")], inc="dve")
            return tok("dve"), t_rstd

        def ln_apply(xst, xhat, buf, col, pre=()):
            E("scalar", lambda e: e.activation(
                xhat, xst, AF.Identity, bias=buf["lnmr"][:, col:col + 1],
                scale=buf["lrstd"][:, col:col + 1]),
              waits=list(pre) + [tok("act")], inc="act")
            return tok("act")

        xst_free = [t_vones] * 4
        xhat_free = [NONE_TOK, NONE_TOK, NONE_TOK, NONE_TOK]
        trbank_free = [NONE_TOK, NONE_TOK]
        kqbank_free = [NONE_TOK, NONE_TOK]
        vbank_free = [NONE_TOK, NONE_TOK]
        xTg_free = [NONE_TOK]
        ln_reader = [("act", 0), NONE_TOK]
        qst_free = [[], []]
        qqs_pe_free = [("pe", 0), ("pe", 0), ("pe", 0)]
        nrm_scr_free = NONE_TOK
        pending_norm = []
        qqs_flip = [0]
        bank_norm_free = [NONE_TOK, NONE_TOK]

        def emit_norm(src_sl, col, first, t_src, par):
            qf = qqs_flip[0]
            qqs_flip[0] = (qf + 1) % 3
            E("scalar", lambda e, src_sl=src_sl, qf=qf: e.activation(
                b1[f"qqs_{qf}"][:], src_sl, AF.Square),
              waits=[t_src, qqs_pe_free[qf], tok("act")], inc="act")
            t_sq = tok("act")
            pending_norm.append((col, first, t_sq, par, qf))

        def flush_norm(force=False):
            nonlocal nrm_scr_free
            while pending_norm and (force or len(pending_norm) >= 3):
                col, first, t_sq, par, qf = pending_norm.pop(0)
                nbank = bank(par)
                E("tensor", lambda e, nbank=nbank, qf=qf: e.matmul(
                    nbank[0:1, :], cones128.ap()[:], b1[f"qqs_{qf}"][:],
                    start=True, stop=True),
                  waits=[t_sq, bank_norm_free[par], trbank_free[par]], inc="pe")
                t_mm = tok("pe")
                qqs_pe_free[qf] = t_mm
                E("vector", lambda e, nbank=nbank: e.reduce_max(
                    cnrm.ap()[0:1, 30:31], nbank[0:1, :], axis=AX.X),
                  waits=[t_mm, nrm_scr_free], inc="dve")
                if first:
                    E("vector", lambda e, col=col: e.tensor_copy(
                        cnrm.ap()[0:1, col:col + 1], cnrm.ap()[0:1, 30:31]),
                      waits=[tok("dve")], inc="dve")
                else:
                    E("vector", lambda e, col=col: e.tensor_max(
                        cnrm.ap()[0:1, col:col + 1], cnrm.ap()[0:1, col:col + 1],
                        cnrm.ap()[0:1, 30:31]),
                      waits=[tok("dve")], inc="dve")
                nrm_scr_free = tok("dve")
                bank_norm_free[par] = tok("dve")

        def p1_front(g):
            nonlocal ln_reader
            applies = []
            t_stats = []
            for sub in range(4):
                i = 4 * g + sub
                xst = vxs[sub]
                t_x = QG.dma(ld(xst, x_kv[128 * i:128 * (i + 1), :]),
                             waits=[xst_free[sub]])
                t_stats.append(ln_stats(
                    xst, b1, sub, pre=[t_x, ln_reader[1], xhat_free[sub]],
                    dummy=b1[f"xhat_{sub}"][:]))
            t_ch_dve, t_ch_act = ln_chain(b1, 0, 4,
                                          pre=[t_stats[-1], ln_reader[0]])
            ln_reader = [tok("act"), t_ch_dve]
            for sub in range(4):
                xst = vxs[sub]
                xhat = b1[f"xhat_{sub}"][:]
                t_apply = ln_apply(xst, xhat, b1, sub,
                                   pre=[t_ch_dve, xhat_free[sub]])
                ln_reader[0] = t_apply
                xst_free[sub] = t_apply
                applies.append(t_apply)
            return applies

        def p1_tr(g, applies):
            tr_evac = []
            for sub in range(4):
                xhat = b1[f"xhat_{sub}"][:]
                pb = sub % 2
                tr_out = bankb(pb)
                for c in range(FC):
                    w = ([applies[sub], trbank_free[pb], bank_norm_free[pb]]
                         + t_consts) if c == 0 else []
                    E("tensor", lambda e, c=c, tr_out=tr_out, xhat=xhat: e.transpose(
                        tr_out[:, 128 * c:128 * (c + 1)],
                        xhat[:, 128 * c:128 * (c + 1)], cident.ap()[:]),
                      waits=w, inc="pe" if c == FC - 1 else None)
                t_tr = tok("pe")
                xhat_free[sub] = t_tr
                xTg_v = b1["xTg"].rearrange("p (c t) -> p c t", c=FC)[:, :, 128 * sub:128 * (sub + 1)]
                tr_v = tr_out[:, 0:FC * 128].rearrange("p (c t) -> p c t", c=FC)
                E("vector", lambda e, xTg_v=xTg_v, tr_v=tr_v: e.tensor_copy(xTg_v, tr_v),
                  waits=[t_tr] + (xTg_free if sub == 0 else []), inc="dve")
                tr_evac.append(tok("dve"))
                trbank_free[pb] = tok("dve")
            return tr_evac[-1]

        def p1_proj(g, t_xTg):
            nonlocal xTg_free
            for jp in range(FC):
                pb = jp % 2
                kps = bank(2 + pb)
                for c in range(FC):
                    w = ([t_xTg, kqbank_free[pb], bank_norm_free[pb]] + t_w1) if c == 0 else []
                    E("tensor", lambda e, c=c, jp=jp, kps=kps: e.matmul(
                        kps,
                        b1["w1"][:, c * D + 128 * jp: c * D + 128 * (jp + 1)],
                        b1["xTg"][:, 512 * c:512 * (c + 1)],
                        start=(c == 0), stop=(c == FC - 1)),
                      waits=w, inc="pe" if c == FC - 1 else None)
                t_kmm = tok("pe")
                dst = K_sb[:, jp * S_kv + 512 * g: jp * S_kv + 512 * (g + 1)]
                E("vector", lambda e, dst=dst, kps=kps, jp=jp: e.tensor_scalar(
                    dst, kps, cbk2.ap()[:, jp:jp + 1], None, ALU.add),
                  waits=[t_kmm] + t_consts, inc="dve")
                kqbank_free[pb] = tok("dve")
                emit_norm(dst, jp, g == 0, tok("dve"), pb)
                flush_norm()
            for sub in range(4):
                j = 4 * g + sub
                pb = sub % 2
                vps = bank(4 + 2 * pb, 2)[:, 0:D]
                for jh in range(JH):
                    w = []
                    if jh == 0:
                        w = [t_xTg, vbank_free[pb]]
                        if g == 0 and sub == 0:
                            w += t_crows + t_w2
                    E("tensor", lambda e, vps=vps, jh=jh: e.matmul(
                        vps[:, 512 * jh:512 * (jh + 1)],
                        cones1x128.ap()[0:1, :], bv_row[:, 512 * jh:512 * (jh + 1)],
                        start=True, stop=False),
                      waits=w, inc=None)
                    for c in range(FC):
                        E("tensor", lambda e, c=c, sub=sub, jh=jh, vps=vps: e.matmul(
                            vps[:, 512 * jh:512 * (jh + 1)],
                            b1["xTg"][:, 512 * c + 128 * sub: 512 * c + 128 * (sub + 1)],
                            b1["w2"][:, c * D + 512 * jh: c * D + 512 * (jh + 1)],
                            start=False, stop=(c == FC - 1)),
                          inc="pe" if (c == FC - 1 and jh == JH - 1) else None)
                t_vmm = tok("pe")
                dst = V_sb[:, j * VW:(j + 1) * VW].rearrange(
                    "p (h c) -> p h c", c=65)[:, :, 0:64]
                vsrc = vps.rearrange("p (h c) -> p h c", c=64)
                E("vector", lambda e, dst=dst, vsrc=vsrc: e.tensor_copy(dst, vsrc),
                  waits=[t_vmm], inc="dve")
                vbank_free[pb] = tok("dve")
            if g < cfg.NGQ:
                for jp in range(FC):
                    pb = jp % 2
                    qps = bank(2 + pb)
                    for c in range(FC):
                        w = ([kqbank_free[pb], bank_norm_free[pb], t_xTg] + t_w3) if c == 0 else []
                        E("tensor", lambda e, c=c, jp=jp, qps=qps: e.matmul(
                            qps,
                            b1["w3"][:, c * D + 128 * jp: c * D + 128 * (jp + 1)],
                            b1["xTg"][:, 512 * c:512 * (c + 1)],
                            start=(c == 0), stop=(c == FC - 1)),
                          waits=w, inc="pe" if c == FC - 1 else None)
                    t_qmm = tok("pe")
                    qb = jp % 2
                    qdst = b1[f"qst_{qb}"][:]
                    E("vector", lambda e, qdst=qdst, qps=qps, jp=jp: e.tensor_scalar(
                        qdst, qps, cqbias.ap()[:, jp:jp + 1], None, ALU.add),
                      waits=[t_qmm] + qst_free[qb], inc="dve")
                    kqbank_free[pb] = tok("dve")
                    t_qev = tok("dve")
                    t_qs = QS.dma(lambda e, qdst=qdst, jp=jp, g=g: e.dma_start(
                        out=q_dram[jp, :, 512 * g:512 * (g + 1)], in_=qdst),
                        waits=[t_qev])
                    emit_norm(qdst, 8 + jp, g == 0, t_qev, pb)
                    flush_norm()
                    qst_free[qb] = [t_qs, tok("act")]
            if g == cfg.NG - 1:
                flush_norm(force=True)
                flush_norm(force=True)
            xTg_free = [tok("pe")]

        ap_cur = p1_front(0)
        for g in range(cfg.NG):
            t_xTg = p1_tr(g, ap_cur)
            if g + 1 < cfg.NG:
                ap_cur = p1_front(g + 1)
            p1_proj(g, t_xTg)
        flush_norm()
        E("vector", lambda e: e.memset(vones_tail, 1.0),
          waits=list(ap_cur) + [tok("dve")], inc="dve")
        t_P1_pe = tok("pe")

        T.barrier()

        nrm = cnrm.ap()
        t_negmh = []
        for p in range(NP):
            E("vector", lambda e, p=p: e.tensor_mul(
                nrm[0:1, 16 + p:17 + p], nrm[0:1, p:p + 1], nrm[0:1, 8 + p:9 + p]),
              waits=[tok("dve")], inc="dve")
            E("scalar", lambda e, p=p: e.activation(
                nrm[0:1, 24:25], nrm[0:1, 16 + p:17 + p], AF.Ln),
              waits=[tok("dve"), tok("act")], inc="act")
            E("scalar", lambda e: e.activation(
                nrm[0:1, 24:25], nrm[0:1, 24:25], AF.Exp, scale=0.5),
              waits=[tok("act")], inc="act")
            E("vector", lambda e: e.tensor_scalar(
                nrm[0:1, 24:25], nrm[0:1, 24:25], -0.125, 7.9, ALU.mult, ALU.add),
              waits=[tok("act")], inc="dve")
            t_val = tok("dve")
            E("tensor", lambda e: e.matmul(
                bank(6)[:, 0:1], cones1x128f.ap()[0:1, :], nrm[0:1, 24:25],
                start=True, stop=True),
              waits=[t_val], inc="pe")
            E("vector", lambda e, p=p: e.tensor_copy(b2["negmh"][:, p:p + 1],
                                                     bank(6)[:, 0:1]),
              waits=[tok("pe")], inc="dve")
            t_negmh.append(tok("dve"))

        T.barrier()


        stbank_free = [NONE_TOK, NONE_TOK]
        pt_free = [NONE_TOK] * 5
        av_free = NONE_TOK
        sums_free = NONE_TOK
        b6_free = tok("dve")


        def band_ops(qt, j):
            res = []
            e = j - 4 * qt
            if 0 <= e <= 3:
                lo = 128 * e - 2
                if e == 0:
                    res.append((0, 130, cband.ap()[:, 2:132], 0, 128))
                elif lo + 132 > 512:
                    res.append((lo, 512, cband.ap()[:, 0:512 - lo], 0, 128))
                else:
                    res.append((lo, lo + 132, cband.ap()[:, 0:132], 0, 128))
            if j == (4 * qt - 1) % KC:
                cap = ccorn.ap()[64:128, 4:6] if qt == 0 else ccorn.ap()[64:128, 0:2]
                res.append((0, 2, cap, 64, 128))
            if j == 4 * qt + 4:
                cap = ccorn.ap()[0:64, 6:8] if qt == QT - 1 else ccorn.ap()[0:64, 2:4]
                res.append((510, 512, cap, 0, 64))
            return res

        items = [(p, qt, j) for p in range(NP) for qt in range(QT) for j in range(KC)]
        av_free = [NONE_TOK, NONE_TOK, NONE_TOK, NONE_TOK]
        exp_toks = {}
        pending_tail = None
        NT2 = NP * QT
        q2_tok = [None] * NT2
        q2_qkdone = [("pe", 0), ("pe", 0)]

        def emit_qload(t2):
            p_, qt_ = t2 // QT, t2 % QT
            q2_tok[t2] = QS.dma(
                lambda e, p_=p_, qt_=qt_, t2=t2: e.dma_start(
                    out=b2[f"q2_{t2 % 2}"][:],
                    in_=q_dram[p_, :, 512 * qt_:512 * (qt_ + 1)]),
                waits=[q2_qkdone[t2 % 2]])

        emit_qload(0)
        emit_qload(1)

        def wq_steps(hidx):
            BLK = WH // 32
            B = wq
            par = hidx % 2
            src3 = wq[f"src_{par}"].rearrange("p (b k) -> p b k", k=32)
            dst = wq[f"dst_{par}"]
            dst3 = dst.rearrange("p (b k) -> p b k", k=32)
            rsb = B["rscale"].unsqueeze(2).to_broadcast([128, BLK, 32])
            ssb = B["scale"].unsqueeze(2).to_broadcast([128, BLK, 32])
            qy3 = B["qy"].rearrange("p (b k) -> p b k", k=32)
            qv3 = B["qv"].rearrange("p (b k) -> p b k", k=32)
            return [
                ("vector", lambda e: e.reduce_max(
                    B["amax"].unsqueeze(2), src3, axis=AX.X, apply_absolute_value=True)),
                ("vector", lambda e: e.tensor_scalar(
                    B["scale"][:], B["amax"][:], 1.0 / FP6_MAX, MX_EPS, ALU.mult, ALU.max)),
                ("vector", lambda e: e.reciprocal(B["rscale"][:], B["scale"][:])),
                ("pool", lambda e: e.tensor_tensor(qy3, src3, rsb, ALU.mult)),
                ("pool", lambda e: e.tensor_mul(B["qt"][:], B["qy"][:], B["qy"][:])),
                ("vector", lambda e: e.tensor_single_scalar(
                    B["qc"][:], B["qt"][:], 4.0, ALU.is_ge)),
                ("vector", lambda e: e.tensor_single_scalar(
                    B["qs"][:], B["qt"][:], 16.0, ALU.is_ge)),
                ("vector", lambda e: e.tensor_scalar(
                    B["qe"][:], B["qs"][:], -2.0, 8.0, ALU.mult, ALU.add)),
                ("vector", lambda e: e.scalar_tensor_tensor(
                    B["qr"][:], B["qc"][:], -4.0, B["qe"][:], ALU.mult, ALU.add)),
                ("vector", lambda e: e.tensor_scalar(
                    B["qe"][:], B["qs"][:], 0.25, 0.125, ALU.mult, ALU.add)),
                ("vector", lambda e: e.scalar_tensor_tensor(
                    B["qs"][:], B["qc"][:], 0.125, B["qe"][:], ALU.mult, ALU.add)),
                ("pool", lambda e: e.tensor_mul(B["qt"][:], B["qy"][:], B["qr"][:])),
                ("vector", lambda e: e.tensor_single_scalar(
                    B["qt"][:], B["qt"][:], -0.5, ALU.add)),
                ("vector", lambda e: e.tensor_copy(B["qvi"][:], B["qt"][:])),
                ("vector", lambda e: e.tensor_copy(B["qv"][:], B["qvi"][:])),
                ("vector", lambda e: e.tensor_tensor(B["qc"][:], B["qt"][:], B["qv"][:],
                                                     ALU.is_gt)),
                ("vector", lambda e: e.tensor_add(B["qv"][:], B["qv"][:], B["qc"][:])),
                ("pool", lambda e: e.tensor_mul(B["qv"][:], B["qv"][:], B["qs"][:])),
                ("vector", lambda e: e.tensor_scalar(
                    B["qv"][:], B["qv"][:], FP6_MAX, -FP6_MAX, ALU.min, ALU.max)),
                ("pool", lambda e: e.tensor_tensor(dst3, qv3, ssb, ALU.mult)),
            ]

        wq_queue = []
        NWH = FC * (D // WH)
        wq_state = {"tok": None, "h": 0, "dst_free": [None, None], "st": []}

        def wq_pump():
            if not wq_queue:
                if wq_state["h"] >= NWH:
                    return
                h = wq_state["h"]
                wq_state["h"] += 1
                jt, hh = h // (D // WH), h % (D // WH)
                par = h % 2
                w = [wq_state["tok"]] if wq_state["tok"] else []
                if wq_state["dst_free"][par] is not None:
                    w.append(wq_state["dst_free"][par])
                t_l = QS.dma(ld(wq[f"src_{par}"][:],
                                wffn[128 * jt:128 * (jt + 1), WH * hh:WH * (hh + 1)]),
                             waits=w)
                wq_state["tok"] = t_l
                wq_queue.extend(wq_steps(h))
                wq_queue.append(("dma", h))
                return
            item = wq_queue.pop(0)
            if item[0] == "dma":
                h = item[1]
                jt, hh = h // (D // WH), h % (D // WH)
                par = h % 2
                t_s = QS.dma(lambda e, jt=jt, hh=hh, par=par: e.dma_start(
                    out=wd_dram[jt, :, WH * hh:WH * (hh + 1)],
                    in_=wq[f"dst_{par}"][:]),
                    waits=[wq_state["tok"]])
                wq_state["dst_free"][par] = t_s
                wq_state["st"].append(t_s)
                return
            eng, fn = item
            sem = {"scalar": "act", "pool": "pool"}.get(eng, "dve")
            E(eng, fn, waits=[wq_state["tok"], tok(sem)], inc=sem)
            wq_state["tok"] = tok(sem)

        def emit_qk_band_exp(n, p, qt, j):
            u = n % 2
            stv = bank(2 * u, 2)
            t2 = p * QT + qt
            qsl = b2[f"q2_{t2 % 2}"]
            w0 = [stbank_free[u]]
            if j == 0:
                w0 = w0 + [q2_tok[t2]]
            E("tensor", lambda e, j=j, p=p, stv=stv, qsl=qsl: e.matmul(
                stv[:, 0:512],
                K_sb[0:64, p * S_kv + 128 * j: p * S_kv + 128 * (j + 1)],
                qsl[0:64, :],
                start=True, stop=True, tile_position=(0, 0)),
              waits=w0, inc=None)
            E("tensor", lambda e, j=j, p=p, stv=stv, qsl=qsl: e.matmul(
                stv[:, 512:1024],
                K_sb[64:128, p * S_kv + 128 * j: p * S_kv + 128 * (j + 1)],
                qsl[64:128, :],
                start=True, stop=True, tile_position=(64, 0)),
              inc="pe")
            t_qk = tok("pe")
            if j == KC - 1:
                q2_qkdone[t2 % 2] = t_qk
                if t2 + 2 < NT2:
                    emit_qload(t2 + 2)
            pt = b2[f"pt_{n % 5}"]
            w = [t_qk, pt_free[n % 5]] + ([t_negmh[p]] if n == 0 else [])
            E("scalar", lambda e, stv=stv, pt=pt, p=p: e.activation(
                pt[:], stv, AF.Exp, bias=b2["negmh"][:, p:p + 1], scale=0.125),
              waits=w, inc="act")
            t_exp = tok("act")
            stbank_free[u] = t_exp
            for (c0, c1, cap, r0, r1) in band_ops(qt, j):
                W = c1 - c0
                pt3 = pt.rearrange("p (h w) -> p h w", h=2)[r0:r1, :, c0:c1]
                cap3 = cap.unsqueeze(1).to_broadcast([r1 - r0, 2, W])
                E("pool", lambda e, pt3=pt3, cap3=cap3: e.tensor_tensor(
                    pt3, pt3, cap3, ALU.mult),
                  waits=[t_exp, tok("pool")], inc="pool")
                t_exp = tok("pool")
            exp_toks[n] = t_exp

        def emit_av(n, p, qt, j):
            pt = b2[f"pt_{n % 5}"]
            tp2 = (p * QT + qt) % 2
            ava, avb = bank(4 + 2 * tp2), bank(5 + 2 * tp2)
            w = [exp_toks.pop(n)]
            if j == 0:
                w += [av_free[2 * tp2], av_free[2 * tp2 + 1], t_vones]
            E("tensor", lambda e, p=p, j=j, pt=pt, ava=ava: e.matmul(
                ava[0:65, :],
                V_sb[:, j * VW + 65 * (2 * p): j * VW + 65 * (2 * p) + 65],
                pt[:, 0:512],
                start=(j == 0), stop=(j == KC - 1)),
              waits=w, inc=None)
            E("tensor", lambda e, p=p, j=j, pt=pt, avb=avb: e.matmul(
                avb[0:65, :],
                V_sb[:, j * VW + 65 * (2 * p + 1): j * VW + 65 * (2 * p + 2)],
                pt[:, 512:1024],
                start=(j == 0), stop=(j == KC - 1)),
              inc="pe")
            pt_free[n % 5] = tok("pe")

        lsb_free = NONE_TOK
        rlb_free = NONE_TOK
        stageb_free = NONE_TOK

        def emit_tail1(p, qt):
            nonlocal lsb_free
            t_avmm = tok("pe")
            tp2 = (p * QT + qt) % 2
            ava, avb = bank(4 + 2 * tp2), bank(5 + 2 * tp2)
            E("vector", lambda e, ava=ava: e.tensor_copy(
                b2["lsb"][64:65, 0:512], ava[64:65, :]),
              waits=[t_avmm, lsb_free, rlb_free], inc="dve")
            E("vector", lambda e, avb=avb: e.tensor_copy(
                b2["lsb"][64:65, 512:1024], avb[64:65, :]),
              waits=[tok("dve")], inc="dve")
            E("vector", lambda e: e.reciprocal(b2["lsb"][64:65, :],
                                               b2["lsb"][64:65, :]),
              waits=[tok("dve")], inc="dve")
            t_rl = tok("dve")
            lsb_free = t_rl
            return t_avmm, t_rl

        def emit_tail1b(t_rl):
            t_st = QS.dma(lambda e: e.dma_start(out=rl_dram, in_=b2["lsb"][64:65, :]),
                          waits=[t_rl])
            t_bc = QS.dma(lambda e: e.dma_start(
                out=b2["rlb"][:], in_=rl_dram.to_broadcast([128, 1024])),
                waits=[t_st])
            return t_bc

        def emit_tail2(p, qt, t_avmm_tile, t_bc):
            nonlocal rlb_free, stageb_free
            tp2 = (p * QT + qt) % 2
            ava, avb = bank(4 + 2 * tp2), bank(5 + 2 * tp2)
            ao_a = b2["attnout"][0:64, (p * QT + qt) * 512: (p * QT + qt + 1) * 512]
            ao_b = b2["attnout"][64:128, (p * QT + qt) * 512: (p * QT + qt + 1) * 512]
            E("vector", lambda e, ao_a=ao_a, ava=ava: e.tensor_tensor(
                ao_a, ava[0:64, :], b2["rlb"][0:64, 0:512], ALU.mult),
              waits=[t_bc, t_avmm_tile], inc="dve")
            av_free[2 * tp2] = tok("dve")
            E("vector", lambda e, avb=avb: e.tensor_tensor(
                b2["stageb"][0:64, :], avb[0:64, :], b2["rlb"][0:64, 512:1024],
                ALU.mult),
              waits=[stageb_free], inc="dve")
            t_avn_b = tok("dve")
            av_free[2 * tp2 + 1] = t_avn_b
            rlb_free = t_avn_b
            stageb_free = QS.dma(lambda e, ao_b=ao_b: e.dma_start(
                out=ao_b, in_=b2["stageb"][0:64, :]),
                waits=[t_avn_b])
            return stageb_free

        LAG = 5
        pending_1b = None

        def run_pending(n):
            nonlocal pending_1b, pending_tail
            if pending_1b is not None and n >= pending_1b[0]:
                _, tp, tqt, t_avm, t_rl = pending_1b
                t_bc = emit_tail1b(t_rl)
                pending_tail = (pending_1b[0] + 3, tp, tqt, t_avm, t_bc)
                pending_1b = None
            if pending_tail is not None and n >= pending_tail[0]:
                _, tp, tqt, t_avm, t_bc = pending_tail
                emit_tail2(tp, tqt, t_avm, t_bc)
                pending_tail = None

        for n, (p, qt, j) in enumerate(items):
            if n % 3 == 1:
                wq_pump()
            run_pending(n)
            if n >= LAG:
                pp, pq, pj = items[n - LAG]
                emit_av(n - LAG, pp, pq, pj)
                if pj == KC - 1:
                    t_avm, t_rl = emit_tail1(pp, pq)
                    pending_1b = (n + 3, pp, pq, t_avm, t_rl)
            emit_qk_band_exp(n, p, qt, j)
        for m in range(len(items) - LAG, len(items)):
            run_pending(m + len(items))
            pp, pq, pj = items[m]
            emit_av(m, pp, pq, pj)
            if pj == KC - 1:
                t_avm, t_rl = emit_tail1(pp, pq)
                pending_1b = (0, pp, pq, t_avm, t_rl)
        run_pending(10 ** 9)
        run_pending(10 ** 9)

        T.barrier()

        for c in range(FC):
            QG.dma(ld(p3["woT"][:, c * D:(c + 1) * D], woT[128 * c:128 * (c + 1), :]))
        t_wo = QG.total_tokens()

        def quant_steps(src_f32, dst_bf16, par, tab=None):
            BLK = D // 32
            B = tab if tab is not None else {
                k: p3[f"{k}_{par}"] for k in
                ("qy", "qt", "qv", "qc", "qr", "qs", "qe", "qvi",
                 "amax", "scale", "rscale")}
            src3 = src_f32.rearrange("p (b k) -> p b k", k=32)
            dst3 = dst_bf16.rearrange("p (b k) -> p b k", k=32)
            rsb = B["rscale"].unsqueeze(2).to_broadcast([128, BLK, 32])
            ssb = B["scale"].unsqueeze(2).to_broadcast([128, BLK, 32])
            qy3 = B["qy"].rearrange("p (b k) -> p b k", k=32)
            qv3 = B["qv"].rearrange("p (b k) -> p b k", k=32)
            tt = "vector"
            return [
                ("vector", lambda e: e.reduce_max(
                    B["amax"].unsqueeze(2), src3, axis=AX.X, apply_absolute_value=True)),
                ("vector", lambda e: e.tensor_scalar(
                    B["scale"][:], B["amax"][:], 1.0 / FP6_MAX, MX_EPS, ALU.mult, ALU.max)),
                ("vector", lambda e: e.reciprocal(B["rscale"][:], B["scale"][:])),
                (tt, lambda e: e.tensor_tensor(qy3, src3, rsb, ALU.mult)),
                ("scalar", lambda e: e.activation(B["qt"][:], B["qy"][:], AF.Square)),
                ("vector", lambda e: e.tensor_single_scalar(
                    B["qc"][:], B["qt"][:], 4.0, ALU.is_ge)),
                ("vector", lambda e: e.tensor_single_scalar(
                    B["qs"][:], B["qt"][:], 16.0, ALU.is_ge)),
                ("scalar", lambda e: e.activation(B["qe"][:], B["qs"][:], AF.Copy,
                                                  scale=-2.0, bias=8.0)),
                ("vector", lambda e: e.scalar_tensor_tensor(
                    B["qr"][:], B["qc"][:], -4.0, B["qe"][:], ALU.mult, ALU.add)),
                ("scalar", lambda e: e.activation(B["qe"][:], B["qs"][:], AF.Copy,
                                                  scale=0.25, bias=0.125)),
                ("vector", lambda e: e.scalar_tensor_tensor(
                    B["qs"][:], B["qc"][:], 0.125, B["qe"][:], ALU.mult, ALU.add)),
                ("vector", lambda e: e.tensor_mul(B["qt"][:], B["qy"][:], B["qr"][:])),
                ("scalar", lambda e: e.activation(B["qt"][:], B["qt"][:], AF.Copy,
                                                  scale=1.0, bias=-0.5)),
                ("vector", lambda e: e.tensor_copy(B["qvi"][:], B["qt"][:])),
                ("vector", lambda e: e.tensor_copy(B["qv"][:], B["qvi"][:])),
                ("vector", lambda e: e.tensor_tensor(B["qc"][:], B["qt"][:], B["qv"][:],
                                                     ALU.is_gt)),
                (tt, lambda e: e.tensor_add(B["qv"][:], B["qv"][:], B["qc"][:])),
                (tt, lambda e: e.tensor_mul(B["qv"][:], B["qv"][:], B["qs"][:])),
                ("vector", lambda e: e.tensor_scalar(
                    B["qv"][:], B["qv"][:], FP6_MAX, -FP6_MAX, ALU.min, ALU.max)),
                (tt, lambda e: e.tensor_tensor(dst3, qv3, ssb, ALU.mult)),
            ]

        qbuf_free = [NONE_TOK, NONE_TOK]

        def run_quant(tiles):
            chains = []
            for tt_ in tiles:
                src_, dst_, par, pre = tt_[:4]
                tab = tt_[4] if len(tt_) > 4 else None
                chains.append([list(pre) + ([qbuf_free[par]] if tab is None else []),
                               quant_steps(src_, dst_, par, tab)])
            nsteps = max(len(c[1]) for c in chains)
            finals = [None] * len(tiles)
            s3toks = [None] * len(tiles)
            for k in range(nsteps):
                for t, (wt, steps) in enumerate(chains):
                    if k >= len(steps):
                        continue
                    eng, fn = steps[k]
                    sem = {"scalar": "act", "pool": "pool"}.get(eng, "dve")
                    E(eng, fn, waits=wt + [tok(sem)], inc=sem)
                    chains[t][0] = [tok(sem)]
                    finals[t] = tok(sem)
                    if k == 3:
                        s3toks[t] = tok(sem)
            for (tt, tile) in enumerate(tiles):
                if len(tile) <= 4:
                    qbuf_free[tile[2]] = finals[tt]
            return finals, s3toks

        while wq_queue or wq_state["h"] < NWH:
            wq_pump()
        wf_free = [NONE_TOK, NONE_TOK]
        wd_trbank_free = [NONE_TOK, NONE_TOK]
        wdl = {}
        wdl_free = [NONE_TOK, NONE_TOK]
        for jt in range(FC):
            par = jt % 2
            wstg = wq[f"wstg_{par}"][:]
            t_ld = QS.dma(ld(wstg, wd_dram[jt]),
                          waits=wq_state["st"][-4:] + [wdl_free[par]])
            tr_out = bankb(4 + par)
            for c in range(FC):
                w = [t_ld, wd_trbank_free[par]] if c == 0 else []
                E("tensor", lambda e, c=c, tr_out=tr_out, wstg=wstg: e.transpose(
                    tr_out[:, 128 * c:128 * (c + 1)],
                    wstg[:, 128 * c:128 * (c + 1)],
                    cident.ap()[:]),
                  waits=w, inc="pe" if c == FC - 1 else None)
            t_tr = tok("pe")
            wdl_free[par] = t_tr
            wd_v = p3["wd"].rearrange("p (c j) -> p c j", c=FC)[:, :, 128 * jt:128 * (jt + 1)]
            tr_v = tr_out[:, 0:FC * 128].rearrange("p (c j) -> p c j", c=FC)
            E("vector", lambda e, wd_v=wd_v, tr_v=tr_v: e.tensor_copy(wd_v, tr_v),
              waits=[t_tr], inc="dve")
            wd_trbank_free[par] = tok("dve")
        t_wd = tok("dve")

        T.barrier()

        p3buf = {"dummy": b1["xstage_0"], "lsum": p3["l3sum"], "lvar": p3["l3var"],
                 "lrstd": p3["l3rstd"], "lnmr": p3["l3nmr"], "lmusq": p3["l3musq"]}
        x3_free = list(wf_free)
        ffnbank_free = NONE_TOK
        xdtr_free = [NONE_TOK, NONE_TOK]
        x1_free = [NONE_TOK] * 4
        xd_free = [NONE_TOK, NONE_TOK]
        xdT_free = [NONE_TOK, NONE_TOK]
        ln3_reader = [("act", 0), ("act", 0)]
        out_dma_par = [NONE_TOK, NONE_TOK]
        out_toks = []

        t_op_toks = {}
        t_x_toks = {}
        x1_toks = {}
        opbank_tok = {}
        tr_toks = {}
        xd_ready = {}
        apply_tok = {}

        def p3_stage1(i):
            ib = i % 2
            xst = p3[f"wfstage_{ib}"][:]
            t_x_toks[i] = QS.dma(ld(xst, x_kv[128 * i:128 * (i + 1), :]),
                                 waits=[x3_free[ib]])
            qt, s4 = i // 4, i % 4
            ops = bank(2 * ib, 2)[:, 0:D]
            for jh in range(JH):
                w = ([opbank_tok.pop(i - 2)] if (i - 2) in opbank_tok else []) \
                    if jh == 0 else []
                if i == 0 and jh == 0:
                    w += t_wo
                E("tensor", lambda e, ops=ops, jh=jh: e.matmul(
                    ops[:, 512 * jh:512 * (jh + 1)],
                    cones1x128.ap()[32:33, :], bo_row[:, 512 * jh:512 * (jh + 1)],
                    start=True, stop=False),
                  waits=w, inc=None)
                for c in range(FC):
                    ao = b2["attnout"][:, (c * QT + qt) * 512 + 128 * s4:
                                       (c * QT + qt) * 512 + 128 * (s4 + 1)]
                    E("tensor", lambda e, ao=ao, c=c, jh=jh, ops=ops: e.matmul(
                        ops[:, 512 * jh:512 * (jh + 1)],
                        ao, p3["woT"][:, c * D + 512 * jh: c * D + 512 * (jh + 1)],
                        start=False, stop=(c == FC - 1)),
                      inc="pe" if (c == FC - 1 and jh == JH - 1) else None)
            t_op_toks[i] = tok("pe")

        def p3_tr(i):
            ib = i % 2
            tr_out = bankb(4 + ib)
            t_xd = xd_ready.pop(i)
            for c in range(FC):
                w = [t_xd, xdtr_free[ib]] if c == 0 else []
                E("tensor", lambda e, c=c, tr_out=tr_out, ib=ib: e.transpose(
                    tr_out[:, 128 * c:128 * (c + 1)],
                    p3[f"xd_{ib}"][:, 128 * c:128 * (c + 1)], cident.ap()[:]),
                  waits=w, inc="pe" if c == FC - 1 else None)
            t_tr = tok("pe")
            xd_free[ib] = t_tr
            tr_toks[i] = t_tr

        def mk_quant_chain(i):
            ib = i % 2
            steps = []
            for k_, (eng, fn) in enumerate(quant_steps(
                    p3[f"xhat2_{ib}"][:], p3[f"xd_{ib}"][:], ib)):
                extras = []
                key = None
                if k_ == 3:
                    key = "qy"
                elif k_ == 7:
                    extras = [out_dma_par[ib]]
                elif k_ == 10:
                    key = "qelast"
                elif k_ == 19:
                    key = "final"
                steps.append((eng, fn, extras, key))
            return {"st8": [apply_tok.pop(i, ("act", 0)), xd_free[ib],
                            qbuf_free[ib]],
                    "steps": steps, "rec": {}}

        def mk_copy_chain(pair_lo):
            steps = []
            for par in range(2):
                i = pair_lo + par
                ib = i % 2
                tr_out = bankb(4 + ib)
                xdt = p3[f"xdT_{ib}"][:]
                steps.append(("scalar",
                              lambda e, xdt=xdt, tr_out=tr_out: e.activation(
                                  xdt, tr_out[:, 0:FC * 128], AF.Copy),
                              [tr_toks.pop(i), xdT_free[ib]], ("xdt", i)))
            return {"st8": [], "steps": steps, "rec": {}}

        def mk_prep_chain(i0p, chA, chB):
            steps = [None] * 12
            for par, pad in ((0, 0), (1, 2)):
                i = i0p + par
                ib, i4 = i % 2, i % 4
                xst = p3[f"wfstage_{ib}"][:]
                ops = bank(2 * ib, 2)[:, 0:D]
                x1 = p3[f"x1_{i4}"][:]
                steps += [None] * pad
                steps.append(("vector",
                              lambda e, x1=x1, ops=ops, xst=xst: e.tensor_tensor(
                                  x1, ops, xst, ALU.add),
                              [t_op_toks.pop(i), t_x_toks.pop(i),
                               (lambda i4=i4: x1_free[i4])], ("x1", par)))
                x1v = p3[f"x1_{i4}"][:]
                steps.append(("scalar",
                              lambda e, x1v=x1v, ib=ib: e.activation(
                                  p3buf["dummy"][:], x1v, AF.Copy,
                                  accum_out=p3buf["lsum"][:, ib:ib + 1]),
                              [ln3_reader[ib]], None))
                steps.append(("scalar",
                              lambda e, x1v=x1v, ib=ib: e.activation(
                                  p3buf["dummy"][:], x1v, AF.Square,
                                  accum_out=p3buf["lvar"][:, ib:ib + 1]),
                              [], None))
            buf = p3buf
            sl = slice(0, 2)
            steps += [
                ("vector", lambda e: e.tensor_single_scalar(
                    buf["lsum"][:, sl], buf["lsum"][:, sl], 1.0 / D, ALU.mult),
                 [], None),
                ("vector", lambda e: e.tensor_mul(
                    buf["lmusq"][:, sl], buf["lsum"][:, sl], buf["lsum"][:, sl]),
                 [], None),
                ("vector", lambda e: e.tensor_scalar(
                    buf["lvar"][:, sl], buf["lvar"][:, sl], 1.0 / D, LN_EPS,
                    ALU.mult, ALU.add), [], None),
                ("vector", lambda e: e.tensor_sub(
                    buf["lvar"][:, sl], buf["lvar"][:, sl], buf["lmusq"][:, sl]),
                 [], None),
                ("scalar", lambda e: e.activation(
                    buf["lrstd"][:, sl], buf["lvar"][:, sl], AF.Ln), [], None),
                ("scalar", lambda e: e.activation(
                    buf["lrstd"][:, sl], buf["lrstd"][:, sl], AF.Exp,
                    scale=-0.5), [], None),
                ("vector", lambda e: e.scalar_tensor_tensor(
                    buf["lnmr"][:, sl], buf["lsum"][:, sl], -1.0,
                    buf["lrstd"][:, sl], ALU.mult, ALU.mult), [], None),
            ]
            for par in range(2):
                i = i0p + par
                ib, i4 = i % 2, i % 4
                ch = chA if par == 0 else chB
                steps.append(("scalar",
                              lambda e, i4=i4, ib=ib: e.activation(
                                  p3[f"xhat2_{ib}"][:], p3[f"x1_{i4}"][:],
                                  AF.Identity,
                                  bias=p3buf["lnmr"][:, ib:ib + 1],
                                  scale=p3buf["lrstd"][:, ib:ib + 1]),
                              [(lambda ch=ch: ch["rec"]["qy"])], ("apply", par)))
            return {"st8": [], "steps": steps, "rec": {}}

        def run_zip(chains, mids=()):
            midmap = dict(mids)
            nsteps = max(len(c["steps"]) for c in chains)
            for r in range(nsteps):
                if r in midmap:
                    midmap.pop(r)()
                for c in chains:
                    if r >= len(c["steps"]) or c["steps"][r] is None:
                        continue
                    eng, fn, extras, key = c["steps"][r]
                    sem = {"scalar": "act", "pool": "pool"}.get(eng, "dve")
                    ext = [x() if callable(x) else x for x in extras]
                    E(eng, fn, waits=c["st8"] + ext + [tok(sem)], inc=sem)
                    c["st8"] = [tok(sem)]
                    if key is not None:
                        c["rec"][key] = tok(sem)
            for fn in midmap.values():
                fn()

        def p3_ffn(i, chC, qch):
            nonlocal ffnbank_free
            ib = i % 2
            i4 = i % 4
            x1 = p3[f"x1_{i4}"][:]
            xdt = p3[f"xdT_{ib}"][:]
            fps = bank(6, 2)[:, 0:D]
            for jh in range(JH):
                w = ([ffnbank_free, chC["rec"][("xdt", i)]] if jh == 0 else [])
                if i == 0 and jh == 0:
                    w += [t_wd]
                E("tensor", lambda e, fps=fps, jh=jh: e.matmul(
                    fps[:, 512 * jh:512 * (jh + 1)],
                    cones1x128.ap()[64:65, :], fb_row[:, 512 * jh:512 * (jh + 1)],
                    start=True, stop=False),
                  waits=w, inc=None)
                for c in range(FC):
                    E("tensor", lambda e, c=c, jh=jh, fps=fps, xdt=xdt: e.matmul(
                        fps[:, 512 * jh:512 * (jh + 1)],
                        xdt[:, 128 * c:128 * (c + 1)],
                        p3["wd"][:, c * D + 512 * jh: c * D + 512 * (jh + 1)],
                        start=False, stop=(c == FC - 1)),
                      inc="pe" if (c == FC - 1 and jh == JH - 1) else None)
            t_ffn = tok("pe")
            outst = p3[f"outst_{ib}"][:]
            w = [t_ffn, x1_toks.pop(i)]
            if qch is not None:
                w.append(qch["rec"]["qelast"])
            if len(out_toks) >= 2:
                w.append(out_toks[-2])
            E("vector", lambda e, outst=outst, fps=fps, x1=x1: e.tensor_tensor(
                outst, fps, x1, ALU.add),
              waits=w, inc="dve")
            t_out = tok("dve")
            x1_free[i4] = t_out
            ffnbank_free = t_out
            xdT_free[ib] = t_ffn
            out_toks.append(QS.dma(lambda e, outst=outst, i=i: e.dma_start(
                out=out_d[128 * i:128 * (i + 1), :], in_=outst),
                waits=[t_out]))
            out_dma_par[ib] = out_toks[-1]

        def resolve_recs(i0, chA, chB, chC, chD):
            for par, ch in ((0, chA), (1, chB)):
                qbuf_free[(i0 + par) % 2] = ch["rec"]["final"]
                xd_ready[i0 + par] = ch["rec"]["final"]
            if chC is not None:
                for par in range(2):
                    xdtr_free[(i0 - 2 + par) % 2] = chC["rec"][("xdt", i0 - 2 + par)]
            if chD is not None:
                for par in range(2):
                    i = i0 + 2 + par
                    t_x1 = chD["rec"][("x1", par)]
                    x1_toks[i] = t_x1
                    x3_free[i % 2] = t_x1
                    opbank_tok[i] = t_x1
                    t_ap = chD["rec"][("apply", par)]
                    ln3_reader[i % 2] = t_ap
                    apply_tok[i] = t_ap

        p3_stage1(0)
        p3_stage1(1)
        t_x1_0 = []
        for par in range(2):
            ib = par
            xst = p3[f"wfstage_{ib}"][:]
            ops = bank(2 * ib, 2)[:, 0:D]
            x1 = p3[f"x1_{par}"][:]
            E("vector", lambda e, x1=x1, ops=ops, xst=xst: e.tensor_tensor(
                x1, ops, xst, ALU.add),
              waits=[t_op_toks.pop(par), t_x_toks.pop(par)], inc="dve")
            x1_toks[par] = tok("dve")
            x3_free[ib] = tok("dve")
            opbank_tok[par] = tok("dve")
            t_x1_0.append(tok("dve"))
        t_sts0 = [ln_stats(p3[f"x1_{par}"][:], p3buf, par, pre=[t_x1_0[par]])
                  for par in range(2)]
        t_ch_dve0, _ = ln_chain(p3buf, 0, 2, pre=[t_sts0[-1]])
        for par in range(2):
            apply_tok[par] = ln_apply(p3[f"x1_{par}"][:], p3[f"xhat2_{par}"][:],
                                      p3buf, par, pre=[t_ch_dve0])
            ln3_reader[par] = apply_tok[par]

        NPAIR = cfg.NTQ // 2
        for kw in range(NPAIR):
            i0 = 2 * kw
            if kw >= 1:
                p3_tr(i0 - 2)
                p3_tr(i0 - 1)
            if kw + 1 < NPAIR:
                p3_stage1(i0 + 2)
                p3_stage1(i0 + 3)
            chA = mk_quant_chain(i0)
            chB = mk_quant_chain(i0 + 1)
            chC = mk_copy_chain(i0 - 2) if kw >= 1 else None
            chD = mk_prep_chain(i0 + 2, chA, chB) if kw + 1 < NPAIR else None
            chains = [c for c in (chA, chB, chC, chD) if c is not None]
            mids = []
            if kw >= 1:
                mids = [(12, lambda i=i0 - 2, chC=chC, chA=chA: p3_ffn(i, chC, chA)),
                        (16, lambda i=i0 - 1, chC=chC, chB=chB: p3_ffn(i, chC, chB))]
            run_zip(chains, mids)
            resolve_recs(i0, chA, chB, chC, chD)

        i0 = cfg.NTQ - 2
        p3_tr(i0)
        p3_tr(i0 + 1)
        chC = mk_copy_chain(i0)
        run_zip([chC])
        p3_ffn(i0, chC, None)
        p3_ffn(i0 + 1, chC, None)

        E("gpsimd", lambda e: e.engine_nop(),
          waits=QG.total_tokens() + QS.total_tokens(), inc=None)

        for name, filt in (("sync", ("sync",)), ("tensor", ("tensor",)),
                           ("scalar", ("scalar",)), ("vector", ("vector",)),
                           ("gpsimd", ("gpsimd", "pool"))):
            def body(eng, filt=filt):
                T.replay(filt, eng, sems)
            getattr(blk, name)(body)

    return nc



def _make_in_maps(cfg, x, Wq, Wk, Wv, Wo, bq, bk, bv, bo, ocr, Wf, fbias, n_cores):
    import ml_dtypes
    bft = ml_dtypes.bfloat16
    B, S, D = x.shape
    per_b = n_cores // B
    S_q = cfg.S_q
    FC = cfg.FC
    shared = {
        "wqT": np.ascontiguousarray(Wq.T), "wkT": np.ascontiguousarray(Wk.T),
        "wvT": np.ascontiguousarray(Wv.T), "woT": np.ascontiguousarray(Wo.T),
        "wffn": Wf,
        "bq2": np.ascontiguousarray(bq.reshape(FC, 128).T),
        "bk2": np.ascontiguousarray(bk.reshape(FC, 128).T),
        "ocr2": np.ascontiguousarray(ocr.reshape(FC, 128).T),
        "bv_row": bv.reshape(1, D).copy(), "bo_row": bo.reshape(1, D).copy(),
        "fb_row": fbias.reshape(1, D).copy(),
        "ident": np.eye(128, dtype=bft),
        "ones128": np.ones((128, 1), dtype=bft),
        "ones1x128": np.ones((65, 128), dtype=bft),
        "ones1x128f": np.ones((65, 128), dtype=np.float32),
        "band_main": band_main_np().astype(bft),
    }
    in_maps = []
    for c in range(n_cores):
        b, half = c // per_b, c % per_b
        q_off = half * S_q
        m = dict(shared)
        m["x_kv"] = np.ascontiguousarray(np.roll(x[b], -q_off, axis=0))
        corners = np.zeros((128, 8), dtype=np.float32)
        corners[:, 0:2] = corner_pred_np(True)
        corners[:, 2:4] = corner_succ_np(True)
        corners[:, 4:6] = corner_pred_np(q_off > 0)
        corners[:, 6:8] = corner_succ_np(q_off + S_q < S)
        m["corners"] = corners.astype(bft)
        in_maps.append(m)
    return in_maps


def kernel(L0_attn_x, L0_attn_ocr_context, L0_attn_Wq, L0_attn_bq, L0_attn_Wk,
           L0_attn_bk, L0_attn_Wv, L0_attn_bv, L0_attn_Wo, L0_attn_bo,
           L0_attn_ocr_bias, L0_ffn_weight, L0_ffn_bias):
    x = np.asarray(L0_attn_x, dtype=np.float32)
    B, S, D = x.shape
    n_cores = 8
    per_b = n_cores // B
    S_q = S // per_b
    cfg = Cfg(D, S_q, S)

    f32 = lambda a: np.ascontiguousarray(np.asarray(a, dtype=np.float32))
    in_maps = _make_in_maps(
        cfg, x, f32(L0_attn_Wq), f32(L0_attn_Wk), f32(L0_attn_Wv), f32(L0_attn_Wo),
        f32(L0_attn_bq), f32(L0_attn_bk), f32(L0_attn_bv), f32(L0_attn_bo),
        f32(L0_attn_ocr_bias), f32(L0_ffn_weight), f32(L0_ffn_bias), n_cores)

    nc = build_nc(cfg)
    res = run_bass_kernel_spmd(nc, in_maps, core_ids=list(range(n_cores)))
    out = np.empty((B, S, D), dtype=np.float32)
    for c in range(n_cores):
        b, half = c // per_b, c % per_b
        out[b, half * S_q:(half + 1) * S_q, :] = res.results[c]["out"]
    return out
